# version 07
# baseline (speedup 1.0000x reference)
import numpy as np
import ml_dtypes
from contextlib import ExitStack
from types import SimpleNamespace
import concourse.bass as bass
import concourse.mybir as mybir
from concourse.bass_utils import run_bass_kernel_spmd

F32 = mybir.dt.float32
BF16 = mybir.dt.bfloat16
I8 = mybir.dt.int8
AF = mybir.ActivationFunctionType
ALU = mybir.AluOpType
BF16NP = ml_dtypes.bfloat16

NCORES = 8
D = 2048
TOK = 16384
TPC = TOK // NCORES
NFC = D // 128
CLAMP = 21.0


class Ctr:
    def __init__(self, handle):
        self.h = handle
        self.n = 0

    def inc(self, instr, k=1):
        self.n += k
        instr.then_inc(self.h, k)
        return instr

    def wait(self, eng, n=None):
        eng.wait_ge(self.h, self.n if n is None else n)



def l0attn_consts():
    GT = 512
    col = np.arange(GT)
    c = {}
    for m in (8, 16, 32, 64, 128):
        c[f"rm{m}"] = np.broadcast_to((col % m != 0).astype(np.float32), (128, GT)).copy()
    c["evn8"] = np.broadcast_to(((col // 8) % 2 == 0).astype(BF16NP), (128, GT)).copy()
    c["odd8"] = np.broadcast_to(((col // 8) % 2 == 1).astype(BF16NP), (128, GT)).copy()
    s = np.arange(128)[:, None]; t = np.arange(128)[None, :]
    c["maskAb"] = np.where(s <= t, -1, 0).astype(np.int32)
    c["maskMb"] = np.where(s < t, -1, 0).astype(np.int32)
    c["identb"] = np.eye(128, dtype=BF16NP)
    c["identf"] = np.eye(128, dtype=np.float32)
    return c


def l0attn_inmap(core, xn_fm_bf16, ins):
    c = core
    hs = slice(256 * c, 256 * c + 256)
    CST = l0attn_consts()
    cwq = ins["L0_attn_q_conv_w"][hs].reshape(2, 128, 4)
    cwk = ins["L0_attn_k_conv_w"][hs].reshape(2, 128, 4)
    cwv = ins["L0_attn_v_conv_w"][hs].reshape(2, 128, 4)
    cw = np.concatenate([cwq[0], cwq[1], cwk[0], cwk[1], cwv[0], cwv[1]],
                        axis=1)
    m = {
        "wq": np.ascontiguousarray(ins["L0_attn_q_w"][:, hs]),
        "wk": np.ascontiguousarray(ins["L0_attn_k_w"][:, hs]),
        "wv": np.ascontiguousarray(ins["L0_attn_v_w"][:, hs]),
        "cw": np.ascontiguousarray(cw),
        "fw1": ins["L0_attn_f_w1"],
        "fw2": np.ascontiguousarray(ins["L0_attn_f_w2"][:, hs]),
        "gw1": ins["L0_attn_g_w1"],
        "gw2": np.ascontiguousarray(ins["L0_attn_g_w2"][:, hs]),
        "gb2": np.ascontiguousarray(ins["L0_attn_g_b2"][hs][None, :]),
        "bw": np.ascontiguousarray(ins["L0_attn_b_w"][:, 2 * c:2 * c + 2]),
        "alog": np.ascontiguousarray(ins["L0_attn_A_log"][2 * c:2 * c + 2][None, :]),
        "dtb": np.ascontiguousarray(ins["L0_attn_dt_bias"].reshape(16, 128)
                                    [2 * c:2 * c + 2].T),
    }
    for k, v in CST.items():
        m[k] = v
    return m


def mid_inmap(core, ofeat_full_bf16, xn_fm_bf16, x_fm_f32, ins):
    c = core
    sl = slice(TPC * c, TPC * (c + 1))
    CST = l0attn_consts()
    m_ = {}
    if ofeat_full_bf16 is not None:
        m_["ofm"] = np.ascontiguousarray(ofeat_full_bf16.T[:, sl])
        m_["xns"] = np.ascontiguousarray(xn_fm_bf16[:, sl])
        m_["xs"] = np.ascontiguousarray(x_fm_f32[:, sl])
    return {
        **m_,
        "onw": np.ascontiguousarray(ins["L0_attn_o_norm_w"][:, None]),
        "ow": ins["L0_attn_o_w"],
        "rout": ins["L0_ffn_router"],
        "wug": ins["L0_ffn_w_up_gate"],
        "wdn": ins["L0_ffn_w_down"],
        "swu": ins["L0_ffn_shared_w_up_gate"],
        "swd": ins["L0_ffn_shared_w_down"],
        "identf": CST["identf"],
        "esel": np.ascontiguousarray(
            np.concatenate([np.tile((np.arange(8) == e).astype(BF16NP)[:, None],
                                    (1, 128)) for e in range(8)], axis=1)),
    }


def l1attn_consts(core):
    S = 4096
    t = np.arange(S, dtype=np.float32)
    inv = 1.0 / 10000.0 ** (np.arange(0, 64, 2) / 64.0)
    fr = t[:, None] * inv[None, :]
    cos = np.cos(fr).astype(np.float32)
    sin = np.sin(fr).astype(np.float32)
    cos_t = np.ascontiguousarray(cos.reshape(32, 128, 32).transpose(1, 0, 2).reshape(128, 32 * 32))
    sin_t = np.ascontiguousarray(sin.reshape(32, 128, 32).transpose(1, 0, 2).reshape(128, 32 * 32))
    slopes = 2.0 ** (-(np.arange(1, 33) * 8.0 / 32))
    qaug = np.zeros((2, 4 * S), np.float32)
    for h in range(4):
        sl = slopes[4 * core + h]
        qaug[0, h * S:(h + 1) * S] = -sl * t
        qaug[1, h * S:(h + 1) * S] = sl
    kpos = np.stack([np.ones(S, np.float32), t]).astype(np.float32)
    sq = np.arange(128)[:, None]; tq = np.arange(128)[None, :]
    causal = (sq <= tq).astype(BF16NP)
    return {"cos_t": cos_t, "sin_t": sin_t, "qaug": qaug, "kpos": kpos, "causal": causal}


def l1attn_inmap(core, xn1_fm_bf16, ins):
    c = core
    CST = l1attn_consts(c)
    m = {
        "wq": np.ascontiguousarray(ins["L1_attn_Wq"][:, 256 * c:256 * (c + 1)]),
        "wk": np.ascontiguousarray(ins["L1_attn_Wk"][:, 64 * c:64 * (c + 1)]),
        "wv": np.ascontiguousarray(ins["L1_attn_Wv"][:, 64 * c:64 * (c + 1)]),
    }
    m.update(CST)
    if xn1_fm_bf16 is not None:
        m["xn1"] = xn1_fm_bf16
    return m


def fin_inmap(core, ctx_fm_bf16, xn1_fm_bf16, x1_fm_f32, ins):
    c = core
    sl = slice(TPC * c, TPC * (c + 1))
    m_ = {}
    if ctx_fm_bf16 is not None:
        m_["ctx"] = np.ascontiguousarray(ctx_fm_bf16[:, sl])
        m_["xn1s"] = np.ascontiguousarray(xn1_fm_bf16[:, sl])
        m_["x1s"] = np.ascontiguousarray(x1_fm_f32[:, sl])
    return {
        **m_,
        "wo": ins["L1_attn_Wo"],
        "bo": np.ascontiguousarray(ins["L1_attn_bo"].reshape(NFC, 128).T),
        "wg8": ins["L1_ffn_w_gate"],
        "wu8": ins["L1_ffn_w_up"],
        "wd8": ins["L1_ffn_w_down"],
        "sg": np.ascontiguousarray(ins["L1_ffn_s_gate"].reshape(64, 128).T),
        "su": np.ascontiguousarray(ins["L1_ffn_s_up"].reshape(64, 128).T),
        "sd": np.ascontiguousarray(ins["L1_ffn_s_down"].reshape(NFC, 128).T),
    }


I32F = mybir.dt.int32


def mkctx(nc, st, block, ph, pfx):
    C = SimpleNamespace(nc=nc, st=st, block=block, ph=ph, pfx=pfx)

    def sbuf(n, sh, dt=F32):
        return st.enter_context(nc.sbuf_tensor(pfx + n, sh, dt))

    def psum(n, sh, dt=F32):
        return st.enter_context(nc.psum_tensor(pfx + n, sh, dt))

    def T(fn):
        @block.tensor
        def _(t):
            ph.wait(t)
            fn(t)

    def V(fn):
        @block.vector
        def _(v):
            ph.wait(v)
            fn(v)

    def SC(fn):
        @block.scalar
        def _(s):
            ph.wait(s)
            fn(s)

    def SY(fn):
        @block.sync
        def _(sy):
            ph.wait(sy)
            fn(sy)

    def GQ(fn):
        @block.gpsimd
        def _(g):
            ph.wait(g)
            fn(g)

    C.sbuf, C.psum = sbuf, psum
    C.T, C.V, C.SC, C.SY, C.GQ = T, V, SC, SY, GQ
    return C


def emit_coll(C, kind, in_ap, out_ap):
    ph = C.ph
    op = ALU.add if kind == "ReduceScatter" else ALU.bypass

    @C.block.gpsimd
    def _(g):
        ph.wait(g)
        ins = g.collective_compute(
            kind, op,
            replica_groups=[list(range(NCORES))],
            ins=[in_ap.opt()], outs=[out_ap.opt()])
        ph.inc(ins, 1)


def emit_ln(C, x_in, xn_out, eps=1e-5):
    TT = 512
    NT = TPC // TT
    nc, ph = C.nc, C.ph
    with ExitStack() as st2:
        def sb(n, sh, dt=F32):
            return st2.enter_context(nc.sbuf_tensor(C.pfx + n, sh, dt))
        xb = sb("xb", [128, NFC * TT])
        sqf = sb("sqf", [128, TT])
        onecol = sb("onecol", [128, 1])
        onesc = sb("onesc", [1, 128])
        epsc = sb("epsc", [1, 1])
        statm = sb("statm", [1, 6 * TT])
        xnb = sb("xnb", [128, TT], BF16)
        pS = st2.enter_context(nc.psum_tensor(C.pfx + "pS", [1, TT], F32))
        pQ = st2.enter_context(nc.psum_tensor(C.pfx + "pQ", [1, TT], F32))
        pM = st2.enter_context(nc.psum_tensor(C.pfx + "pM", [128, TT], F32))
        pR = st2.enter_context(nc.psum_tensor(C.pfx + "pR", [128, TT], F32))

        def setup(v):
            v.memset(onecol[:, :], 1.0)
            v.memset(onesc[:, :], 1.0)
            ph.inc(v.memset(epsc[:, :], eps))
        C.V(setup)
        for tt in range(NT):
            def ldx(sy, tt=tt):
                for j in range(NFC):
                    ph.inc(sy.dma_start(out=xb[:, j * TT:(j + 1) * TT],
                                        in_=x_in[j * 128:(j + 1) * 128,
                                                 tt * TT:(tt + 1) * TT]), 16)
            C.SY(ldx)
            def mmsum(t):
                for j in range(NFC):
                    ins = t.matmul(out=pS[:, :], lhsT=onecol[:, 0:1],
                                   rhs=xb[:, j * TT:(j + 1) * TT],
                                   start=(j == 0), stop=(j == NFC - 1))
                ph.inc(ins)
            C.T(mmsum)
            for j in range(NFC):
                C.SC(lambda s, j=j: ph.inc(s.activation(out=sqf[:, :],
                                                        in_=xb[:, j * TT:(j + 1) * TT],
                                                        func=AF.Square)))
                C.T(lambda t, j=j: ph.inc(t.matmul(out=pQ[:, :], lhsT=onecol[:, 0:1],
                                                   rhs=sqf[:, :], start=(j == 0),
                                                   stop=(j == NFC - 1))))
            def stats(v):
                v.tensor_scalar_mul(out=statm[:, 0 * TT:1 * TT], in0=pS[:, :], scalar1=1.0 / D)
                v.tensor_scalar_mul(out=statm[:, 1 * TT:2 * TT], in0=pQ[:, :], scalar1=1.0 / D)
                v.tensor_mul(out=statm[:, 2 * TT:3 * TT], in0=statm[:, 0 * TT:1 * TT],
                             in1=statm[:, 0 * TT:1 * TT])
                ph.inc(v.tensor_sub(out=statm[:, 3 * TT:4 * TT], in0=statm[:, 1 * TT:2 * TT],
                                    in1=statm[:, 2 * TT:3 * TT]))
            C.V(stats)
            C.SC(lambda s: ph.inc(s.activation(out=statm[:, 4 * TT:5 * TT],
                                               in_=statm[:, 3 * TT:4 * TT],
                                               func=AF.Sqrt, bias=epsc[0:1, 0:1])))
            C.V(lambda v: ph.inc(v.reciprocal(out=statm[:, 5 * TT:6 * TT],
                                              in_=statm[:, 4 * TT:5 * TT])))
            def bcst(t):
                t.matmul(out=pM[:, :], lhsT=onesc[:, :], rhs=statm[:, 0 * TT:1 * TT],
                         start=True, stop=True)
                ph.inc(t.matmul(out=pR[:, :], lhsT=onesc[:, :],
                                rhs=statm[:, 5 * TT:6 * TT], start=True, stop=True))
            C.T(bcst)
            for j in range(NFC):
                def norm(v, j=j):
                    v.tensor_sub(out=sqf[:, :], in0=xb[:, j * TT:(j + 1) * TT], in1=pM[:, :])
                    ph.inc(v.tensor_mul(out=xnb[:, :], in0=sqf[:, :], in1=pR[:, :]))
                C.V(norm)
                def stxn(g, tt=tt, j=j):
                    ph.inc(g.dma_start(out=xn_out[j * 128:(j + 1) * 128,
                                                  tt * TT:(tt + 1) * TT],
                                       in_=xnb[:, :]), 16)
                C.GQ(stxn)


def build_fused(core_check=None, upto=99):
    nc = bass.Bass("TRN2", target_bir_lowering=False, debug=False)
    EI = lambda n, sh, dt=F32: nc.dram_tensor(n, sh, dt, kind="ExternalInput").ap()
    IN = lambda n, sh, dt=F32: nc.dram_tensor(n, sh, dt, kind="Internal").ap()
    SH = lambda n, sh, dt=F32: nc.dram_tensor(n, sh, dt, kind="Internal", addr_space="Shared").ap()
    x_sl = EI("x_sl", [D, TPC])
    xn_sl = IN("xn_sl", [D, TPC], BF16)
    xn_g = IN("xn_g", [NCORES * D, TPC], BF16)
    pattn = IN("pattn", [NCORES * D, TPC], BF16)
    attn_sl = IN("attn_sl", [D, TPC], BF16)
    x1_sl = IN("x1_sl", [D, TPC], F32)
    xn1_sl = IN("xn1_sl", [D, TPC], BF16)
    xn1_g = IN("xn1_g", [NCORES * D, TPC], BF16)
    pattn2 = IN("pattn2", [NCORES * D, TPC], BF16)
    attn2_sl = IN("attn2_sl", [D, TPC], BF16)
    x2_sl = nc.dram_tensor("x2_sl", [D, TPC], F32, kind="ExternalOutput").ap()

    with ExitStack() as st:
        ph = Ctr(st.enter_context(nc.semaphore("ph")))
        block = st.enter_context(nc.Block())
        st.enter_context(nc.allow_low_precision(
            reason="bf16 elementwise per problem statement (honest bf16 blessed)"))
        C = mkctx(nc, st, block, ph, "f_")

        emit_ln(C, x_sl, xn_sl)
        if upto >= 1:
            emit_coll(C, "AllGather", xn_sl, xn_g)
        if upto >= 2:
            emit_l0attn_f(C, nc, xn_g, pattn)
        if upto >= 3:
            emit_coll(C, "ReduceScatter", pattn, attn_sl)
        if upto >= 4:
            emit_mid_f(C, nc, attn_sl, xn_sl, x_sl, x1_sl, xn1_sl)
        if upto >= 5:
            emit_coll(C, "AllGather", xn1_sl, xn1_g)
        if upto >= 6:
            emit_l1attn_f(C, nc, xn1_g, pattn2)
        if upto >= 7:
            emit_coll(C, "ReduceScatter", pattn2, attn2_sl)
        if upto >= 8:
            emit_fin_f(C, nc, attn2_sl, xn1_sl, x1_sl, x2_sl)
        if upto < 8:
            fin_map = {0: xn_sl, 1: xn_g, 2: pattn, 3: attn_sl, 4: xn1_sl,
                       5: xn1_g, 6: pattn2, 7: attn2_sl}
            srcap = fin_map[upto]
            dbgo = nc.dram_tensor("dbgo", list(srcap.shape), srcap.dtype,
                                  kind="ExternalOutput").ap()

            @block.gpsimd
            def _(g):
                ph.wait(g)
                rows = srcap.shape[0]
                step = min(2048, rows)
                for r0 in range(0, rows, step):
                    ph.inc(g.dma_start(out=dbgo[r0:r0 + step, :],
                                       in_=srcap[r0:r0 + step, :]), 16)

        @block.gpsimd
        def _(g):
            ph.wait(g)
    return nc


def emit_l0attn_f(C, nc, xn_g, pattn):
    GT = 512
    NG = TOK // GT
    NCH = GT // 128
    wq = nc.dram_tensor("wq", [D, 256], F32, kind="ExternalInput").ap()
    wk = nc.dram_tensor("wk", [D, 256], F32, kind="ExternalInput").ap()
    wv = nc.dram_tensor("wv", [D, 256], F32, kind="ExternalInput").ap()
    cw = nc.dram_tensor("cw", [128, 24], F32, kind="ExternalInput").ap()
    fw1 = nc.dram_tensor("fw1", [D, 128], F32, kind="ExternalInput").ap()
    fw2 = nc.dram_tensor("fw2", [128, 256], F32, kind="ExternalInput").ap()
    gw1 = nc.dram_tensor("gw1", [D, 128], F32, kind="ExternalInput").ap()
    gw2 = nc.dram_tensor("gw2", [128, 256], F32, kind="ExternalInput").ap()
    gb2 = nc.dram_tensor("gb2", [1, 256], F32, kind="ExternalInput").ap()
    bw = nc.dram_tensor("bw", [D, 2], F32, kind="ExternalInput").ap()
    alog = nc.dram_tensor("alog", [1, 2], F32, kind="ExternalInput").ap()
    dtb = nc.dram_tensor("dtb", [128, 2], F32, kind="ExternalInput").ap()
    owp = nc.dram_tensor("owp", [256, D], F32, kind="ExternalInput").ap()
    onw = nc.dram_tensor("onw", [128, 1], F32, kind="ExternalInput").ap()
    CST = l0attn_consts()
    I32 = mybir.dt.int32
    cin = {}
    for k, v in CST.items():
        dt = BF16 if v.dtype == BF16NP else (I32 if v.dtype == np.int32 else F32)
        cin[k] = nc.dram_tensor(k, list(v.shape), dt, kind="ExternalInput").ap()

    with ExitStack() as st:
        def sbuf(n, sh, dt=F32):
            return st.enter_context(nc.sbuf_tensor("a_" + n, sh, dt))
        wqs = sbuf("wqs", [128, NFC * 256], BF16)
        wks = sbuf("wks", [128, NFC * 256], BF16)
        wvs = sbuf("wvs", [128, NFC * 256], BF16)
        fw1s = sbuf("fw1s", [128, NFC * 128], BF16)
        gw1s = sbuf("gw1s", [128, NFC * 128], BF16)
        fw2s = sbuf("fw2s", [128, 256], BF16)
        gw2s = sbuf("gw2s", [128, 256], BF16)
        gb2s = sbuf("gb2s", [1, 256], F32)
        bws = sbuf("bws", [128, NFC * 2], BF16)
        cws = sbuf("cws", [128, 24], F32)
        alogs = sbuf("alogs", [1, 2], F32)
        negA = sbuf("negA", [128, 2], F32)
        dts = sbuf("dts", [128, 2], F32)
        onesc = sbuf("onesc", [1, 128], F32)
        ones1b = sbuf("ones1b", [1, 128], BF16)
        onecol = sbuf("onecol", [128, 1], F32)
        onebias = sbuf("onebias", [128, 1], F32)
        epsl2 = sbuf("epsl2", [1, 1], F32)
        cs = {}
        for k, v in CST.items():
            dt = BF16 if v.dtype == BF16NP else (mybir.dt.int32 if v.dtype == np.int32 else F32)
            cs[k] = sbuf("c2_" + k, list(v.shape), dt)
        mskd = sbuf("mskd", [128, 128], mybir.dt.int32)
        xng = sbuf("xng0", [128, NFC * GT], BF16)
        raw = {nm: sbuf("raw" + nm, [128, GT + 3], BF16)
               for nm in ("q0", "q1", "k0", "k1", "v0", "v1")}
        qh = [sbuf(f"qh{h}", [128, GT], BF16) for h in range(2)]
        kh = [sbuf(f"kh{h}", [128, GT], BF16) for h in range(2)]
        vh = [sbuf(f"vh{h}", [128, GT], BF16) for h in range(2)]
        innf = sbuf("innf", [128, GT], BF16)
        inng = sbuf("inng", [128, GT], BF16)
        betac = sbuf("betac", [2, GT], F32)
        nsc = sbuf("nsc", [1, GT], F32)
        beta_f = sbuf("beta_f", [128, NCH * 2], F32)
        beta_fn = sbuf("beta_fn", [128, NCH * 2], F32)
        gate_tm = sbuf("gate_tm", [128, NCH * 256], BF16)
        Gf = sbuf("Gf", [128, GT], F32)
        RDf = sbuf("RDf", [128, GT], F32)
        SDf = sbuf("SDf", [128, GT], F32)
        ERf = sbuf("ERf", [128, GT], F32)
        gkf = sbuf("gkf", [128, GT], F32)
        sp1 = sbuf("sp1", [128, GT], F32)
        spb = sbuf("spb", [128, GT], F32)
        spc = sbuf("spc", [128, GT], BF16)
        spb2 = sbuf("spb2", [128, GT], BF16)
        mapn = ["EQ", "ED", "rowb", "colb", "r8", "c8", "r16", "c16", "r32", "c32", "r64", "c64"]
        maps = {k: sbuf("m_" + k, [128, GT], BF16) for k in mapn}
        C1 = sbuf("C1", [128, GT], BF16); C2 = sbuf("C2", [128, GT], BF16)
        C3 = sbuf("C3", [128, GT], BF16); C4 = sbuf("C4", [128, GT], BF16)
        C5 = sbuf("C5", [128, GT], BF16); C6 = sbuf("C6", [128, GT], BF16)
        RBe = sbuf("RBe", [128, 2 * GT], BF16); RBo = sbuf("RBo", [128, 2 * GT], BF16)
        R8 = sbuf("R8", [128, 2 * GT], BF16); R16 = sbuf("R16", [128, 2 * GT], BF16)
        R32 = sbuf("R32", [128, 2 * GT], BF16); R64 = sbuf("R64", [128, 2 * GT], BF16)
        Qs = sbuf("Qs", [128, GT], BF16); Ws = sbuf("Ws", [128, GT], BF16)
        KD = sbuf("KD", [128, GT], BF16)
        eGend = sbuf("eGend", [128, NCH], F32)
        Aqt4 = sbuf("Aqt4", [128, NCH * 128], BF16)
        Ntn4 = sbuf("Ntn4", [128, NCH * 128], BF16)
        powT4 = [sbuf(f"powT4_{i}", [128, NCH * 128], BF16) for i in range(6)]
        pown4 = [sbuf(f"pown4_{i}", [128, NCH * 128], BF16) for i in range(6)]
        ybuf4 = [sbuf(f"ybuf4_{i}", [128, NCH * 256], BF16) for i in range(2)]
        rhsb4 = sbuf("rhsb4", [128, NCH * 256], BF16)
        S_f = [sbuf(f"S_f{h}", [128, 128], F32) for h in range(2)]
        S_b = [sbuf(f"S_b{h}", [128, 128], BF16) for h in range(2)]
        UWc = sbuf("UWc", [128, 128], BF16)
        KDt = sbuf("KDt", [128, 128], BF16)
        xdel = sbuf("xdel", [128, 128], BF16)
        useq = sbuf("useq", [128, 128], BF16)
        ssq = sbuf("ssq", [128, 12], F32)
        obuf = sbuf("obuf", [128, NCH * 256], BF16)
        ows2 = sbuf("ows2", [128, 2 * D], BF16)
        ofcm = sbuf("ofcm", [128, 2 * GT], BF16)
        patb2 = sbuf("patb2", [128, 2 * GT], BF16)
        onws = sbuf("onws", [128, 1], F32)
        pA = st.enter_context(nc.psum_tensor("a_pA", [128, GT], F32))
        pC = st.enter_context(nc.psum_tensor("a_pC", [2, GT], F32))
        pScoreB = st.enter_context(nc.psum_tensor("a_pScoreB", [128, NCH * 256], F32))
        pStageB = st.enter_context(nc.psum_tensor("a_pStageB", [16, NCH * 32], F32))
        pT4 = st.enter_context(nc.psum_tensor("a_pT4", [128, NCH * 128], BF16))
        pSolve = st.enter_context(nc.psum_tensor("a_pSolve", [128, NCH * 256], F32))
        ph = C.ph
        block = C.block

        def T(fn):
            @block.tensor
            def _(t):
                ph.wait(t)
                fn(t)
        def V(fn):
            @block.vector
            def _(v):
                ph.wait(v)
                fn(v)
        def SC(fn):
            @block.scalar
            def _(s):
                ph.wait(s)
                fn(s)
        def G3(fn):
            @block.gpsimd
            def _(g):
                ph.wait(g)
                fn(g)

        @block.gpsimd
        def _(g):
            ph.wait(g)
            for w_dram, w_sb, nch in ((wq, wqs, 256), (wk, wks, 256), (wv, wvs, 256),
                                      (fw1, fw1s, 128), (gw1, gw1s, 128), (bw, bws, 2)):
                src = w_dram.rearrange("(j p) n -> p j n", p=128)
                dst = w_sb[:, :].rearrange("p (j n) -> p j n", n=nch)
                ph.inc(g.dma_start(out=dst, in_=src), 16)
            for w_dram, w_sb in ((fw2, fw2s), (gw2, gw2s), (cw, cws)):
                ph.inc(g.dma_start(out=w_sb[:, :], in_=w_dram), 16)
            srcw = owp.rearrange("(j p) n -> p j n", p=128)
            dstw = ows2[:, :].rearrange("p (j n) -> p j n", n=D)
            ph.inc(g.dma_start(out=dstw, in_=srcw), 16)

        @block.sync
        def _(sy):
            ph.wait(sy)
            ph.inc(sy.dma_start(out=gb2s[:, :], in_=gb2), 16)
            ph.inc(sy.dma_start(out=alogs[:, :], in_=alog), 16)
            ph.inc(sy.dma_start(out=dts[:, :], in_=dtb), 16)
            ph.inc(sy.dma_start(out=onws[:, :], in_=onw), 16)
            for k in CST:
                ph.inc(sy.dma_start(out=cs[k][:, :], in_=cin[k]), 16)

        def setup1(v):
            v.memset(onesc[:, :], 1.0)
            v.memset(ones1b[:, :], 1.0)
            v.memset(onecol[:, :], 1.0)
            v.memset(onebias[:, :], 1.0)
            v.memset(epsl2[:, :], 1e-6)
            for h in range(2):
                v.memset(S_f[h][:, :], 0.0)
                v.memset(S_b[h][:, :], 0.0)
            ph.inc(v.engine_nop())
        V(setup1)
        SC(lambda s: ph.inc(s.activation(out=alogs[:, :], in_=alogs[:, :], func=AF.Exp)))
        V(lambda v: ph.inc(v.tensor_scalar_mul(out=alogs[:, :], in0=alogs[:, :], scalar1=-1.0)))
        T(lambda t: ph.inc(t.matmul(out=pA[:, 0:2], lhsT=onesc[:, :], rhs=alogs[:, :],
                                    start=True, stop=True)))
        V(lambda v: ph.inc(v.tensor_copy(out=negA[:, :], in_=pA[:, 0:2])))

        for g in range(NG):
            bstart = (g % max(NG // 4, 8) == 0)
            xbuf = xng

            def loadxn(sy, g=g, xbuf=xbuf):
                rk = (g * GT) // TPC
                cb = (g * GT) % TPC
                for j in range(NFC):
                    ins = sy.dma_start(out=xbuf[:, j * GT:(j + 1) * GT],
                                       in_=xn_g[rk * D + j * 128:rk * D + (j + 1) * 128,
                                                cb:cb + GT])
                    ph.inc(ins, 16)
            @block.sync
            def _(sy):
                ph.wait(sy)
                loadxn(sy)

            for pi, wsb in enumerate((wqs, wks, wvs)):
                for ht in range(2):
                    nm = ("q0q1", "k0k1", "v0v1")[pi][ht * 2:ht * 2 + 2]
                    rb = raw[nm]
                    def mmproj(t, wsb=wsb, ht=ht, xbuf=xbuf):
                        for j in range(NFC):
                            ins = t.matmul(out=pA[:, :],
                                           lhsT=wsb[:, j * 256 + ht * 128:j * 256 + ht * 128 + 128],
                                           rhs=xbuf[:, j * GT:(j + 1) * GT],
                                           start=(j == 0), stop=(j == NFC - 1))
                        ph.inc(ins)
                    T(mmproj)
                    def convf(v, rb=rb, pi=pi, ht=ht, bstart=bstart):
                        v.tensor_copy(out=rb[:, 3:3 + GT], in_=pA[:, :])
                        if bstart:
                            v.memset(rb[:, 0:3], 0.0)
                        v.engine_nop()
                        ci = (pi * 2 + ht) * 4
                        v.tensor_scalar(out=spc[:, :], in0=rb[:, 0:GT],
                                        scalar1=cws[:, ci:ci + 1], scalar2=None, op0=ALU.mult)
                        for i in range(1, 4):
                            v.scalar_tensor_tensor(out=spc[:, :], in0=rb[:, i:i + GT],
                                                   scalar=cws[:, ci + i:ci + i + 1],
                                                   in1=spc[:, :], op0=ALU.mult, op1=ALU.add)
                        ph.inc(v.tensor_copy(out=rb[:, 0:3], in_=rb[:, GT:GT + 3]))
                    V(convf)
                    dst = (qh, kh, vh)[pi][ht]
                    SC(lambda s: ph.inc(s.activation(out=sp1[:, :], in_=spc[:, :],
                                                     func=AF.Tanh, scale=0.5)))
                    def silu_v(v, dst=dst):
                        v.tensor_scalar(out=spb2[:, :], in0=sp1[:, :],
                                        scalar1=0.5, scalar2=0.5, op0=ALU.mult, op1=ALU.add)
                        v.engine_nop()
                        ph.inc(v.tensor_mul(out=dst[:, :], in0=spc[:, :], in1=spb2[:, :]))
                    V(silu_v)

            for pi in range(2):
                for ht in range(2):
                    src = (qh, kh)[pi][ht]
                    SC(lambda s, src=src: ph.inc(s.activation(out=sp1[:, :], in_=src[:, :],
                                                              func=AF.Square)))
                    T(lambda t: ph.inc(t.matmul(out=pC[0:1, :], lhsT=onecol[:, 0:1],
                                                rhs=sp1[:, :], start=True, stop=True)))
                    SC(lambda s: ph.inc(s.activation(out=nsc[0:1, :], in_=pC[0:1, :],
                                                     func=AF.Sqrt, bias=epsl2[0:1, 0:1])))
                    V(lambda v: ph.inc(v.reciprocal(out=nsc[0:1, :], in_=nsc[0:1, :])))
                    T(lambda t: ph.inc(t.matmul(out=pA[:, :], lhsT=onesc[:, :],
                                                rhs=nsc[0:1, :], start=True, stop=True)))
                    scl = (128.0 ** -0.5) if pi == 0 else 1.0
                    def l2fin(v, src=src, scl=scl):
                        v.tensor_scalar_mul(out=spb2[:, :], in0=pA[:, :], scalar1=scl)
                        v.engine_nop()
                        ph.inc(v.tensor_mul(out=src[:, :], in0=src[:, :], in1=spb2[:, :]))
                    V(l2fin)

            for wsb, dst in ((fw1s, innf), (gw1s, inng)):
                def mminner(t, wsb=wsb, xbuf=xbuf):
                    for j in range(NFC):
                        ins = t.matmul(out=pA[:, :], lhsT=wsb[:, j * 128:(j + 1) * 128],
                                       rhs=xbuf[:, j * GT:(j + 1) * GT],
                                       start=(j == 0), stop=(j == NFC - 1))
                    ph.inc(ins)
                T(mminner)
                V(lambda v, dst=dst: ph.inc(v.tensor_copy(out=dst[:, :], in_=pA[:, :])))

            def mmbeta(t, xbuf=xbuf):
                for j in range(NFC):
                    ins = t.matmul(out=pC[:, :], lhsT=bws[:, j * 2:(j + 1) * 2],
                                   rhs=xbuf[:, j * GT:(j + 1) * GT],
                                   start=(j == 0), stop=(j == NFC - 1))
                ph.inc(ins)
            T(mmbeta)
            SC(lambda s: ph.inc(s.activation(out=betac[:, :], in_=pC[:, :],
                                             func=AF.Tanh, scale=0.5)))
            V(lambda v: ph.inc(v.tensor_scalar(out=betac[:, :], in0=betac[:, :],
                                               scalar1=0.5, scalar2=0.5,
                                               op0=ALU.mult, op1=ALU.add)))
            def trbeta(t):
                for ch in range(NCH):
                    ins = t.transpose(out=pSolve[:, ch * 256:ch * 256 + 2],
                                      in_=betac[:, ch * 128:(ch + 1) * 128],
                                      identity=cs["identf"][0:2, 0:2])
                ph.inc(ins)
            T(trbeta)
            def betaev(v):
                for ch in range(NCH):
                    v.tensor_copy(out=beta_f[:, ch * 2:(ch + 1) * 2],
                                  in_=pSolve[:, ch * 256:ch * 256 + 2])
                for ch in range(NCH):
                    ins = v.tensor_scalar_mul(out=beta_fn[:, ch * 2:(ch + 1) * 2],
                                              in0=pSolve[:, ch * 256:ch * 256 + 2],
                                              scalar1=-1.0)
                ph.inc(ins)
            V(betaev)

            for cp in range(NCH // 2):
                def mmgate(t, cp=cp):
                    for q in range(2):
                        ch = cp * 2 + q
                        t.matmul(out=pA[:, q * 256:q * 256 + 256],
                                 lhsT=inng[:, ch * 128:(ch + 1) * 128],
                                 rhs=gw2s[:, :], start=True, stop=False)
                        ins = t.matmul(out=pA[:, q * 256:q * 256 + 256], lhsT=onesc[:, :],
                                       rhs=gb2s[:, :], start=False, stop=True,
                                       skip_group_check=True)
                    ph.inc(ins)
                T(mmgate)
                SC(lambda s: ph.inc(s.activation(out=sp1[:, :], in_=pA[:, :],
                                                 func=AF.Tanh, scale=0.5)))
                def gatefin(v, cp=cp):
                    ph.inc(v.tensor_scalar(out=gate_tm[:, cp * 512:(cp + 1) * 512],
                                           in0=sp1[:, :], scalar1=0.5, scalar2=0.5,
                                           op0=ALU.mult, op1=ALU.add))
                V(gatefin)

            for ht in range(2):
                T(lambda t, ht=ht: ph.inc(t.matmul(out=pA[:, :],
                                                   lhsT=fw2s[:, ht * 128:(ht + 1) * 128],
                                                   rhs=innf[:, :], start=True, stop=True)))
                V(lambda v, ht=ht: ph.inc(v.tensor_scalar_add(out=sp1[:, :], in0=pA[:, :],
                                                              scalar1=dts[:, ht:ht + 1])))
                SC(lambda s: ph.inc(s.activation(out=gkf[:, :], in_=sp1[:, :], func=AF.Exp)))
                SC(lambda s: ph.inc(s.activation(out=sp1[:, :], in_=gkf[:, :], func=AF.Ln,
                                                 bias=onebias[:, 0:1])))
                V(lambda v, ht=ht: ph.inc(v.tensor_scalar(out=gkf[:, :], in0=sp1[:, :],
                                                          scalar1=negA[:, ht:ht + 1],
                                                          scalar2=-CLAMP,
                                                          op0=ALU.mult, op1=ALU.max)))

                def scan_map(rm, dst):
                    V(lambda v, rm=rm, dst=dst: ph.inc(
                        v.tensor_tensor_scan(out=dst[:, :], data0=cs[rm][:, :],
                                             data1=gkf[:, :], initial=0.0,
                                             op0=ALU.mult, op1=ALU.add)))
                def sufd(v, src, dst, last, step):
                    nb = GT // step
                    gath = src[:, last::step].unsqueeze(2).broadcast_to([128, nb, step])
                    s3 = src[:, :].rearrange("p (a b) -> p a b", b=step)
                    d3 = dst[:, :].rearrange("p (a b) -> p a b", b=step)
                    return v.tensor_sub(out=d3, in0=gath, in1=s3)
                def expm(s, src, dst, scale=1.0):
                    return s.activation(out=maps[dst][:, :], in_=src[:, :], func=AF.Exp,
                                        scale=scale)

                scan_map("rm128", Gf)
                V(lambda v: ph.inc(sufd(v, Gf, SDf, 127, 128)))
                def exps128(s):
                    expm(s, Gf, "EQ")
                    expm(s, SDf, "ED")
                    ph.inc(s.activation(out=eGend[:, 0:NCH], in_=Gf[:, 127::128],
                                        func=AF.Exp))
                SC(exps128)
                scan_map("rm8", RDf)
                def d8(v):
                    sufd(v, RDf, SDf, 7, 8)
                    gath = RDf[:, 3::8].unsqueeze(2).broadcast_to([128, GT // 8, 8])
                    s3 = RDf[:, :].rearrange("p (a b) -> p a b", b=8)
                    d3 = ERf[:, :].rearrange("p (a b) -> p a b", b=8)
                    ph.inc(v.tensor_sub(out=d3, in0=s3, in1=gath))
                V(d8)
                def exps8(s):
                    expm(s, RDf, "r8")
                    expm(s, SDf, "c8")
                    expm(s, ERf, "rowb")
                    ph.inc(expm(s, ERf, "colb", scale=-1.0))
                SC(exps8)
                for m, rn, cn in ((16, "r16", "c16"), (32, "r32", "c32"), (64, "r64", "c64")):
                    scan_map(f"rm{m}", RDf)
                    V(lambda v, m=m: ph.inc(sufd(v, RDf, SDf, m - 1, m)))
                    def expsm(s, rn=rn, cn=cn):
                        expm(s, RDf, rn)
                        ph.inc(expm(s, SDf, cn))
                    SC(expsm)

                def optiles(v, ht=ht):
                    q = qh[ht]; k = kh[ht]
                    v.tensor_mul(out=Qs[:, :], in0=q[:, :], in1=maps["EQ"][:, :])
                    v.tensor_mul(out=Ws[:, :], in0=k[:, :], in1=maps["EQ"][:, :])
                    v.tensor_mul(out=KD[:, :], in0=k[:, :], in1=maps["ED"][:, :])
                    v.tensor_mul(out=C1[:, :], in0=k[:, :], in1=maps["colb"][:, :])
                    v.tensor_mul(out=C2[:, :], in0=C1[:, :], in1=cs["odd8"][:, :])
                    v.tensor_mul(out=C1[:, :], in0=C1[:, :], in1=cs["evn8"][:, :])
                    v.tensor_mul(out=RBe[:, 0:GT], in0=q[:, :], in1=maps["rowb"][:, :])
                    v.tensor_mul(out=RBe[:, GT:2 * GT], in0=k[:, :], in1=maps["rowb"][:, :])
                    v.tensor_mul(out=RBo[:, 0:GT], in0=RBe[:, 0:GT], in1=cs["odd8"][:, :])
                    v.tensor_mul(out=RBo[:, GT:2 * GT], in0=RBe[:, GT:2 * GT], in1=cs["odd8"][:, :])
                    v.tensor_mul(out=RBe[:, 0:GT], in0=RBe[:, 0:GT], in1=cs["evn8"][:, :])
                    v.tensor_mul(out=RBe[:, GT:2 * GT], in0=RBe[:, GT:2 * GT], in1=cs["evn8"][:, :])
                    v.tensor_mul(out=C3[:, :], in0=k[:, :], in1=maps["c8"][:, :])
                    v.tensor_mul(out=C3[:, :], in0=C3[:, :], in1=cs["evn8"][:, :])
                    v.tensor_mul(out=R8[:, 0:GT], in0=q[:, :], in1=maps["r8"][:, :])
                    v.tensor_mul(out=R8[:, GT:2 * GT], in0=k[:, :], in1=maps["r8"][:, :])
                    v.tensor_mul(out=R8[:, 0:GT], in0=R8[:, 0:GT], in1=cs["odd8"][:, :])
                    v.tensor_mul(out=R8[:, GT:2 * GT], in0=R8[:, GT:2 * GT], in1=cs["odd8"][:, :])
                    for Cx, Rx, rn, cn in ((C4, R16, "r16", "c16"), (C5, R32, "r32", "c32"),
                                           (C6, R64, "r64", "c64")):
                        v.tensor_mul(out=Cx[:, :], in0=k[:, :], in1=maps[cn][:, :])
                        v.tensor_mul(out=Rx[:, 0:GT], in0=q[:, :], in1=maps[rn][:, :])
                        v.tensor_mul(out=Rx[:, GT:2 * GT], in0=k[:, :], in1=maps[rn][:, :])
                    ph.inc(v.engine_nop())
                V(optiles)

                def scoresb(t):
                    for ch in range(NCH):
                        c0 = ch * 128
                        def rhs2(Rt, lo, hi):
                            r3 = Rt[:, :].rearrange("p (h t) -> p h t", h=2)
                            return r3[:, :, c0 + lo:c0 + hi]
                        ps3 = pScoreB[:, ch * 256:(ch + 1) * 256].rearrange(
                            "p (h t) -> p h t", h=2)
                        t.matmul(out=ps3[:, :, :], lhsT=C1[:, c0:c0 + 128],
                                 rhs=rhs2(RBe, 0, 128), start=True, stop=False)
                        t.matmul(out=ps3[:, :, :], lhsT=C2[:, c0:c0 + 128],
                                 rhs=rhs2(RBo, 0, 128), start=False, stop=False)
                        t.matmul(out=ps3[:, :, :], lhsT=C3[:, c0:c0 + 128],
                                 rhs=rhs2(R8, 0, 128), start=False, stop=True)
                        for p in range(3):
                            t.matmul(out=ps3[32 * p:32 * p + 16, :, 32 * p + 16:32 * p + 32],
                                     lhsT=C4[:, c0 + 32 * p:c0 + 32 * p + 16],
                                     rhs=rhs2(R16, 32 * p + 16, 32 * p + 32),
                                     start=True, stop=True)
                        st3 = pStageB[:, ch * 32:(ch + 1) * 32].rearrange(
                            "p (h t) -> p h t", h=2)
                        t.matmul(out=st3[0:16, :, 0:16], lhsT=C4[:, c0 + 96:c0 + 112],
                                 rhs=rhs2(R16, 112, 128), start=True, stop=True)
                        for p in range(2):
                            t.matmul(out=ps3[64 * p:64 * p + 32, :, 64 * p + 32:64 * p + 64],
                                     lhsT=C5[:, c0 + 64 * p:c0 + 64 * p + 32],
                                     rhs=rhs2(R32, 64 * p + 32, 64 * p + 64),
                                     start=True, stop=True)
                        ins = t.matmul(out=ps3[0:64, :, 64:128], lhsT=C6[:, c0:c0 + 64],
                                       rhs=rhs2(R64, 64, 128), start=True, stop=True)
                    ph.inc(ins)
                T(scoresb)

                def evacb(v, ht=ht):
                    for ch in range(NCH):
                        cA = ch * 128
                        v.tensor_tensor(out=mskd[:, :],
                                        in0=pScoreB[:, ch * 256:ch * 256 + 128]
                                        .bitcast(mybir.dt.int32),
                                        in1=cs["maskAb"][:, :], op=ALU.bitwise_and)
                        v.engine_nop()
                        v.tensor_copy(out=Aqt4[:, cA:cA + 128], in_=mskd[:, :].bitcast(F32))
                        v.tensor_tensor(out=mskd[:, :],
                                        in0=pScoreB[:, ch * 256 + 128:ch * 256 + 256]
                                        .bitcast(mybir.dt.int32),
                                        in1=cs["maskMb"][:, :], op=ALU.bitwise_and)
                        v.engine_nop()
                        v.tensor_scalar_mul(out=Ntn4[:, cA:cA + 128],
                                            in0=mskd[:, :].bitcast(F32),
                                            scalar1=beta_fn[:, ch * 2 + ht:ch * 2 + ht + 1])
                        v.tensor_copy(out=Aqt4[96:112, cA + 112:cA + 128],
                                      in_=pStageB[0:16, ch * 32:ch * 32 + 16])
                        ins = v.tensor_scalar_mul(
                            out=Ntn4[96:112, cA + 112:cA + 128],
                            in0=pStageB[0:16, ch * 32 + 16:ch * 32 + 32],
                            scalar1=beta_fn[96:112, ch * 2 + ht:ch * 2 + ht + 1])
                    ph.inc(ins)
                V(evacb)

                def trz(t):
                    for ch in range(NCH):
                        ins = t.transpose(out=pT4[:, ch * 128:(ch + 1) * 128],
                                          in_=Ntn4[:, ch * 128:(ch + 1) * 128],
                                          identity=cs["identb"][:, :])
                    ph.inc(ins)
                T(trz)
                def cpz(v):
                    for ch in range(NCH):
                        ins = v.tensor_copy(out=pown4[0][:, ch * 128:(ch + 1) * 128],
                                            in_=pT4[:, ch * 128:(ch + 1) * 128])
                    ph.inc(ins)
                V(cpz)
                for j in range(1, 7):
                    def sqrb(t, j=j):
                        for ch in range(NCH):
                            sl = slice(ch * 128, (ch + 1) * 128)
                            lastT = Ntn4[:, sl] if j == 1 else powT4[j - 2][:, sl]
                            lastN = pown4[j - 1][:, sl]
                            ins = t.matmul(out=pA[:, sl], lhsT=lastN, rhs=lastT,
                                           start=True, stop=True)
                            if j < 6:
                                ins = t.matmul(out=pSolve[:, sl], lhsT=lastT, rhs=lastN,
                                               start=True, stop=True)
                        ph.inc(ins)
                    T(sqrb)
                    def sqevb(v, j=j):
                        for ch in range(NCH):
                            sl = slice(ch * 128, (ch + 1) * 128)
                            ins = v.tensor_copy(out=powT4[j - 1][:, sl], in_=pA[:, sl])
                            if j < 6:
                                ins = v.tensor_copy(out=pown4[j][:, sl], in_=pSolve[:, sl])
                        ph.inc(ins)
                    V(sqevb)
                def trv(t, ht=ht):
                    for ch in range(NCH):
                        ins = t.transpose(out=pT4[:, ch * 128:(ch + 1) * 128],
                                          in_=vh[ht][:, ch * 128:(ch + 1) * 128],
                                          identity=cs["identb"][:, :])
                    ph.inc(ins)
                T(trv)
                def cpv(v):
                    for ch in range(NCH):
                        ins = v.tensor_copy(out=rhsb4[:, ch * 256:ch * 256 + 128],
                                            in_=pT4[:, ch * 128:(ch + 1) * 128])
                    ph.inc(ins)
                V(cpv)
                def trw(t):
                    for ch in range(NCH):
                        ins = t.transpose(out=pT4[:, ch * 128:(ch + 1) * 128],
                                          in_=Ws[:, ch * 128:(ch + 1) * 128],
                                          identity=cs["identb"][:, :])
                    ph.inc(ins)
                T(trw)
                def cpw(v):
                    for ch in range(NCH):
                        ins = v.tensor_copy(out=rhsb4[:, ch * 256 + 128:ch * 256 + 256],
                                            in_=pT4[:, ch * 128:(ch + 1) * 128])
                    ph.inc(ins)
                V(cpw)
                ydl = spb[:, :].bitcast(BF16)
                ycur = rhsb4
                for j in range(6, -1, -1):
                    ynext = ybuf4[j % 2]
                    def chmm(t, j=j, ycur=ycur):
                        for ch in range(NCH):
                            p = Ntn4 if j == 0 else powT4[j - 1]
                            ins = t.matmul(out=pSolve[:, ch * 256:(ch + 1) * 256],
                                           lhsT=p[:, ch * 128:(ch + 1) * 128],
                                           rhs=ycur[:, ch * 256:(ch + 1) * 256],
                                           start=True, stop=True)
                        ph.inc(ins)
                    T(chmm)
                    def chadd(v, ycur=ycur, ynext=ynext):
                        for ch in range(NCH):
                            v.tensor_copy(out=ydl[:, ch * 256:(ch + 1) * 256],
                                          in_=pSolve[:, ch * 256:(ch + 1) * 256])
                        v.engine_nop()
                        for ch in range(NCH):
                            ins = v.tensor_add(out=ynext[:, ch * 256:(ch + 1) * 256],
                                               in0=ycur[:, ch * 256:(ch + 1) * 256],
                                               in1=ydl[:, ch * 256:(ch + 1) * 256])
                        ph.inc(ins)
                    V(chadd)
                    ycur = ynext
                U4 = ycur

                for ch in range(NCH):
                    c0 = ch * 128
                    def trs(t, ch=ch, c0=c0, U4=U4):
                        t.transpose(out=pT4[:, 0:128], in_=U4[:, ch * 256 + 128:ch * 256 + 256],
                                    identity=cs["identb"][:, :])
                        ph.inc(t.transpose(out=pT4[:, 128:256], in_=KD[:, c0:c0 + 128],
                                           identity=cs["identb"][:, :]))
                    T(trs)
                    def cps(v):
                        v.tensor_copy(out=UWc[:, :], in_=pT4[:, 0:128])
                        ph.inc(v.tensor_copy(out=KDt[:, :], in_=pT4[:, 128:256]))
                    V(cps)
                    T(lambda t, ht=ht: ph.inc(t.matmul(out=pSolve[:, 0:128], lhsT=UWc[:, :],
                                                       rhs=S_b[ht][:, :],
                                                       start=True, stop=True)))
                    def useqf(v, U4=U4, ch=ch, ht=ht):
                        v.tensor_copy(out=xdel[:, :], in_=pSolve[:, 0:128])
                        v.engine_nop()
                        v.tensor_sub(out=spb[:, 0:128], in0=U4[:, ch * 256:ch * 256 + 128],
                                     in1=xdel[:, :])
                        v.engine_nop()
                        ph.inc(v.tensor_scalar_mul(out=useq[:, :], in0=spb[:, 0:128],
                                                   scalar1=beta_f[:, ch * 2 + ht:ch * 2 + ht + 1]))
                    V(useqf)
                    def omm3(t, c0=c0, ht=ht):
                        t.matmul(out=pSolve[:, 256:384], lhsT=Qs[:, c0:c0 + 128],
                                 rhs=S_b[ht][:, :], start=True, stop=False)
                        t.matmul(out=pSolve[:, 256:384], lhsT=Aqt4[:, c0:c0 + 128],
                                 rhs=useq[:, :], start=False, stop=True)
                        ph.inc(t.matmul(out=pSolve[:, 128:256], lhsT=KDt[:, :],
                                        rhs=useq[:, :], start=True, stop=True))
                    T(omm3)
                    def supdf(v, ch=ch, ht=ht):
                        v.scalar_tensor_tensor(out=S_f[ht][:, :], in0=S_f[ht][:, :],
                                               scalar=eGend[:, ch:ch + 1],
                                               in1=pSolve[:, 128:256],
                                               op0=ALU.mult, op1=ALU.add)
                        v.tensor_copy(out=S_b[ht][:, :], in_=S_f[ht][:, :])
                        ph.inc(v.tensor_copy(out=sp1[:, c0:c0 + 128],
                                             in_=pSolve[:, 256:384]))
                    V(supdf)

                def episq(s):
                    for ch in range(NCH):
                        ins = s.activation(out=spb[:, ch * 128:(ch + 1) * 128],
                                           in_=sp1[:, ch * 128:(ch + 1) * 128],
                                           func=AF.Square, accum_out=ssq[:, ch:ch + 1])
                    ph.inc(ins)
                SC(episq)
                V(lambda v: ph.inc(v.tensor_scalar(out=ssq[:, 4:8], in0=ssq[:, 0:4],
                                                   scalar1=1.0 / 128, scalar2=1e-5,
                                                   op0=ALU.mult, op1=ALU.add)))
                SC(lambda s: ph.inc(s.activation(out=ssq[:, 8:12], in_=ssq[:, 4:8],
                                                 func=AF.Sqrt)))
                V(lambda v: ph.inc(v.reciprocal(out=ssq[:, 8:12], in_=ssq[:, 8:12])))
                def ofinb(v, ht=ht):
                    for ch in range(NCH):
                        v.tensor_scalar_mul(out=spb2[:, ch * 128:(ch + 1) * 128],
                                            in0=sp1[:, ch * 128:(ch + 1) * 128],
                                            scalar1=ssq[:, 8 + ch:9 + ch])
                    v.engine_nop()
                    for ch in range(NCH):
                        dst = obuf[:, ch * 256 + ht * 128:ch * 256 + ht * 128 + 128]
                        ins = v.tensor_mul(out=dst, in0=spb2[:, ch * 128:(ch + 1) * 128],
                                           in1=gate_tm[:, ch * 256 + ht * 128:
                                                       ch * 256 + ht * 128 + 128])
                    ph.inc(ins)
                V(ofinb)

            for fc in range(2):
                def trob(t, fc=fc):
                    for ch in range(NCH):
                        ins = t.transpose(
                            out=pT4[:, ch * 128:(ch + 1) * 128],
                            in_=obuf[:, ch * 256 + fc * 128:ch * 256 + fc * 128 + 128],
                            identity=cs["identb"][:, :])
                    ph.inc(ins)
                T(trob)
                def cpob(v, fc=fc):
                    for ch in range(NCH):
                        ins = v.tensor_scalar_mul(
                            out=ofcm[:, fc * GT + ch * 128:fc * GT + (ch + 1) * 128],
                            in0=pT4[:, ch * 128:(ch + 1) * 128], scalar1=onws[:, 0:1])
                    ph.inc(ins)
                V(cpob)
            dstb = (g * GT) // TPC
            cb0 = (g * GT) % TPC
            for jp in range(NFC // 2):
                def pmm(t, jp=jp):
                    for q in range(2):
                        jo = jp * 2 + q
                        pdst = pA[:, 0:GT] if q == 0 else pScoreB[:, 0:GT]
                        t.matmul(out=pdst,
                                 lhsT=ows2[:, 0 * D + jo * 128:0 * D + jo * 128 + 128],
                                 rhs=ofcm[:, 0:GT], start=True, stop=False)
                        ins = t.matmul(out=pdst,
                                       lhsT=ows2[:, 1 * D + jo * 128:1 * D + jo * 128 + 128],
                                       rhs=ofcm[:, GT:2 * GT], start=False, stop=True)
                    ph.inc(ins)
                T(pmm)
                def cppat(v):
                    v.tensor_copy(out=patb2[:, 0:GT], in_=pA[:, 0:GT])
                    ph.inc(v.tensor_copy(out=patb2[:, GT:2 * GT], in_=pScoreB[:, 0:GT]))
                V(cppat)
                def stp(gq, jp=jp, dstb=dstb, cb0=cb0):
                    for q in range(2):
                        jo = jp * 2 + q
                        ins = gq.dma_start(
                            out=pattn[dstb * D + jo * 128:dstb * D + (jo + 1) * 128,
                                      cb0:cb0 + GT],
                            in_=patb2[:, q * GT:(q + 1) * GT])
                        ph.inc(ins, 16)
                G3(stp)

        @block.gpsimd
        def _(gq):
            ph.wait(gq)


def emit_mid_f(C, nc, attn_sl, xns, xs, x1o, xn1o):
    debug = False
    TT = 512
    NT = TPC // TT
    E, I = 8, 1408
    NI = I // 128
    rout = nc.dram_tensor("rout", [D, 8], F32, kind="ExternalInput").ap()
    wug = nc.dram_tensor("wug", [E, D, 2 * I], F32, kind="ExternalInput").ap()
    wdn = nc.dram_tensor("wdn", [E, I, D], F32, kind="ExternalInput").ap()
    swu = nc.dram_tensor("swu", [D, 2 * 5632], F32, kind="ExternalInput").ap()
    swd = nc.dram_tensor("swd", [5632, D], F32, kind="ExternalInput").ap()
    identf_in = nc.dram_tensor("identf_m", [128, 128], F32, kind="ExternalInput").ap()
    esel_in = nc.dram_tensor("esel", [8, 8 * 128], BF16, kind="ExternalInput").ap()
    if debug:
        dbg_w8 = nc.dram_tensor("dbg_w8", [8, TPC], F32, kind="ExternalOutput").ap()
        dbg_y = nc.dram_tensor("dbg_y", [128, TPC], F32, kind="ExternalOutput").ap()
        dbg_lg = nc.dram_tensor("dbg_lg", [8, TPC], F32, kind="ExternalOutput").ap()
        dbg_h = nc.dram_tensor("dbg_h", [128, TPC], F32, kind="ExternalOutput").ap()

    with ExitStack() as st:
        def sbuf(n, sh, dt=F32):
            return st.enter_context(nc.sbuf_tensor("m_" + n, sh, dt))
        ph = C.ph
        block = C.block

        def T(fn):
            @block.tensor
            def _(t):
                ph.wait(t)
                fn(t)
        def V(fn):
            @block.vector
            def _(v):
                ph.wait(v)
                fn(v)
        def SC(fn):
            @block.scalar
            def _(s):
                ph.wait(s)
                fn(s)
        def SY(fn):
            @block.sync
            def _(sy):
                ph.wait(sy)
                fn(sy)
        def GQ(fn):
            @block.gpsimd
            def _(g):
                ph.wait(g)
                fn(g)

        y_sb = sbuf("y_sb", [128, NFC * TPC], BF16)
        idf = sbuf("idf", [128, 128], F32)
        onesc = sbuf("onesc", [1, 128], F32)
        onecol = sbuf("onecol", [128, 1], F32)
        epsc = sbuf("epsc", [1, 1], F32)
        gt2 = [sbuf(f"gtile{i}", [128, TT], BF16) for i in range(2)]
        util = sbuf("util", [128, TT], BF16)
        wbt = sbuf("wbt", [128, TT], BF16)
        pA6 = [st.enter_context(nc.psum_tensor(f"m_pA{i}", [128, TT], F32))
               for i in range(6)]
        pL8 = st.enter_context(nc.psum_tensor("m_pL8", [8, TT], F32))
        s_w = st.enter_context(nc.semaphore("m_sw"))
        s_t = st.enter_context(nc.semaphore("m_st"))
        s_sc = st.enter_context(nc.semaphore("m_ssc"))
        s_v = st.enter_context(nc.semaphore("m_sv"))
        s_wd = st.enter_context(nc.semaphore("m_swd"))
        s_dt = st.enter_context(nc.semaphore("m_sdt"))
        s_dv = st.enter_context(nc.semaphore("m_sdv"))

        def setup(v):
            v.memset(onesc[:, :], 1.0)
            v.memset(onecol[:, :], 1.0)
            v.memset(epsc[:, :], 1e-5)
            for j in range(NFC):
                v.memset(y_sb[:, j * TPC:(j + 1) * TPC], 0.0)
            ph.inc(v.engine_nop())
        V(setup)

        with ExitStack() as stB:
            xnsb = stB.enter_context(nc.sbuf_tensor("m_xnsb", [128, NFC * TPC], BF16))
            wga0 = stB.enter_context(nc.sbuf_tensor("m_wga0", [128, NFC * 128], BF16))
            wgb0 = stB.enter_context(nc.sbuf_tensor("m_wgb0", [128, NFC * 128], BF16))
            wga1 = stB.enter_context(nc.sbuf_tensor("m_wga1", [128, NFC * 128], BF16))
            wgb1 = stB.enter_context(nc.sbuf_tensor("m_wgb1", [128, NFC * 128], BF16))
            wdb_a = stB.enter_context(nc.sbuf_tensor("m_wdb", [128, NI * 128], BF16))
            wdb_b = stB.enter_context(nc.sbuf_tensor("m_wdb2", [128, NI * 128], BF16))
            w8cm = stB.enter_context(nc.sbuf_tensor("m_w8cm", [8, TPC], BF16))
            esel = stB.enter_context(nc.sbuf_tensor("m_esel_sb", [8, 8 * 128], BF16))
            stR = stB.enter_context(ExitStack())
            routs = stR.enter_context(nc.sbuf_tensor("m_routs", [128, NFC * 8], BF16))
            lgt = stR.enter_context(nc.sbuf_tensor("m_lgt", [128, 8], F32))
            msk8 = stR.enter_context(nc.sbuf_tensor("m_msk8", [128, 8], F32))
            negbig = stR.enter_context(nc.sbuf_tensor("m_negbig", [128, 1], F32))
            lg2 = stR.enter_context(nc.sbuf_tensor("m_lg2", [128, 8], F32))
            sm = stR.enter_context(nc.sbuf_tensor("m_sm", [128, 6], F32))
            lgcm = stR.enter_context(nc.sbuf_tensor("m_lgcm", [8, TPC], F32))
            def ldxn(sy):
                for j in range(NFC):
                    ph.inc(sy.dma_start(out=xnsb[:, j * TPC:(j + 1) * TPC],
                                        in_=xns[j * 128:(j + 1) * 128, :]), 16)
            SY(ldxn)
            def ldid(sy):
                ph.inc(sy.dma_start(out=idf[:, :], in_=identf_in), 16)
            SY(ldid)
            def ldrt(g):
                src = rout.rearrange("(j p) n -> p j n", p=128)
                dst = routs[:, :].rearrange("p (j n) -> p j n", n=8)
                ph.inc(g.dma_start(out=dst, in_=src), 16)
            GQ(ldrt)
            SY(lambda sy: ph.inc(sy.dma_start(out=esel[:, :], in_=esel_in), 16))
            V(lambda v: ph.inc(v.memset(negbig[:, :], -1e30)))
            for tt in range(NT):
                def mmlg(t, tt=tt):
                    for j in range(NFC):
                        ins = t.matmul(out=pL8[:, :], lhsT=routs[:, j * 8:(j + 1) * 8],
                                       rhs=xnsb[:, j * TPC + tt * TT:j * TPC + (tt + 1) * TT],
                                       start=(j == 0), stop=(j == NFC - 1))
                    ph.inc(ins)
                T(mmlg)
                V(lambda v, tt=tt: ph.inc(v.tensor_copy(out=lgcm[:, tt * TT:(tt + 1) * TT],
                                                  in_=pL8[:, :])))
                for cq in range(TT // 128):
                    def trlg(t, tt=tt, cq=cq):
                        ph.inc(t.transpose(
                            out=pA6[4][:, 0:8],
                            in_=lgcm[:, tt * TT + cq * 128:tt * TT + (cq + 1) * 128],
                            identity=idf[0:8, 0:8]))
                    T(trlg)
                    V(lambda v: ph.inc(v.tensor_copy(out=lgt[:, :], in_=pA6[4][:, 0:8])))
                    V(lambda v: ph.inc(v.reduce_max(out=sm[:, 0:1], in_=lgt[:, :],
                                                    axis=mybir.AxisListType.X)))
                    V(lambda v: ph.inc(v.tensor_scalar(out=msk8[:, :], in0=lgt[:, :],
                                                       scalar1=sm[:, 0:1], scalar2=None,
                                                       op0=ALU.is_ge)))
                    V(lambda v: ph.inc(v.scalar_tensor_tensor(out=lg2[:, :], in0=msk8[:, :],
                                                              scalar=negbig[:, 0:1],
                                                              in1=lgt[:, :], op0=ALU.mult,
                                                              op1=ALU.add)))
                    V(lambda v: ph.inc(v.reduce_max(out=sm[:, 1:2], in_=lg2[:, :],
                                                    axis=mybir.AxisListType.X)))
                    V(lambda v: ph.inc(v.tensor_sub(out=sm[:, 2:3], in0=sm[:, 1:2],
                                                    in1=sm[:, 0:1])))
                    V(lambda v: ph.inc(v.tensor_scalar_mul(out=sm[:, 3:4], in0=sm[:, 0:1],
                                                           scalar1=-1.0)))
                    def top2b(s):
                        s.activation(out=lg2[:, :], in_=lgt[:, :], func=AF.Exp, bias=sm[:, 3:4])
                        ph.inc(s.activation(out=sm[:, 4:5], in_=sm[:, 2:3], func=AF.Exp))
                    SC(top2b)
                    V(lambda v: ph.inc(v.tensor_scalar_add(out=sm[:, 4:5], in0=sm[:, 4:5],
                                                           scalar1=1.0)))
                    V(lambda v: ph.inc(v.reciprocal(out=sm[:, 5:6], in_=sm[:, 4:5])))
                    V(lambda v: ph.inc(v.tensor_scalar(out=lg2[:, :], in0=lg2[:, :],
                                                       scalar1=sm[:, 5:6], scalar2=None,
                                                       op0=ALU.mult)))
                    V(lambda v: ph.inc(v.tensor_scalar(out=lgt[:, :], in0=lgt[:, :],
                                                       scalar1=sm[:, 1:2], scalar2=None,
                                                       op0=ALU.is_ge)))
                    V(lambda v: ph.inc(v.tensor_mul(out=lg2[:, :], in0=lg2[:, :],
                                                    in1=lgt[:, :])))
                    def trw(t):
                        ph.inc(t.transpose(out=pA6[4][0:8, 0:128], in_=lg2[:, :],
                                           identity=idf[:, :]))
                    T(trw)
                    V(lambda v, tt=tt, cq=cq: ph.inc(
                        v.tensor_copy(out=w8cm[:, tt * TT + cq * 128:tt * TT + (cq + 1) * 128],
                                      in_=pA6[4][0:8, 0:128])))

            if debug:
                def dmpw(g):
                    for tt in range(NT):
                        ph.inc(g.dma_start(out=dbg_w8[:, tt * TT:(tt + 1) * TT].bitcast(BF16)[:, 0:TT] if False else dbg_w8[:, tt * TT:(tt + 1) * TT],
                                           in_=w8cm[:, tt * TT:(tt + 1) * TT]), 16)
                    ph.inc(g.dma_start(out=dbg_lg[:, :], in_=lgcm[:, :]), 16)
                GQ(dmpw)
            stR.close()
            hprime = stB.enter_context(nc.sbuf_tensor("m_hprime", [128, NI * TPC], BF16))
            passes = [("e", e) for e in range(E)] + [("s", g_) for g_ in range(4)]
            UPT = NI * NT
            DNT = NFC * NT
            wgp = [(wga0, wgb0), (wga1, wgb1)]
            wdp = [wdb_a, wdb_b]
            for pidx, (kind, ei) in enumerate(passes):
                for icp in range(NI):
                    wl = pidx * NI + icp

                    @block.gpsimd
                    def _(g, kind=kind, ei=ei, icp=icp, wl=wl):
                        if wl == 0:
                            ph.wait(g)
                        if wl >= 2:
                            g.wait_ge(s_t, 4 * (wl - 1))
                        if kind == "e":
                            gsrc = wug[ei, :, icp * 128:(icp + 1) * 128]
                            usrc = wug[ei, :, I + icp * 128:I + (icp + 1) * 128]
                        else:
                            base = ei * NI + icp
                            gsrc = swu[:, base * 128:(base + 1) * 128]
                            usrc = swu[:, 5632 + base * 128:5632 + (base + 1) * 128]
                        wga, wgb = wgp[wl % 2]
                        g.dma_start(out=wga[:, :].rearrange("p (j n) -> p j n", n=128),
                                    in_=gsrc.rearrange("(j p) n -> p j n", p=128)
                                    ).then_inc(s_w, 16)
                        g.dma_start(out=wgb[:, :].rearrange("p (j n) -> p j n", n=128),
                                    in_=usrc.rearrange("(j p) n -> p j n", p=128)
                                    ).then_inc(s_w, 16)
                    for tt in range(NT):
                        k = pidx * UPT + icp * NT + tt
                        q = k % 2

                        @block.tensor
                        def _(t, kind=kind, ei=ei, tt=tt, wl=wl, k=k, q=q, pidx=pidx):
                            if k == 0:
                                ph.wait(t)
                            t.wait_ge(s_w, 32 * (wl + 1))
                            if k >= 2:
                                t.wait_ge(s_v, k - 1)
                            if pidx >= 1:
                                t.wait_ge(s_dv, 64 * pidx)
                            wga, wgb = wgp[wl % 2]
                            for j in range(NFC):
                                t.matmul(out=pA6[2 * q][:, :],
                                         lhsT=wga[:, j * 128:(j + 1) * 128],
                                         rhs=xnsb[:, j * TPC + tt * TT:j * TPC + (tt + 1) * TT],
                                         start=(j == 0), stop=(j == NFC - 1))
                            for j in range(NFC):
                                ins = t.matmul(out=pA6[2 * q + 1][:, :],
                                               lhsT=wgb[:, j * 128:(j + 1) * 128],
                                               rhs=xnsb[:, j * TPC + tt * TT:
                                                        j * TPC + (tt + 1) * TT],
                                               start=(j == 0), stop=(j == NFC - 1))
                            if kind == "e":
                                ins = t.matmul(out=pA6[4 + q][:, :],
                                               lhsT=esel[:, ei * 128:(ei + 1) * 128],
                                               rhs=w8cm[0:8, tt * TT:(tt + 1) * TT],
                                               start=True, stop=True)
                            ins.then_inc(s_t, 1)

                        @block.scalar
                        def _(s, k=k, q=q):
                            if k == 0:
                                ph.wait(s)
                            s.wait_ge(s_t, k + 1)
                            if k >= 2:
                                s.wait_ge(s_v, k - 1)
                            s.activation(out=gt2[q][:, :], in_=pA6[2 * q][:, :],
                                         func=AF.Silu).then_inc(s_sc, 1)

                        @block.vector
                        def _(v, kind=kind, icp=icp, tt=tt, k=k, q=q, pidx=pidx):
                            if k == 0:
                                ph.wait(v)
                            v.wait_ge(s_sc, k + 1)
                            if pidx >= 1:
                                v.wait_ge(s_dt, 64 * pidx)
                            v.tensor_copy(out=util[:, :], in_=pA6[2 * q + 1][:, :])
                            if kind == "e":
                                v.tensor_copy(out=wbt[:, :], in_=pA6[4 + q][:, :])
                            v.engine_nop()
                            dst = hprime[:, icp * TPC + tt * TT:icp * TPC + (tt + 1) * TT]
                            if kind == "e":
                                v.tensor_mul(out=gt2[q][:, :], in0=gt2[q][:, :],
                                             in1=util[:, :])
                                v.tensor_mul(out=dst, in0=gt2[q][:, :],
                                             in1=wbt[:, :]).then_inc(s_v, 1)
                            else:
                                v.tensor_mul(out=dst, in0=gt2[q][:, :],
                                             in1=util[:, :]).then_inc(s_v, 1)
                for jo in range(NFC):
                    dl = pidx * NFC + jo

                    @block.gpsimd
                    def _(g, kind=kind, ei=ei, jo=jo, dl=dl):
                        if dl >= 2:
                            g.wait_ge(s_dt, 4 * (dl - 1))
                        if kind == "e":
                            src = wdn[ei, :, jo * 128:(jo + 1) * 128]
                        else:
                            src = swd[ei * I:(ei + 1) * I, jo * 128:(jo + 1) * 128]
                        wdb = wdp[dl % 2]
                        g.dma_start(out=wdb[:, :].rearrange("p (j n) -> p j n", n=128),
                                    in_=src.rearrange("(j p) n -> p j n", p=128)
                                    ).then_inc(s_wd, 16)
                    for tt in range(NT):
                        m = pidx * DNT + jo * NT + tt

                        @block.tensor
                        def _(t, tt=tt, dl=dl, m=m, pidx=pidx):
                            t.wait_ge(s_wd, 16 * (dl + 1))
                            if m >= 4:
                                t.wait_ge(s_dv, m - 3)
                            t.wait_ge(s_v, UPT * (pidx + 1))
                            t.wait_ge(s_sc, UPT * (pidx + 1))
                            wdb = wdp[dl % 2]
                            for ic in range(NI):
                                ins = t.matmul(out=pA6[m % 4][:, :],
                                               lhsT=wdb[:, ic * 128:(ic + 1) * 128],
                                               rhs=hprime[:, ic * TPC + tt * TT:
                                                          ic * TPC + (tt + 1) * TT],
                                               start=(ic == 0), stop=(ic == NI - 1))
                            ins.then_inc(s_dt, 1)

                        @block.vector
                        def _(v, jo=jo, tt=tt, m=m):
                            v.wait_ge(s_dt, m + 1)
                            v.tensor_copy(out=util[:, :], in_=pA6[m % 4][:, :])
                            v.engine_nop()
                            dst = y_sb[:, jo * TPC + tt * TT:jo * TPC + (tt + 1) * TT]
                            v.tensor_add(out=dst, in0=dst, in1=util[:, :]).then_inc(s_dv, 1)

            @block.vector
            def _(v):
                ph.inc(v.engine_nop())

            if debug:
                def dmph(g):
                    ph.inc(g.dma_start(out=dbg_h[:, :], in_=hprime[:, 0:TPC]), 16)
                GQ(dmph)
        if debug:
            def dmpy(g):
                ph.inc(g.dma_start(out=dbg_y[:, :], in_=y_sb[:, 0:TPC]), 16)
            GQ(dmpy)
        with nc.sbuf_tensor("m_x1b", [128, NFC * TT], F32) as x1b, \
             nc.sbuf_tensor("m_sqf", [128, TT], F32) as sqf, \
             nc.sbuf_tensor("m_statm", [1, 6 * TT], F32) as statm, \
             nc.sbuf_tensor("m_xn1b", [128, TT], BF16) as xn1b, \
             nc.sbuf_tensor("m_f32t", [128, TT], F32) as f32t, \
             nc.sbuf_tensor("m_atb", [128, TT], BF16) as atb:
            for tt in range(NT):
                def ldx(sy, tt=tt):
                    for j in range(NFC):
                        ph.inc(sy.dma_start(out=x1b[:, j * TT:(j + 1) * TT],
                                            in_=xs[j * 128:(j + 1) * 128,
                                                   tt * TT:(tt + 1) * TT]), 16)
                SY(ldx)
                for j in range(NFC):
                    def ldat(sy, tt=tt, j=j):
                        ph.inc(sy.dma_start(out=atb[:, :],
                                            in_=attn_sl[j * 128:(j + 1) * 128,
                                                        tt * TT:(tt + 1) * TT]), 16)
                    SY(ldat)
                    V(lambda v, j=j: ph.inc(v.tensor_copy(out=f32t[:, :], in_=atb[:, :])))
                    V(lambda v, j=j: ph.inc(v.tensor_add(out=x1b[:, j * TT:(j + 1) * TT],
                                                         in0=x1b[:, j * TT:(j + 1) * TT],
                                                         in1=f32t[:, :])))
                    V(lambda v, tt=tt, j=j: ph.inc(
                        v.tensor_copy(out=f32t[:, :],
                                      in_=y_sb[:, j * TPC + tt * TT:j * TPC + (tt + 1) * TT])))
                    V(lambda v, j=j: ph.inc(v.tensor_add(out=x1b[:, j * TT:(j + 1) * TT],
                                                         in0=x1b[:, j * TT:(j + 1) * TT],
                                                         in1=f32t[:, :])))
                def stx1(g, tt=tt):
                    for j in range(NFC):
                        ph.inc(g.dma_start(out=x1o[j * 128:(j + 1) * 128,
                                                   tt * TT:(tt + 1) * TT],
                                           in_=x1b[:, j * TT:(j + 1) * TT]), 16)
                GQ(stx1)
                def mmsum(t, tt=tt):
                    for j in range(NFC):
                        ins = t.matmul(out=pA6[0][0:1, :], lhsT=onecol[:, 0:1],
                                       rhs=x1b[:, j * TT:(j + 1) * TT],
                                       start=(j == 0), stop=(j == NFC - 1))
                    ph.inc(ins)
                T(mmsum)
                for j in range(NFC):
                    SC(lambda s, j=j: ph.inc(s.activation(out=sqf[:, :],
                                                          in_=x1b[:, j * TT:(j + 1) * TT],
                                                          func=AF.Square)))
                    def mmsq(t, j=j):
                        ph.inc(t.matmul(out=pA6[1][0:1, :], lhsT=onecol[:, 0:1], rhs=sqf[:, :],
                                        start=(j == 0), stop=(j == NFC - 1)))
                    T(mmsq)
                def stats(v):
                    v.tensor_scalar_mul(out=statm[:, 0 * TT:1 * TT], in0=pA6[0][0:1, :], scalar1=1.0 / D)
                    v.tensor_scalar_mul(out=statm[:, 1 * TT:2 * TT], in0=pA6[1][0:1, :], scalar1=1.0 / D)
                    v.tensor_mul(out=statm[:, 2 * TT:3 * TT], in0=statm[:, 0 * TT:1 * TT],
                                 in1=statm[:, 0 * TT:1 * TT])
                    ph.inc(v.tensor_sub(out=statm[:, 3 * TT:4 * TT], in0=statm[:, 1 * TT:2 * TT],
                                        in1=statm[:, 2 * TT:3 * TT]))
                V(stats)
                SC(lambda s: ph.inc(s.activation(out=statm[:, 4 * TT:5 * TT],
                                                 in_=statm[:, 3 * TT:4 * TT],
                                                 func=AF.Sqrt, bias=epsc[0:1, 0:1])))
                V(lambda v: ph.inc(v.reciprocal(out=statm[:, 5 * TT:6 * TT],
                                                in_=statm[:, 4 * TT:5 * TT])))
                def bcst(t):
                    t.matmul(out=pA6[2][:, :], lhsT=onesc[:, :], rhs=statm[:, 0 * TT:1 * TT],
                             start=True, stop=True)
                    ph.inc(t.matmul(out=pA6[3][:, :], lhsT=onesc[:, :],
                                    rhs=statm[:, 5 * TT:6 * TT], start=True, stop=True))
                T(bcst)
                for j in range(NFC):
                    def norm(v, j=j):
                        v.tensor_sub(out=sqf[:, :], in0=x1b[:, j * TT:(j + 1) * TT],
                                     in1=pA6[2][:, :])
                        ph.inc(v.tensor_mul(out=xn1b[:, :], in0=sqf[:, :], in1=pA6[3][:, :]))
                    V(norm)
                    def stxn(g, tt=tt, j=j):
                        ph.inc(g.dma_start(out=xn1o[j * 128:(j + 1) * 128,
                                                    tt * TT:(tt + 1) * TT],
                                           in_=xn1b[:, :]), 16)
                    GQ(stxn)

        @block.gpsimd
        def _(g):
            ph.wait(g)


def emit_l1attn_f(C, nc, xn1_g, pattn2):
    core = 0
    S = 4096
    NB = 4
    NQT = S // 128
    wq = nc.dram_tensor("wq1", [D, 256], F32, kind="ExternalInput").ap()
    wk = nc.dram_tensor("wk1", [D, 64], F32, kind="ExternalInput").ap()
    wv = nc.dram_tensor("wv1", [D, 64], F32, kind="ExternalInput").ap()
    CST = l1attn_consts(core)
    cin = {}
    for k, v in CST.items():
        dt = BF16 if v.dtype == BF16NP else F32
        cin[k] = nc.dram_tensor(k, list(v.shape), dt, kind="ExternalInput").ap()
    wop = nc.dram_tensor("wop", [256, D], F32, kind="ExternalInput").ap()
    qtm_d = nc.dram_tensor("l_qtm_d", [S, 384], BF16, kind="Internal").ap()

    with ExitStack() as st:
        def sbuf(n, sh, dt=F32):
            return st.enter_context(nc.sbuf_tensor("l_" + n, sh, dt))
        ph = C.ph
        block = C.block

        def T(fn):
            @block.tensor
            def _(t):
                ph.wait(t)
                fn(t)
        def V(fn):
            @block.vector
            def _(v):
                ph.wait(v)
                fn(v)
        def SC(fn):
            @block.scalar
            def _(s):
                ph.wait(s)
                fn(s)
        def SY(fn):
            @block.sync
            def _(sy):
                ph.wait(sy)
                fn(sy)
        def GQ(fn):
            @block.gpsimd
            def _(g):
                ph.wait(g)
                fn(g)

        wqs = sbuf("wqs", [128, NFC * 256], BF16)
        wkv = sbuf("wkv", [128, NFC * 128], BF16)
        cs = {}
        for k, v in CST.items():
            dt = BF16 if v.dtype == BF16NP else F32
            cs[k] = sbuf("c3_" + k, list(v.shape), dt)
        xnt = sbuf("xnt", [128, NFC * 128], BF16)
        qtmb = sbuf("qtmb", [128, 256], BF16)
        ktmb = sbuf("ktmb", [128, 64], BF16)
        rt1 = sbuf("rt1", [128, 128], BF16)
        rt2 = sbuf("rt2", [128, 128], BF16)
        vaug = sbuf("vaug", [128, NQT * 65], BF16)
        q4cm = sbuf("q4cm", [64, 4 * S], BF16)
        qx2 = sbuf("qx2", [128, S], BF16)
        kcmx = sbuf("kcmx", [128, S], BF16)
        st1 = sbuf("st1", [128, 512], F32)
        ptile = sbuf("ptile", [128, 512], BF16)
        neg30 = sbuf("neg30", [128, 1], F32)
        ctxb = sbuf("ctxb", [65, 512], BF16)
        rnb = sbuf("rnb", [64, 512], BF16)
        rsum = sbuf("rsum", [1, 512], BF16)
        ones64 = sbuf("ones64", [1, 64], BF16)
        ctxf = sbuf("ctxf", [64, 512], BF16)
        ctx2g = sbuf("ctx2g", [128, 2 * 512], BF16)
        wos2 = sbuf("wos2", [128, 2 * D], BF16)
        pat2 = sbuf("pat2", [128, 512], BF16)
        pPt = st.enter_context(nc.psum_tensor("l_pPt", [128, 512], F32))
        pP = st.enter_context(nc.psum_tensor("l_pP", [128, 256], F32))
        pKV = st.enter_context(nc.psum_tensor("l_pKV", [128, 128], F32))
        pSc = st.enter_context(nc.psum_tensor("l_pSc", [128, 512], F32))
        pCtx = st.enter_context(nc.psum_tensor("l_pCtx", [65, 512], F32))
        pBc = st.enter_context(nc.psum_tensor("l_pBc", [64, 512], F32))

        def ld0(g):
            srcw = wop.rearrange("(j p) n -> p j n", p=128)
            dstw = wos2[:, :].rearrange("p (j n) -> p j n", n=D)
            ph.inc(g.dma_start(out=dstw, in_=srcw), 16)
            src = wq.rearrange("(j p) n -> p j n", p=128)
            ph.inc(g.dma_start(out=wqs[:, :].rearrange("p (j n) -> p j n", n=256), in_=src), 16)
            ph.inc(g.dma_start(out=wkv[:, :].rearrange("p (j n) -> p j n", n=128)[:, :, 0:64],
                               in_=wk.rearrange("(j p) n -> p j n", p=128)), 16)
            ph.inc(g.dma_start(out=wkv[:, :].rearrange("p (j n) -> p j n", n=128)[:, :, 64:128],
                               in_=wv.rearrange("(j p) n -> p j n", p=128)), 16)
        GQ(ld0)
        def ld1(sy):
            for k in CST:
                ph.inc(sy.dma_start(out=cs[k][:, :], in_=cin[k]), 16)
        SY(ld1)
        def st0(v):
            v.memset(neg30[:, :], -30.0)
            ph.inc(v.memset(ones64[:, :], 1.0))
        V(st0)

        for b in range(NB):
            for qt in range(NQT):
                tok0 = b * S + qt * 128
                def ldxn(sy, tok0=tok0):
                    rk = tok0 // TPC
                    cb = tok0 % TPC
                    for j in range(NFC):
                        ph.inc(sy.dma_start(out=xnt[:, j * 128:(j + 1) * 128],
                                            in_=xn1_g[rk * D + j * 128:rk * D + (j + 1) * 128,
                                                      cb:cb + 128]), 16)
                SY(ldxn)
                def mmq(t):
                    for j in range(NFC):
                        t.matmul(out=pP[:, :], lhsT=xnt[:, j * 128:(j + 1) * 128],
                                 rhs=wqs[:, j * 256:(j + 1) * 256],
                                 start=(j == 0), stop=(j == NFC - 1))
                    for j in range(NFC):
                        ins = t.matmul(out=pKV[:, :], lhsT=xnt[:, j * 128:(j + 1) * 128],
                                       rhs=wkv[:, j * 128:(j + 1) * 128],
                                       start=(j == 0), stop=(j == NFC - 1))
                    ph.inc(ins)
                T(mmq)
                def ropeq(v, qt=qt):
                    _lp = st.enter_context(nc.allow_low_precision(
                        reason="bf16 rope elementwise, blessed tolerance")) if False else None
                    cosb = cs["cos_t"][:, qt * 32:(qt + 1) * 32]
                    sinb = cs["sin_t"][:, qt * 32:(qt + 1) * 32]
                    q3 = pP[:, :].rearrange("p (h d) -> p h d", h=4)
                    o3 = qtmb[:, :].rearrange("p (h d) -> p h d", h=4)
                    c3 = cosb.unsqueeze(1).broadcast_to([128, 4, 32])
                    s3 = sinb.unsqueeze(1).broadcast_to([128, 4, 32])
                    v.tensor_mul(out=o3[:, :, 0::2], in0=q3[:, :, 1::2], in1=s3)
                    v.engine_nop()
                    v.tensor_mul(out=rt1[:, :].rearrange("p (h d) -> p h d", h=4)[:, :, 0:32],
                                 in0=q3[:, :, 0::2], in1=c3)
                    v.engine_nop()
                    v.tensor_sub(out=o3[:, :, 0::2],
                                 in0=rt1[:, :].rearrange("p (h d) -> p h d", h=4)[:, :, 0:32],
                                 in1=o3[:, :, 0::2])
                    v.tensor_mul(out=o3[:, :, 1::2], in0=q3[:, :, 0::2], in1=s3)
                    v.engine_nop()
                    v.tensor_mul(out=rt1[:, :].rearrange("p (h d) -> p h d", h=4)[:, :, 0:32],
                                 in0=q3[:, :, 1::2], in1=c3)
                    v.engine_nop()
                    ph.inc(v.tensor_add(out=o3[:, :, 1::2],
                                        in0=rt1[:, :].rearrange("p (h d) -> p h d", h=4)[:, :, 0:32],
                                        in1=o3[:, :, 1::2]))
                V(ropeq)
                def ropek(v, qt=qt):
                    cosb = cs["cos_t"][:, qt * 32:(qt + 1) * 32]
                    sinb = cs["sin_t"][:, qt * 32:(qt + 1) * 32]
                    k2 = pKV[:, 0:64]
                    o2 = ktmb[:, :]
                    v.tensor_mul(out=o2[:, 0::2], in0=k2[:, 1::2], in1=sinb)
                    v.engine_nop()
                    v.tensor_mul(out=rt2[:, 0:32], in0=k2[:, 0::2], in1=cosb)
                    v.engine_nop()
                    v.tensor_sub(out=o2[:, 0::2], in0=rt2[:, 0:32], in1=o2[:, 0::2])
                    v.tensor_mul(out=o2[:, 1::2], in0=k2[:, 0::2], in1=sinb)
                    v.engine_nop()
                    v.tensor_mul(out=rt2[:, 0:32], in0=k2[:, 1::2], in1=cosb)
                    v.engine_nop()
                    v.tensor_add(out=o2[:, 1::2], in0=rt2[:, 0:32], in1=o2[:, 1::2])
                    v.tensor_copy(out=vaug[:, qt * 65:qt * 65 + 64], in_=pKV[:, 64:128])
                    ph.inc(v.memset(vaug[:, qt * 65 + 64:qt * 65 + 65], 1.0))
                V(ropek)
                def stqk(g, qt=qt):
                    ph.inc(g.dma_start(out=qtm_d[qt * 128:(qt + 1) * 128, 0:256],
                                       in_=qtmb[:, :]), 16)
                    ph.inc(g.dma_start(out=qtm_d[qt * 128:(qt + 1) * 128, 256:320],
                                       in_=ktmb[:, :]), 16)
                GQ(stqk)
            for hp in range(2):
                def trq(sy, hp=hp):
                    ph.inc(sy.dma_start_transpose(out=qx2[:, :],
                                                  in_=qtm_d[:, hp * 128:(hp + 1) * 128]), 16)
                SY(trq)
                def spl(v, hp=hp):
                    v.tensor_copy(out=q4cm[:, (2 * hp) * S:(2 * hp + 1) * S], in_=qx2[0:64, :])
                    ph.inc(v.tensor_copy(out=q4cm[:, (2 * hp + 1) * S:(2 * hp + 2) * S],
                                         in_=qx2[64:128, :]))
                V(spl)
            def trk(sy):
                ph.inc(sy.dma_start_transpose(out=kcmx[:, :],
                                              in_=qtm_d[:, 256:384]), 16)
            SY(trk)
            for qt in range(NQT):
                for kt in range(qt + 1):
                    def mmsc(t, qt=qt, kt=kt):
                        q3 = q4cm[:, :].rearrange("p (h t) -> p h t", h=4)
                        ps3 = pSc[:, :].rearrange("p (h t) -> p h t", h=4)
                        t.matmul(out=ps3[:, :, :], lhsT=kcmx[0:64, kt * 128:(kt + 1) * 128],
                                 rhs=q3[:, :, qt * 128:(qt + 1) * 128], start=True, stop=False)
                        qa3 = cs["qaug"][:, :].rearrange("p (h t) -> p h t", h=4)
                        ph.inc(t.matmul(out=ps3[:, :, :],
                                        lhsT=cs["kpos"][:, kt * 128:(kt + 1) * 128],
                                        rhs=qa3[:, :, qt * 128:(qt + 1) * 128],
                                        start=False, stop=True, skip_group_check=True))
                    T(mmsc)
                    def capexp(s, qt=qt, kt=kt):
                        s.activation(out=st1[:, :], in_=pSc[:, :], func=AF.Tanh,
                                     scale=1.0 / 240.0)
                        ph.inc(s.activation(out=ptile[:, :], in_=st1[:, :], func=AF.Exp,
                                            scale=30.0, bias=neg30[:, 0:1]))
                    SC(capexp)
                    if kt == qt:
                        def dmask(v):
                            p3 = ptile[:, :].rearrange("p (h t) -> p h t", h=4)
                            m3 = cs["causal"][:, :].unsqueeze(1).broadcast_to([128, 4, 128])
                            ph.inc(v.tensor_tensor(out=p3, in0=p3, in1=m3, op=ALU.mult))
                        V(dmask)
                    def mmpv(t, qt=qt, kt=kt):
                        ph.inc(t.matmul(out=pCtx[:, :], lhsT=vaug[:, kt * 65:(kt + 1) * 65],
                                        rhs=ptile[:, :], start=(kt == 0), stop=(kt == qt),
                                        skip_group_check=True))
                    T(mmpv)
                V(lambda v: ph.inc(v.tensor_copy(out=ctxb[:, :], in_=pCtx[:, :])))
                V(lambda v: ph.inc(v.tensor_copy(out=rsum[:, :], in_=ctxb[64:65, :])))
                V(lambda v: ph.inc(v.reciprocal(out=rsum[:, :], in_=rsum[:, :])))
                def bcn(t):
                    ph.inc(t.matmul(out=pBc[:, :], lhsT=ones64[:, :],
                                    rhs=rsum[:, :], start=True, stop=True))
                T(bcn)
                V(lambda v: ph.inc(v.tensor_copy(out=rnb[:, :], in_=pBc[:, :])))
                def cnorm(v):
                    ph.inc(v.tensor_mul(out=ctxf[:, :], in0=ctxb[0:64, :], in1=rnb[:, :]))
                V(cnorm)
                def asm2(v, qt=qt):
                    q4 = qt % 4
                    for hp in range(2):
                        v.tensor_copy(out=ctx2g[0:64, hp * 512 + q4 * 128:
                                                hp * 512 + (q4 + 1) * 128],
                                      in_=ctxf[:, (2 * hp) * 128:(2 * hp + 1) * 128])
                        v.tensor_copy(out=ctx2g[64:128, hp * 512 + q4 * 128:
                                                hp * 512 + (q4 + 1) * 128],
                                      in_=ctxf[:, (2 * hp + 1) * 128:(2 * hp + 2) * 128])
                    ph.inc(v.engine_nop())
                V(asm2)
                if qt % 4 == 3:
                    tok0 = b * S + (qt - 3) * 128
                    dstb = tok0 // TPC
                    cb0 = tok0 % TPC
                    for jo in range(NFC):
                        def pmm2(t, jo=jo):
                            t.matmul(out=pPt[:, :],
                                     lhsT=wos2[:, 0 * D + jo * 128:0 * D + jo * 128 + 128],
                                     rhs=ctx2g[:, 0:512], start=True, stop=False)
                            ph.inc(t.matmul(
                                out=pPt[:, :],
                                lhsT=wos2[:, 1 * D + jo * 128:1 * D + jo * 128 + 128],
                                rhs=ctx2g[:, 512:1024], start=False, stop=True))
                        T(pmm2)
                        V(lambda v: ph.inc(v.tensor_copy(out=pat2[:, :], in_=pPt[:, :])))
                        def stp2(g, jo=jo, dstb=dstb, cb0=cb0):
                            ph.inc(g.dma_start(
                                out=pattn2[dstb * D + jo * 128:dstb * D + (jo + 1) * 128,
                                           cb0:cb0 + 512],
                                in_=pat2[:, :]), 16)
                        GQ(stp2)

        @block.gpsimd
        def _(g):
            ph.wait(g)


def emit_fin_f(C, nc, attn2_sl, xn1s, x1s, x2o):
    TT = 512
    NT = TPC // TT
    NIC = 8192 // 128
    bo = nc.dram_tensor("bo", [128, NFC], F32, kind="ExternalInput").ap()
    wg8 = nc.dram_tensor("wg8", [D, 8192], I8, kind="ExternalInput").ap()
    wu8 = nc.dram_tensor("wu8", [D, 8192], I8, kind="ExternalInput").ap()
    wd8 = nc.dram_tensor("wd8", [8192, D], I8, kind="ExternalInput").ap()
    sg = nc.dram_tensor("sg", [128, NIC], F32, kind="ExternalInput").ap()
    su = nc.dram_tensor("su", [128, NIC], F32, kind="ExternalInput").ap()
    sd = nc.dram_tensor("sd", [128, NFC], F32, kind="ExternalInput").ap()

    with ExitStack() as st:
        def sbuf(n, sh, dt=F32):
            return st.enter_context(nc.sbuf_tensor("n_" + n, sh, dt))
        ph = C.ph
        block = C.block
        def T(fn):
            @block.tensor
            def _(t):
                ph.wait(t)
                fn(t)
        def V(fn):
            @block.vector
            def _(v):
                ph.wait(v)
                fn(v)
        def SC(fn):
            @block.scalar
            def _(s):
                ph.wait(s)
                fn(s)
        def SY(fn):
            @block.sync
            def _(sy):
                ph.wait(sy)
                fn(sy)
        def GQ(fn):
            @block.gpsimd
            def _(g):
                ph.wait(g)
                fn(g)

        sgs = sbuf("sgs", [128, NIC], F32)
        sus = sbuf("sus", [128, NIC], F32)
        sds = sbuf("sds", [128, NFC], F32)
        bos = sbuf("bos", [128, NFC], F32)
        f32t = sbuf("f32t", [128, TT], F32)
        f32u = sbuf("f32u", [128, TT], F32)
        gt2f = [sbuf(f"gt{i}", [128, TT], BF16) for i in range(2)]
        at2p = [sbuf(f"at2b{i}", [128, TT], BF16) for i in range(2)]
        x2p = [sbuf(f"x2b{i}", [128, TT], F32) for i in range(2)]
        ut = sbuf("ut", [128, TT], BF16)
        pF6 = [st.enter_context(nc.psum_tensor(f"n_pF{i}", [128, TT], F32))
               for i in range(6)]
        f_sw = st.enter_context(nc.semaphore("n_sw"))
        f_st = st.enter_context(nc.semaphore("n_st"))
        f_ssc = st.enter_context(nc.semaphore("n_ssc"))
        f_sv = st.enter_context(nc.semaphore("n_sv"))
        f_swd = st.enter_context(nc.semaphore("n_swd"))
        f_sdt = st.enter_context(nc.semaphore("n_sdt"))
        f_sdv = st.enter_context(nc.semaphore("n_sdv"))
        f_slx = st.enter_context(nc.semaphore("n_slx"))
        f_sst = st.enter_context(nc.semaphore("n_sst"))

        def lds(sy):
            for src, dst in ((sg, sgs), (su, sus), (sd, sds), (bo, bos)):
                ph.inc(sy.dma_start(out=dst[:, :], in_=src), 16)
        SY(lds)

        with nc.sbuf_tensor("n_xn1sb", [128, NFC * TPC], BF16) as xn1sb, \
             nc.sbuf_tensor("n_hst", [128, NIC * TT], BF16) as hst, \
             nc.sbuf_tensor("n_wgc0", [128, NFC * 128], BF16) as wgc0, \
             nc.sbuf_tensor("n_wuc0", [128, NFC * 128], BF16) as wuc0, \
             nc.sbuf_tensor("n_wgc1", [128, NFC * 128], BF16) as wgc1, \
             nc.sbuf_tensor("n_wuc1", [128, NFC * 128], BF16) as wuc1, \
             nc.sbuf_tensor("n_wdc0", [128, NIC * 128], BF16) as wdc0, \
             nc.sbuf_tensor("n_wdc1", [128, NIC * 128], BF16) as wdc1:
            def ldxn(sy):
                for j in range(NFC):
                    ph.inc(sy.dma_start(out=xn1sb[:, j * TPC:(j + 1) * TPC],
                                        in_=xn1s[j * 128:(j + 1) * 128, :]), 16)
            SY(ldxn)
            UPT, DNT = NIC, NFC
            wgup = [(wgc0, wuc0), (wgc1, wuc1)]
            wdnp = [wdc0, wdc1]
            for p in range(NT):
                for ic in range(NIC):
                    wl = p * UPT + ic

                    @block.gpsimd
                    def _(g, ic=ic, wl=wl):
                        if wl == 0:
                            ph.wait(g)
                        if wl >= 2:
                            g.wait_ge(f_st, wl - 1)
                        wgc, wuc = wgup[wl % 2]
                        g.dma_start(out=wgc[:, :].rearrange("p (j n) -> p j n", n=128),
                                    in_=wg8[:, ic * 128:(ic + 1) * 128]
                                    .rearrange("(j p) n -> p j n", p=128)).then_inc(f_sw, 16)
                        g.dma_start(out=wuc[:, :].rearrange("p (j n) -> p j n", n=128),
                                    in_=wu8[:, ic * 128:(ic + 1) * 128]
                                    .rearrange("(j p) n -> p j n", p=128)).then_inc(f_sw, 16)
                    k = wl
                    q = k % 2

                    @block.tensor
                    def _(t, tt=p, wl=wl, k=k, q=q, p=p):
                        if k == 0:
                            ph.wait(t)
                        t.wait_ge(f_sw, 32 * (wl + 1))
                        if k >= 2:
                            t.wait_ge(f_sv, k - 1)
                        if p >= 1:
                            t.wait_ge(f_sdv, DNT * p)
                        wgc, wuc = wgup[wl % 2]
                        for j in range(NFC):
                            t.matmul(out=pF6[2 * q][:, :], lhsT=wgc[:, j * 128:(j + 1) * 128],
                                     rhs=xn1sb[:, j * TPC + tt * TT:j * TPC + (tt + 1) * TT],
                                     start=(j == 0), stop=(j == NFC - 1))
                        for j in range(NFC):
                            ins = t.matmul(out=pF6[2 * q + 1][:, :],
                                           lhsT=wuc[:, j * 128:(j + 1) * 128],
                                           rhs=xn1sb[:, j * TPC + tt * TT:
                                                    j * TPC + (tt + 1) * TT],
                                           start=(j == 0), stop=(j == NFC - 1))
                        ins.then_inc(f_st, 1)

                    @block.scalar
                    def _(s, ic=ic, k=k, q=q):
                        if k == 0:
                            ph.wait(s)
                        s.wait_ge(f_st, k + 1)
                        if k >= 2:
                            s.wait_ge(f_sv, k - 1)
                        s.activation(out=gt2f[q][:, :], in_=pF6[2 * q][:, :],
                                     func=AF.Silu,
                                     scale=sgs[:, ic:ic + 1]).then_inc(f_ssc, 1)

                    @block.vector
                    def _(v, ic=ic, k=k, q=q, p=p):
                        if k == 0:
                            ph.wait(v)
                        v.wait_ge(f_ssc, k + 1)
                        if p >= 1:
                            v.wait_ge(f_sdt, DNT * p)
                        v.tensor_scalar_mul(out=ut[:, :], in0=pF6[2 * q + 1][:, :],
                                            scalar1=sus[:, ic:ic + 1])
                        v.engine_nop()
                        v.tensor_mul(out=hst[:, ic * TT:(ic + 1) * TT], in0=gt2f[q][:, :],
                                     in1=ut[:, :]).then_inc(f_sv, 1)
                for jo in range(NFC):
                    m = p * DNT + jo

                    @block.gpsimd
                    def _(g, jo=jo, m=m):
                        if m >= 2:
                            g.wait_ge(f_sdt, m - 1)
                        wdc = wdnp[m % 2]
                        g.dma_start(out=wdc[:, :].rearrange("p (j n) -> p j n", n=128),
                                    in_=wd8[:, jo * 128:(jo + 1) * 128]
                                    .rearrange("(j p) n -> p j n", p=128)).then_inc(f_swd, 16)

                    @block.sync
                    def _(sy, tt=p, jo=jo, m=m):
                        if m == 0:
                            ph.wait(sy)
                        if m >= 2:
                            sy.wait_ge(f_sst, 16 * (m - 1))
                        sy.dma_start(out=x2p[m % 2][:, :],
                                     in_=x1s[jo * 128:(jo + 1) * 128,
                                             tt * TT:(tt + 1) * TT]).then_inc(f_slx, 16)
                        sy.dma_start(out=at2p[m % 2][:, :],
                                     in_=attn2_sl[jo * 128:(jo + 1) * 128,
                                                  tt * TT:(tt + 1) * TT]).then_inc(f_slx, 16)

                    @block.tensor
                    def _(t, m=m, p=p):
                        t.wait_ge(f_swd, 16 * (m + 1))
                        if m >= 4:
                            t.wait_ge(f_sdv, m - 3)
                        t.wait_ge(f_sv, UPT * (p + 1))
                        wdc = wdnp[m % 2]
                        for ic in range(NIC):
                            ins = t.matmul(out=pF6[m % 4][:, :],
                                           lhsT=wdc[:, ic * 128:(ic + 1) * 128],
                                           rhs=hst[:, ic * TT:(ic + 1) * TT],
                                           start=(ic == 0), stop=(ic == NIC - 1))
                        ins.then_inc(f_sdt, 1)

                    @block.vector
                    def _(v, jo=jo, m=m):
                        v.wait_ge(f_sdt, m + 1)
                        v.wait_ge(f_slx, 32 * (m + 1))
                        x2b = x2p[m % 2]
                        v.tensor_scalar_mul(out=f32t[:, :], in0=pF6[m % 4][:, :],
                                            scalar1=sds[:, jo:jo + 1])
                        v.tensor_scalar_add(out=f32u[:, :], in0=at2p[m % 2][:, :],
                                            scalar1=bos[:, jo:jo + 1])
                        v.engine_nop()
                        v.tensor_add(out=x2b[:, :], in0=x2b[:, :], in1=f32u[:, :])
                        v.engine_nop()
                        v.tensor_add(out=x2b[:, :], in0=x2b[:, :],
                                     in1=f32t[:, :]).then_inc(f_sdv, 1)

                    @block.gpsimd
                    def _(g, tt=p, jo=jo, m=m):
                        g.wait_ge(f_sdv, m + 1)
                        g.dma_start(out=x2o[jo * 128:(jo + 1) * 128,
                                            tt * TT:(tt + 1) * TT],
                                    in_=x2p[m % 2][:, :]).then_inc(f_sst, 16)

            @block.gpsimd
            def _(g):
                g.wait_ge(f_sst, 16 * NT * NFC)
                ph.inc(g.engine_nop())

        @block.gpsimd
        def _(g):
            ph.wait(g)



def fused_inmap(core, ins):
    c = core
    x = ins['L0_attn_hidden_states'].reshape(TOK, D)
    x_fm = np.ascontiguousarray(np.asarray(x, np.float32).T)
    m = {"x_sl": np.ascontiguousarray(x_fm[:, c * TPC:(c + 1) * TPC])}
    m0 = l0attn_inmap(c, None, ins)
    m0.pop("xn", None)
    m.update(m0)
    m["owp"] = np.ascontiguousarray(ins["L0_attn_o_w"][256 * c:256 * (c + 1), :])
    m["onw"] = np.ascontiguousarray(ins["L0_attn_o_norm_w"][:, None])
    m1 = mid_inmap(c, None, None, None, ins)
    for k in ("ofm", "xns", "xs", "ow", "onw"):
        m1.pop(k, None)
    m1["identf_m"] = m1.pop("identf")
    m.update(m1)
    m2 = l1attn_inmap(c, None, ins)
    m2.pop("xn1", None)
    m2["wq1"] = m2.pop("wq"); m2["wk1"] = m2.pop("wk"); m2["wv1"] = m2.pop("wv")
    m.update(m2)
    m["wop"] = np.ascontiguousarray(ins["L1_attn_Wo"][256 * c:256 * (c + 1), :])
    m3 = fin_inmap(c, None, None, None, ins)
    for k in ("ctx", "xn1s", "x1s", "wo"):
        m3.pop(k, None)
    m.update(m3)
    return m


def kernel_fused(upto=99, **inputs):
    nc = build_fused(upto=upto)
    maps = [fused_inmap(c, inputs) for c in range(NCORES)]
    res = run_bass_kernel_spmd(nc, maps, core_ids=list(range(NCORES)))
    if upto < 8:
        return [np.asarray(r["dbgo"]) for r in res.results]
    x2_fm = np.concatenate([np.asarray(r["x2_sl"]) for r in res.results], axis=1)
    return np.ascontiguousarray(x2_fm.T).reshape(4, 4096, D)



def kernel(**inputs):
    nc = build_fused(upto=8)
    maps = [fused_inmap(c, inputs) for c in range(NCORES)]
    res = run_bass_kernel_spmd(nc, maps, core_ids=list(range(NCORES)))
    x2_fm = np.concatenate([np.asarray(r["x2_sl"]) for r in res.results], axis=1)
    return np.ascontiguousarray(x2_fm.T).reshape(4, 4096, D)


# version 10
# speedup vs baseline: 1.3647x; 1.3647x over previous
import numpy as np
import ml_dtypes
from contextlib import ExitStack
from types import SimpleNamespace
import concourse.bass as bass
import concourse.mybir as mybir
from concourse.bass_utils import run_bass_kernel_spmd

F32 = mybir.dt.float32
BF16 = mybir.dt.bfloat16
I8 = mybir.dt.int8
AF = mybir.ActivationFunctionType
ALU = mybir.AluOpType
BF16NP = ml_dtypes.bfloat16

NCORES = 8
D = 2048
TOK = 16384
TPC = TOK // NCORES
NFC = D // 128
CLAMP = 21.0


class Ctr:
    def __init__(self, handle):
        self.h = handle
        self.n = 0

    def inc(self, instr, k=1):
        self.n += k
        instr.then_inc(self.h, k)
        return instr

    def wait(self, eng, n=None):
        eng.wait_ge(self.h, self.n if n is None else n)



def l0attn_consts():
    GT = 512
    col = np.arange(GT)
    c = {}
    for m in (8, 16, 32, 64, 128):
        c[f"rm{m}"] = np.broadcast_to((col % m != 0).astype(np.float32), (128, GT)).copy()
    c["evn8"] = np.broadcast_to(((col // 8) % 2 == 0).astype(BF16NP), (128, GT)).copy()
    c["odd8"] = np.broadcast_to(((col // 8) % 2 == 1).astype(BF16NP), (128, GT)).copy()
    s = np.arange(128)[:, None]; t = np.arange(128)[None, :]
    c["maskAb"] = np.where(s <= t, -1, 0).astype(np.int32)
    c["maskMb"] = np.where(s < t, -1, 0).astype(np.int32)
    c["identb"] = np.eye(128, dtype=BF16NP)
    c["identf"] = np.eye(128, dtype=np.float32)
    return c


def l0attn_inmap(core, xn_fm_bf16, ins):
    c = core
    hs = slice(256 * c, 256 * c + 256)
    CST = l0attn_consts()
    cwq = ins["L0_attn_q_conv_w"][hs].reshape(2, 128, 4)
    cwk = ins["L0_attn_k_conv_w"][hs].reshape(2, 128, 4)
    cwv = ins["L0_attn_v_conv_w"][hs].reshape(2, 128, 4)
    cw = np.concatenate([cwq[0], cwq[1], cwk[0], cwk[1], cwv[0], cwv[1]],
                        axis=1)
    m = {
        "wq": np.ascontiguousarray(ins["L0_attn_q_w"][:, hs]),
        "wk": np.ascontiguousarray(ins["L0_attn_k_w"][:, hs]),
        "wv": np.ascontiguousarray(ins["L0_attn_v_w"][:, hs]),
        "cw": np.ascontiguousarray(cw),
        "fw1": ins["L0_attn_f_w1"],
        "fw2": np.ascontiguousarray(ins["L0_attn_f_w2"][:, hs]),
        "gw1": ins["L0_attn_g_w1"],
        "gw2": np.ascontiguousarray(ins["L0_attn_g_w2"][:, hs]),
        "gb2": np.ascontiguousarray(ins["L0_attn_g_b2"][hs][None, :]),
        "bw": np.ascontiguousarray(ins["L0_attn_b_w"][:, 2 * c:2 * c + 2]),
        "alog": np.ascontiguousarray(ins["L0_attn_A_log"][2 * c:2 * c + 2][None, :]),
        "dtb": np.ascontiguousarray(ins["L0_attn_dt_bias"].reshape(16, 128)
                                    [2 * c:2 * c + 2].T),
    }
    for k, v in CST.items():
        m[k] = v
    return m


def mid_inmap(core, ofeat_full_bf16, xn_fm_bf16, x_fm_f32, ins):
    c = core
    sl = slice(TPC * c, TPC * (c + 1))
    CST = l0attn_consts()
    m_ = {}
    if ofeat_full_bf16 is not None:
        m_["ofm"] = np.ascontiguousarray(ofeat_full_bf16.T[:, sl])
        m_["xns"] = np.ascontiguousarray(xn_fm_bf16[:, sl])
        m_["xs"] = np.ascontiguousarray(x_fm_f32[:, sl])
    return {
        **m_,
        "onw": np.ascontiguousarray(ins["L0_attn_o_norm_w"][:, None]),
        "ow": ins["L0_attn_o_w"],
        "rout": ins["L0_ffn_router"],
        "wug": ins["L0_ffn_w_up_gate"],
        "wdn": ins["L0_ffn_w_down"],
        "swu": ins["L0_ffn_shared_w_up_gate"],
        "swd": ins["L0_ffn_shared_w_down"],
        "identf": CST["identf"],
        "esel": np.ascontiguousarray(
            np.concatenate([np.tile((np.arange(8) == e).astype(BF16NP)[:, None],
                                    (1, 128)) for e in range(8)], axis=1)),
    }


def l1attn_consts(core):
    S = 4096
    t = np.arange(S, dtype=np.float32)
    inv = 1.0 / 10000.0 ** (np.arange(0, 64, 2) / 64.0)
    fr = t[:, None] * inv[None, :]
    cos = np.cos(fr).astype(np.float32)
    sin = np.sin(fr).astype(np.float32)
    cos_t = np.ascontiguousarray(cos.reshape(32, 128, 32).transpose(1, 0, 2).reshape(128, 32 * 32))
    sin_t = np.ascontiguousarray(sin.reshape(32, 128, 32).transpose(1, 0, 2).reshape(128, 32 * 32))
    slopes = 2.0 ** (-(np.arange(1, 33) * 8.0 / 32))
    qaug = np.zeros((2, 4 * S), np.float32)
    for h in range(4):
        sl = slopes[4 * core + h]
        qaug[0, h * S:(h + 1) * S] = -sl * t
        qaug[1, h * S:(h + 1) * S] = sl
    kpos = np.stack([np.ones(S, np.float32), t]).astype(np.float32)
    sq = np.arange(128)[:, None]; tq = np.arange(128)[None, :]
    causal = (sq <= tq).astype(BF16NP)
    return {"cos_t": cos_t, "sin_t": sin_t, "qaug": qaug, "kpos": kpos, "causal": causal}


def l1attn_inmap(core, xn1_fm_bf16, ins):
    c = core
    CST = l1attn_consts(c)
    m = {
        "wq": np.ascontiguousarray(ins["L1_attn_Wq"][:, 256 * c:256 * (c + 1)]),
        "wk": np.ascontiguousarray(ins["L1_attn_Wk"][:, 64 * c:64 * (c + 1)]),
        "wv": np.ascontiguousarray(ins["L1_attn_Wv"][:, 64 * c:64 * (c + 1)]),
    }
    m.update(CST)
    if xn1_fm_bf16 is not None:
        m["xn1"] = xn1_fm_bf16
    return m


def fin_inmap(core, ctx_fm_bf16, xn1_fm_bf16, x1_fm_f32, ins):
    c = core
    sl = slice(TPC * c, TPC * (c + 1))
    m_ = {}
    if ctx_fm_bf16 is not None:
        m_["ctx"] = np.ascontiguousarray(ctx_fm_bf16[:, sl])
        m_["xn1s"] = np.ascontiguousarray(xn1_fm_bf16[:, sl])
        m_["x1s"] = np.ascontiguousarray(x1_fm_f32[:, sl])
    return {
        **m_,
        "wo": ins["L1_attn_Wo"],
        "bo": np.ascontiguousarray(ins["L1_attn_bo"].reshape(NFC, 128).T),
        "wg8": ins["L1_ffn_w_gate"],
        "wu8": ins["L1_ffn_w_up"],
        "wd8": ins["L1_ffn_w_down"],
        "sg": np.ascontiguousarray(ins["L1_ffn_s_gate"].reshape(64, 128).T),
        "su": np.ascontiguousarray(ins["L1_ffn_s_up"].reshape(64, 128).T),
        "sd": np.ascontiguousarray(ins["L1_ffn_s_down"].reshape(NFC, 128).T),
    }


I32F = mybir.dt.int32


def mkctx(nc, st, block, ph, pfx):
    C = SimpleNamespace(nc=nc, st=st, block=block, ph=ph, pfx=pfx)

    def sbuf(n, sh, dt=F32):
        return st.enter_context(nc.sbuf_tensor(pfx + n, sh, dt))

    def psum(n, sh, dt=F32):
        return st.enter_context(nc.psum_tensor(pfx + n, sh, dt))

    def T(fn):
        @block.tensor
        def _(t):
            ph.wait(t)
            fn(t)

    def V(fn):
        @block.vector
        def _(v):
            ph.wait(v)
            fn(v)

    def SC(fn):
        @block.scalar
        def _(s):
            ph.wait(s)
            fn(s)

    def SY(fn):
        @block.sync
        def _(sy):
            ph.wait(sy)
            fn(sy)

    def GQ(fn):
        @block.gpsimd
        def _(g):
            ph.wait(g)
            fn(g)

    C.sbuf, C.psum = sbuf, psum
    C.T, C.V, C.SC, C.SY, C.GQ = T, V, SC, SY, GQ
    return C


def emit_coll(C, kind, in_ap, out_ap):
    ph = C.ph
    op = ALU.add if kind == "ReduceScatter" else ALU.bypass

    @C.block.gpsimd
    def _(g):
        ph.wait(g)
        ins = g.collective_compute(
            kind, op,
            replica_groups=[list(range(NCORES))],
            ins=[in_ap.opt()], outs=[out_ap.opt()])
        ph.inc(ins, 1)


def emit_ln(C, x_in, xn_out, eps=1e-5):
    TT = 512
    NT = TPC // TT
    nc, ph = C.nc, C.ph
    with ExitStack() as st2:
        def sb(n, sh, dt=F32):
            return st2.enter_context(nc.sbuf_tensor(C.pfx + n, sh, dt))
        xb = sb("xb", [128, NFC * TT])
        sqf = sb("sqf", [128, TT])
        onecol = sb("onecol", [128, 1])
        onesc = sb("onesc", [1, 128])
        epsc = sb("epsc", [1, 1])
        statm = sb("statm", [1, 6 * TT])
        xnb = sb("xnb", [128, TT], BF16)
        pS = st2.enter_context(nc.psum_tensor(C.pfx + "pS", [1, TT], F32))
        pQ = st2.enter_context(nc.psum_tensor(C.pfx + "pQ", [1, TT], F32))
        pM = st2.enter_context(nc.psum_tensor(C.pfx + "pM", [128, TT], F32))
        pR = st2.enter_context(nc.psum_tensor(C.pfx + "pR", [128, TT], F32))

        def setup(v):
            v.memset(onecol[:, :], 1.0)
            v.memset(onesc[:, :], 1.0)
            ph.inc(v.memset(epsc[:, :], eps))
        C.V(setup)
        for tt in range(NT):
            def ldx(sy, tt=tt):
                for j in range(NFC):
                    ph.inc(sy.dma_start(out=xb[:, j * TT:(j + 1) * TT],
                                        in_=x_in[j * 128:(j + 1) * 128,
                                                 tt * TT:(tt + 1) * TT]), 16)
            C.SY(ldx)
            def mmsum(t):
                for j in range(NFC):
                    ins = t.matmul(out=pS[:, :], lhsT=onecol[:, 0:1],
                                   rhs=xb[:, j * TT:(j + 1) * TT],
                                   start=(j == 0), stop=(j == NFC - 1))
                ph.inc(ins)
            C.T(mmsum)
            for j in range(NFC):
                C.SC(lambda s, j=j: ph.inc(s.activation(out=sqf[:, :],
                                                        in_=xb[:, j * TT:(j + 1) * TT],
                                                        func=AF.Square)))
                C.T(lambda t, j=j: ph.inc(t.matmul(out=pQ[:, :], lhsT=onecol[:, 0:1],
                                                   rhs=sqf[:, :], start=(j == 0),
                                                   stop=(j == NFC - 1))))
            def stats(v):
                v.tensor_scalar_mul(out=statm[:, 0 * TT:1 * TT], in0=pS[:, :], scalar1=1.0 / D)
                v.tensor_scalar_mul(out=statm[:, 1 * TT:2 * TT], in0=pQ[:, :], scalar1=1.0 / D)
                v.tensor_mul(out=statm[:, 2 * TT:3 * TT], in0=statm[:, 0 * TT:1 * TT],
                             in1=statm[:, 0 * TT:1 * TT])
                ph.inc(v.tensor_sub(out=statm[:, 3 * TT:4 * TT], in0=statm[:, 1 * TT:2 * TT],
                                    in1=statm[:, 2 * TT:3 * TT]))
            C.V(stats)
            C.SC(lambda s: ph.inc(s.activation(out=statm[:, 4 * TT:5 * TT],
                                               in_=statm[:, 3 * TT:4 * TT],
                                               func=AF.Sqrt, bias=epsc[0:1, 0:1])))
            C.V(lambda v: ph.inc(v.reciprocal(out=statm[:, 5 * TT:6 * TT],
                                              in_=statm[:, 4 * TT:5 * TT])))
            def bcst(t):
                t.matmul(out=pM[:, :], lhsT=onesc[:, :], rhs=statm[:, 0 * TT:1 * TT],
                         start=True, stop=True)
                ph.inc(t.matmul(out=pR[:, :], lhsT=onesc[:, :],
                                rhs=statm[:, 5 * TT:6 * TT], start=True, stop=True))
            C.T(bcst)
            for j in range(NFC):
                def norm(v, j=j):
                    v.tensor_sub(out=sqf[:, :], in0=xb[:, j * TT:(j + 1) * TT], in1=pM[:, :])
                    ph.inc(v.tensor_mul(out=xnb[:, :], in0=sqf[:, :], in1=pR[:, :]))
                C.V(norm)
                def stxn(g, tt=tt, j=j):
                    ph.inc(g.dma_start(out=xn_out[j * 128:(j + 1) * 128,
                                                  tt * TT:(tt + 1) * TT],
                                       in_=xnb[:, :]), 16)
                C.GQ(stxn)


def build_fused(core_check=None, upto=99):
    nc = bass.Bass("TRN2", target_bir_lowering=False, debug=False)
    EI = lambda n, sh, dt=F32: nc.dram_tensor(n, sh, dt, kind="ExternalInput").ap()
    IN = lambda n, sh, dt=F32: nc.dram_tensor(n, sh, dt, kind="Internal").ap()
    SH = lambda n, sh, dt=F32: nc.dram_tensor(n, sh, dt, kind="Internal", addr_space="Shared").ap()
    x_sl = EI("x_sl", [D, TPC])
    xn_sl = IN("xn_sl", [D, TPC], BF16)
    xn_g = IN("xn_g", [NCORES * D, TPC], BF16)
    pattn = IN("pattn", [NCORES * D, TPC], BF16)
    attn_sl = IN("attn_sl", [D, TPC], BF16)
    x1_sl = IN("x1_sl", [D, TPC], F32)
    xn1_sl = IN("xn1_sl", [D, TPC], BF16)
    xn1_g = IN("xn1_g", [NCORES * D, TPC], BF16)
    pattn2 = IN("pattn2", [NCORES * D, TPC], BF16)
    attn2_sl = IN("attn2_sl", [D, TPC], BF16)
    x2_sl = nc.dram_tensor("x2_sl", [D, TPC], F32, kind="ExternalOutput").ap()

    with ExitStack() as st:
        ph = Ctr(st.enter_context(nc.semaphore("ph")))
        block = st.enter_context(nc.Block())
        st.enter_context(nc.allow_low_precision(
            reason="bf16 elementwise per problem statement (honest bf16 blessed)"))
        C = mkctx(nc, st, block, ph, "f_")

        emit_ln(C, x_sl, xn_sl)
        if upto >= 1:
            emit_coll(C, "AllGather", xn_sl, xn_g)
        if upto >= 2:
            emit_l0attn_f(C, nc, xn_g, pattn)
        if upto >= 3:
            emit_coll(C, "ReduceScatter", pattn, attn_sl)
        if upto >= 4:
            emit_mid_f(C, nc, attn_sl, xn_sl, x_sl, x1_sl, xn1_sl)
        if upto >= 5:
            emit_coll(C, "AllGather", xn1_sl, xn1_g)
        if upto >= 6:
            emit_l1attn_f(C, nc, xn1_g, pattn2)
        if upto >= 7:
            emit_coll(C, "ReduceScatter", pattn2, attn2_sl)
        if upto >= 8:
            emit_fin_f(C, nc, attn2_sl, xn1_sl, x1_sl, x2_sl)
        if upto < 8:
            fin_map = {0: xn_sl, 1: xn_g, 2: pattn, 3: attn_sl, 4: xn1_sl,
                       5: xn1_g, 6: pattn2, 7: attn2_sl}
            srcap = fin_map[upto]
            dbgo = nc.dram_tensor("dbgo", list(srcap.shape), srcap.dtype,
                                  kind="ExternalOutput").ap()

            @block.gpsimd
            def _(g):
                ph.wait(g)
                rows = srcap.shape[0]
                step = min(2048, rows)
                for r0 in range(0, rows, step):
                    ph.inc(g.dma_start(out=dbgo[r0:r0 + step, :],
                                       in_=srcap[r0:r0 + step, :]), 16)

        @block.gpsimd
        def _(g):
            ph.wait(g)
    return nc


def emit_l0attn_f(C, nc, xn_g, pattn):
    GT = 512
    NG = TOK // GT
    NCH = GT // 128
    wq = nc.dram_tensor("wq", [D, 256], F32, kind="ExternalInput").ap()
    wk = nc.dram_tensor("wk", [D, 256], F32, kind="ExternalInput").ap()
    wv = nc.dram_tensor("wv", [D, 256], F32, kind="ExternalInput").ap()
    cw = nc.dram_tensor("cw", [128, 24], F32, kind="ExternalInput").ap()
    fw1 = nc.dram_tensor("fw1", [D, 128], F32, kind="ExternalInput").ap()
    fw2 = nc.dram_tensor("fw2", [128, 256], F32, kind="ExternalInput").ap()
    gw1 = nc.dram_tensor("gw1", [D, 128], F32, kind="ExternalInput").ap()
    gw2 = nc.dram_tensor("gw2", [128, 256], F32, kind="ExternalInput").ap()
    gb2 = nc.dram_tensor("gb2", [1, 256], F32, kind="ExternalInput").ap()
    bw = nc.dram_tensor("bw", [D, 2], F32, kind="ExternalInput").ap()
    alog = nc.dram_tensor("alog", [1, 2], F32, kind="ExternalInput").ap()
    dtb = nc.dram_tensor("dtb", [128, 2], F32, kind="ExternalInput").ap()
    owp = nc.dram_tensor("owp", [256, D], F32, kind="ExternalInput").ap()
    onw = nc.dram_tensor("onw", [128, 1], F32, kind="ExternalInput").ap()
    CST = l0attn_consts()
    I32 = mybir.dt.int32
    cin = {}
    for k, v in CST.items():
        dt = BF16 if v.dtype == BF16NP else (I32 if v.dtype == np.int32 else F32)
        cin[k] = nc.dram_tensor(k, list(v.shape), dt, kind="ExternalInput").ap()

    with ExitStack() as st:
        def sbuf(n, sh, dt=F32):
            return st.enter_context(nc.sbuf_tensor("a_" + n, sh, dt))
        wqs = sbuf("wqs", [128, NFC * 256], BF16)
        wks = sbuf("wks", [128, NFC * 256], BF16)
        wvs = sbuf("wvs", [128, NFC * 256], BF16)
        fw1s = sbuf("fw1s", [128, NFC * 128], BF16)
        gw1s = sbuf("gw1s", [128, NFC * 128], BF16)
        fw2s = sbuf("fw2s", [128, 256], BF16)
        gw2s = sbuf("gw2s", [128, 256], BF16)
        gb2s = sbuf("gb2s", [1, 256], F32)
        bws = sbuf("bws", [128, NFC * 2], BF16)
        cws = sbuf("cws", [128, 24], F32)
        alogs = sbuf("alogs", [1, 2], F32)
        negA = sbuf("negA", [128, 2], F32)
        dts = sbuf("dts", [128, 2], F32)
        onesc = sbuf("onesc", [1, 128], F32)
        ones1b = sbuf("ones1b", [1, 128], BF16)
        onecol = sbuf("onecol", [128, 1], F32)
        onebias = sbuf("onebias", [128, 1], F32)
        epsl2 = sbuf("epsl2", [1, 1], F32)
        cs = {}
        for k, v in CST.items():
            dt = BF16 if v.dtype == BF16NP else (mybir.dt.int32 if v.dtype == np.int32 else F32)
            cs[k] = sbuf("c2_" + k, list(v.shape), dt)
        mskd = sbuf("mskd", [128, 128], mybir.dt.int32)
        xng = sbuf("xng0", [128, NFC * GT], BF16)
        raw = {nm: sbuf("raw" + nm, [128, GT + 3], BF16)
               for nm in ("q0", "q1", "k0", "k1", "v0", "v1")}
        qh = [sbuf(f"qh{h}", [128, GT], BF16) for h in range(2)]
        kh = [sbuf(f"kh{h}", [128, GT], BF16) for h in range(2)]
        vh = [sbuf(f"vh{h}", [128, GT], BF16) for h in range(2)]
        innf = sbuf("innf", [128, GT], BF16)
        inng = sbuf("inng", [128, GT], BF16)
        betac = sbuf("betac", [2, GT], F32)
        nsc = sbuf("nsc", [1, GT], F32)
        beta_f = sbuf("beta_f", [128, NCH * 2], F32)
        beta_fn = sbuf("beta_fn", [128, NCH * 2], F32)
        gate_tm = sbuf("gate_tm", [128, NCH * 256], BF16)
        Gf = sbuf("Gf", [128, GT], F32)
        RDf = sbuf("RDf", [128, GT], F32)
        SDf = sbuf("SDf", [128, GT], F32)
        ERf = sbuf("ERf", [128, GT], F32)
        gkf = sbuf("gkf", [128, GT], F32)
        sp1 = sbuf("sp1", [128, GT], F32)
        spb = sbuf("spb", [128, GT], F32)
        spc = sbuf("spc", [128, GT], BF16)
        spb2 = sbuf("spb2", [128, GT], BF16)
        mapn = ["EQ", "ED", "rowb", "colb", "r8", "c8", "r16", "c16", "r32", "c32", "r64", "c64"]
        maps = {k: sbuf("m_" + k, [128, GT], BF16) for k in mapn}
        C1 = sbuf("C1", [128, GT], BF16); C2 = sbuf("C2", [128, GT], BF16)
        C3 = sbuf("C3", [128, GT], BF16); C4 = sbuf("C4", [128, GT], BF16)
        C5 = sbuf("C5", [128, GT], BF16); C6 = sbuf("C6", [128, GT], BF16)
        RBe = sbuf("RBe", [128, 2 * GT], BF16); RBo = sbuf("RBo", [128, 2 * GT], BF16)
        R8 = sbuf("R8", [128, 2 * GT], BF16); R16 = sbuf("R16", [128, 2 * GT], BF16)
        R32 = sbuf("R32", [128, 2 * GT], BF16); R64 = sbuf("R64", [128, 2 * GT], BF16)
        Qs = sbuf("Qs", [128, GT], BF16); Ws = sbuf("Ws", [128, GT], BF16)
        KD = sbuf("KD", [128, GT], BF16)
        eGend = sbuf("eGend", [128, NCH], F32)
        Aqt4 = sbuf("Aqt4", [128, NCH * 128], BF16)
        Ntn4 = sbuf("Ntn4", [128, NCH * 128], BF16)
        powT4 = [sbuf(f"powT4_{i}", [128, NCH * 128], BF16) for i in range(6)]
        pown4 = [sbuf(f"pown4_{i}", [128, NCH * 128], BF16) for i in range(6)]
        ybuf4 = [sbuf(f"ybuf4_{i}", [128, NCH * 256], BF16) for i in range(2)]
        rhsb4 = sbuf("rhsb4", [128, NCH * 256], BF16)
        S_f = [sbuf(f"S_f{h}", [128, 128], F32) for h in range(2)]
        S_b = [sbuf(f"S_b{h}", [128, 128], BF16) for h in range(2)]
        UWc = sbuf("UWc", [128, 128], BF16)
        KDt = sbuf("KDt", [128, 128], BF16)
        xdel = sbuf("xdel", [128, 128], BF16)
        useq = sbuf("useq", [128, 128], BF16)
        ssq = sbuf("ssq", [128, 12], F32)
        obuf = sbuf("obuf", [128, NCH * 256], BF16)
        ows2 = sbuf("ows2", [128, 2 * D], BF16)
        ofcm = sbuf("ofcm", [128, 2 * GT], BF16)
        patb2 = sbuf("patb2", [128, 2 * GT], BF16)
        onws = sbuf("onws", [128, 1], F32)
        pA = st.enter_context(nc.psum_tensor("a_pA", [128, GT], F32))
        pC = st.enter_context(nc.psum_tensor("a_pC", [2, GT], F32))
        pScoreB = st.enter_context(nc.psum_tensor("a_pScoreB", [128, NCH * 256], F32))
        pStageB = st.enter_context(nc.psum_tensor("a_pStageB", [16, NCH * 32], F32))
        pT4 = st.enter_context(nc.psum_tensor("a_pT4", [128, NCH * 128], BF16))
        pSolve = st.enter_context(nc.psum_tensor("a_pSolve", [128, NCH * 256], F32))
        ph = C.ph
        block = C.block

        def T(fn):
            @block.tensor
            def _(t):
                ph.wait(t)
                fn(t)
        def V(fn):
            @block.vector
            def _(v):
                ph.wait(v)
                fn(v)
        def SC(fn):
            @block.scalar
            def _(s):
                ph.wait(s)
                fn(s)
        def G3(fn):
            @block.gpsimd
            def _(g):
                ph.wait(g)
                fn(g)

        @block.gpsimd
        def _(g):
            ph.wait(g)
            for w_dram, w_sb, nch in ((wq, wqs, 256), (wk, wks, 256), (wv, wvs, 256),
                                      (fw1, fw1s, 128), (gw1, gw1s, 128), (bw, bws, 2)):
                src = w_dram.rearrange("(j p) n -> p j n", p=128)
                dst = w_sb[:, :].rearrange("p (j n) -> p j n", n=nch)
                ph.inc(g.dma_start(out=dst, in_=src), 16)
            for w_dram, w_sb in ((fw2, fw2s), (gw2, gw2s), (cw, cws)):
                ph.inc(g.dma_start(out=w_sb[:, :], in_=w_dram), 16)
            srcw = owp.rearrange("(j p) n -> p j n", p=128)
            dstw = ows2[:, :].rearrange("p (j n) -> p j n", n=D)
            ph.inc(g.dma_start(out=dstw, in_=srcw), 16)

        @block.sync
        def _(sy):
            ph.wait(sy)
            ph.inc(sy.dma_start(out=gb2s[:, :], in_=gb2), 16)
            ph.inc(sy.dma_start(out=alogs[:, :], in_=alog), 16)
            ph.inc(sy.dma_start(out=dts[:, :], in_=dtb), 16)
            ph.inc(sy.dma_start(out=onws[:, :], in_=onw), 16)
            for k in CST:
                ph.inc(sy.dma_start(out=cs[k][:, :], in_=cin[k]), 16)

        def setup1(v):
            v.memset(onesc[:, :], 1.0)
            v.memset(ones1b[:, :], 1.0)
            v.memset(onecol[:, :], 1.0)
            v.memset(onebias[:, :], 1.0)
            v.memset(epsl2[:, :], 1e-6)
            for h in range(2):
                v.memset(S_f[h][:, :], 0.0)
                v.memset(S_b[h][:, :], 0.0)
            ph.inc(v.engine_nop())
        V(setup1)
        SC(lambda s: ph.inc(s.activation(out=alogs[:, :], in_=alogs[:, :], func=AF.Exp)))
        V(lambda v: ph.inc(v.tensor_scalar_mul(out=alogs[:, :], in0=alogs[:, :], scalar1=-1.0)))
        T(lambda t: ph.inc(t.matmul(out=pA[:, 0:2], lhsT=onesc[:, :], rhs=alogs[:, :],
                                    start=True, stop=True)))
        V(lambda v: ph.inc(v.tensor_copy(out=negA[:, :], in_=pA[:, 0:2])))

        for g in range(NG):
            bstart = (g % max(NG // 4, 8) == 0)
            xbuf = xng

            def loadxn(sy, g=g, xbuf=xbuf):
                rk = (g * GT) // TPC
                cb = (g * GT) % TPC
                for j in range(NFC):
                    ins = sy.dma_start(out=xbuf[:, j * GT:(j + 1) * GT],
                                       in_=xn_g[rk * D + j * 128:rk * D + (j + 1) * 128,
                                                cb:cb + GT])
                    ph.inc(ins, 16)
            @block.sync
            def _(sy):
                ph.wait(sy)
                loadxn(sy)

            for pi, wsb in enumerate((wqs, wks, wvs)):
                for ht in range(2):
                    nm = ("q0q1", "k0k1", "v0v1")[pi][ht * 2:ht * 2 + 2]
                    rb = raw[nm]
                    def mmproj(t, wsb=wsb, ht=ht, xbuf=xbuf):
                        for j in range(NFC):
                            ins = t.matmul(out=pA[:, :],
                                           lhsT=wsb[:, j * 256 + ht * 128:j * 256 + ht * 128 + 128],
                                           rhs=xbuf[:, j * GT:(j + 1) * GT],
                                           start=(j == 0), stop=(j == NFC - 1))
                        ph.inc(ins)
                    T(mmproj)
                    def convf(v, rb=rb, pi=pi, ht=ht, bstart=bstart):
                        v.tensor_copy(out=rb[:, 3:3 + GT], in_=pA[:, :])
                        if bstart:
                            v.memset(rb[:, 0:3], 0.0)
                        v.engine_nop()
                        ci = (pi * 2 + ht) * 4
                        v.tensor_scalar(out=spc[:, :], in0=rb[:, 0:GT],
                                        scalar1=cws[:, ci:ci + 1], scalar2=None, op0=ALU.mult)
                        for i in range(1, 4):
                            v.scalar_tensor_tensor(out=spc[:, :], in0=rb[:, i:i + GT],
                                                   scalar=cws[:, ci + i:ci + i + 1],
                                                   in1=spc[:, :], op0=ALU.mult, op1=ALU.add)
                        ph.inc(v.tensor_copy(out=rb[:, 0:3], in_=rb[:, GT:GT + 3]))
                    V(convf)
                    dst = (qh, kh, vh)[pi][ht]
                    SC(lambda s: ph.inc(s.activation(out=sp1[:, :], in_=spc[:, :],
                                                     func=AF.Tanh, scale=0.5)))
                    def silu_v(v, dst=dst):
                        v.tensor_scalar(out=spb2[:, :], in0=sp1[:, :],
                                        scalar1=0.5, scalar2=0.5, op0=ALU.mult, op1=ALU.add)
                        v.engine_nop()
                        ph.inc(v.tensor_mul(out=dst[:, :], in0=spc[:, :], in1=spb2[:, :]))
                    V(silu_v)

            for pi in range(2):
                for ht in range(2):
                    src = (qh, kh)[pi][ht]
                    SC(lambda s, src=src: ph.inc(s.activation(out=sp1[:, :], in_=src[:, :],
                                                              func=AF.Square)))
                    T(lambda t: ph.inc(t.matmul(out=pC[0:1, :], lhsT=onecol[:, 0:1],
                                                rhs=sp1[:, :], start=True, stop=True)))
                    SC(lambda s: ph.inc(s.activation(out=nsc[0:1, :], in_=pC[0:1, :],
                                                     func=AF.Sqrt, bias=epsl2[0:1, 0:1])))
                    V(lambda v: ph.inc(v.reciprocal(out=nsc[0:1, :], in_=nsc[0:1, :])))
                    T(lambda t: ph.inc(t.matmul(out=pA[:, :], lhsT=onesc[:, :],
                                                rhs=nsc[0:1, :], start=True, stop=True)))
                    scl = (128.0 ** -0.5) if pi == 0 else 1.0
                    def l2fin(v, src=src, scl=scl):
                        v.tensor_scalar_mul(out=spb2[:, :], in0=pA[:, :], scalar1=scl)
                        v.engine_nop()
                        ph.inc(v.tensor_mul(out=src[:, :], in0=src[:, :], in1=spb2[:, :]))
                    V(l2fin)

            for wsb, dst in ((fw1s, innf), (gw1s, inng)):
                def mminner(t, wsb=wsb, xbuf=xbuf):
                    for j in range(NFC):
                        ins = t.matmul(out=pA[:, :], lhsT=wsb[:, j * 128:(j + 1) * 128],
                                       rhs=xbuf[:, j * GT:(j + 1) * GT],
                                       start=(j == 0), stop=(j == NFC - 1))
                    ph.inc(ins)
                T(mminner)
                V(lambda v, dst=dst: ph.inc(v.tensor_copy(out=dst[:, :], in_=pA[:, :])))

            def mmbeta(t, xbuf=xbuf):
                for j in range(NFC):
                    ins = t.matmul(out=pC[:, :], lhsT=bws[:, j * 2:(j + 1) * 2],
                                   rhs=xbuf[:, j * GT:(j + 1) * GT],
                                   start=(j == 0), stop=(j == NFC - 1))
                ph.inc(ins)
            T(mmbeta)
            SC(lambda s: ph.inc(s.activation(out=betac[:, :], in_=pC[:, :],
                                             func=AF.Tanh, scale=0.5)))
            V(lambda v: ph.inc(v.tensor_scalar(out=betac[:, :], in0=betac[:, :],
                                               scalar1=0.5, scalar2=0.5,
                                               op0=ALU.mult, op1=ALU.add)))
            def trbeta(t):
                for ch in range(NCH):
                    ins = t.transpose(out=pSolve[:, ch * 256:ch * 256 + 2],
                                      in_=betac[:, ch * 128:(ch + 1) * 128],
                                      identity=cs["identf"][0:2, 0:2])
                ph.inc(ins)
            T(trbeta)
            def betaev(v):
                for ch in range(NCH):
                    v.tensor_copy(out=beta_f[:, ch * 2:(ch + 1) * 2],
                                  in_=pSolve[:, ch * 256:ch * 256 + 2])
                for ch in range(NCH):
                    ins = v.tensor_scalar_mul(out=beta_fn[:, ch * 2:(ch + 1) * 2],
                                              in0=pSolve[:, ch * 256:ch * 256 + 2],
                                              scalar1=-1.0)
                ph.inc(ins)
            V(betaev)

            for cp in range(NCH // 2):
                def mmgate(t, cp=cp):
                    for q in range(2):
                        ch = cp * 2 + q
                        t.matmul(out=pA[:, q * 256:q * 256 + 256],
                                 lhsT=inng[:, ch * 128:(ch + 1) * 128],
                                 rhs=gw2s[:, :], start=True, stop=False)
                        ins = t.matmul(out=pA[:, q * 256:q * 256 + 256], lhsT=onesc[:, :],
                                       rhs=gb2s[:, :], start=False, stop=True,
                                       skip_group_check=True)
                    ph.inc(ins)
                T(mmgate)
                SC(lambda s: ph.inc(s.activation(out=sp1[:, :], in_=pA[:, :],
                                                 func=AF.Tanh, scale=0.5)))
                def gatefin(v, cp=cp):
                    ph.inc(v.tensor_scalar(out=gate_tm[:, cp * 512:(cp + 1) * 512],
                                           in0=sp1[:, :], scalar1=0.5, scalar2=0.5,
                                           op0=ALU.mult, op1=ALU.add))
                V(gatefin)

            for ht in range(2):
                T(lambda t, ht=ht: ph.inc(t.matmul(out=pA[:, :],
                                                   lhsT=fw2s[:, ht * 128:(ht + 1) * 128],
                                                   rhs=innf[:, :], start=True, stop=True)))
                V(lambda v, ht=ht: ph.inc(v.tensor_scalar_add(out=sp1[:, :], in0=pA[:, :],
                                                              scalar1=dts[:, ht:ht + 1])))
                SC(lambda s: ph.inc(s.activation(out=gkf[:, :], in_=sp1[:, :], func=AF.Exp)))
                SC(lambda s: ph.inc(s.activation(out=sp1[:, :], in_=gkf[:, :], func=AF.Ln,
                                                 bias=onebias[:, 0:1])))
                V(lambda v, ht=ht: ph.inc(v.tensor_scalar(out=gkf[:, :], in0=sp1[:, :],
                                                          scalar1=negA[:, ht:ht + 1],
                                                          scalar2=-CLAMP,
                                                          op0=ALU.mult, op1=ALU.max)))

                def scan_map(rm, dst):
                    V(lambda v, rm=rm, dst=dst: ph.inc(
                        v.tensor_tensor_scan(out=dst[:, :], data0=cs[rm][:, :],
                                             data1=gkf[:, :], initial=0.0,
                                             op0=ALU.mult, op1=ALU.add)))
                def sufd(v, src, dst, last, step):
                    nb = GT // step
                    gath = src[:, last::step].unsqueeze(2).broadcast_to([128, nb, step])
                    s3 = src[:, :].rearrange("p (a b) -> p a b", b=step)
                    d3 = dst[:, :].rearrange("p (a b) -> p a b", b=step)
                    return v.tensor_sub(out=d3, in0=gath, in1=s3)
                def expm(s, src, dst, scale=1.0):
                    return s.activation(out=maps[dst][:, :], in_=src[:, :], func=AF.Exp,
                                        scale=scale)

                scan_map("rm128", Gf)
                V(lambda v: ph.inc(sufd(v, Gf, SDf, 127, 128)))
                def exps128(s):
                    expm(s, Gf, "EQ")
                    expm(s, SDf, "ED")
                    ph.inc(s.activation(out=eGend[:, 0:NCH], in_=Gf[:, 127::128],
                                        func=AF.Exp))
                SC(exps128)
                scan_map("rm8", RDf)
                def d8(v):
                    sufd(v, RDf, SDf, 7, 8)
                    gath = RDf[:, 3::8].unsqueeze(2).broadcast_to([128, GT // 8, 8])
                    s3 = RDf[:, :].rearrange("p (a b) -> p a b", b=8)
                    d3 = ERf[:, :].rearrange("p (a b) -> p a b", b=8)
                    ph.inc(v.tensor_sub(out=d3, in0=s3, in1=gath))
                V(d8)
                def exps8(s):
                    expm(s, RDf, "r8")
                    expm(s, SDf, "c8")
                    expm(s, ERf, "rowb")
                    ph.inc(expm(s, ERf, "colb", scale=-1.0))
                SC(exps8)
                for m, rn, cn in ((16, "r16", "c16"), (32, "r32", "c32"), (64, "r64", "c64")):
                    scan_map(f"rm{m}", RDf)
                    V(lambda v, m=m: ph.inc(sufd(v, RDf, SDf, m - 1, m)))
                    def expsm(s, rn=rn, cn=cn):
                        expm(s, RDf, rn)
                        ph.inc(expm(s, SDf, cn))
                    SC(expsm)

                def optiles(v, ht=ht):
                    q = qh[ht]; k = kh[ht]
                    v.tensor_mul(out=Qs[:, :], in0=q[:, :], in1=maps["EQ"][:, :])
                    v.tensor_mul(out=Ws[:, :], in0=k[:, :], in1=maps["EQ"][:, :])
                    v.tensor_mul(out=KD[:, :], in0=k[:, :], in1=maps["ED"][:, :])
                    v.tensor_mul(out=C1[:, :], in0=k[:, :], in1=maps["colb"][:, :])
                    v.tensor_mul(out=C2[:, :], in0=C1[:, :], in1=cs["odd8"][:, :])
                    v.tensor_mul(out=C1[:, :], in0=C1[:, :], in1=cs["evn8"][:, :])
                    v.tensor_mul(out=RBe[:, 0:GT], in0=q[:, :], in1=maps["rowb"][:, :])
                    v.tensor_mul(out=RBe[:, GT:2 * GT], in0=k[:, :], in1=maps["rowb"][:, :])
                    v.tensor_mul(out=RBo[:, 0:GT], in0=RBe[:, 0:GT], in1=cs["odd8"][:, :])
                    v.tensor_mul(out=RBo[:, GT:2 * GT], in0=RBe[:, GT:2 * GT], in1=cs["odd8"][:, :])
                    v.tensor_mul(out=RBe[:, 0:GT], in0=RBe[:, 0:GT], in1=cs["evn8"][:, :])
                    v.tensor_mul(out=RBe[:, GT:2 * GT], in0=RBe[:, GT:2 * GT], in1=cs["evn8"][:, :])
                    v.tensor_mul(out=C3[:, :], in0=k[:, :], in1=maps["c8"][:, :])
                    v.tensor_mul(out=C3[:, :], in0=C3[:, :], in1=cs["evn8"][:, :])
                    v.tensor_mul(out=R8[:, 0:GT], in0=q[:, :], in1=maps["r8"][:, :])
                    v.tensor_mul(out=R8[:, GT:2 * GT], in0=k[:, :], in1=maps["r8"][:, :])
                    v.tensor_mul(out=R8[:, 0:GT], in0=R8[:, 0:GT], in1=cs["odd8"][:, :])
                    v.tensor_mul(out=R8[:, GT:2 * GT], in0=R8[:, GT:2 * GT], in1=cs["odd8"][:, :])
                    for Cx, Rx, rn, cn in ((C4, R16, "r16", "c16"), (C5, R32, "r32", "c32"),
                                           (C6, R64, "r64", "c64")):
                        v.tensor_mul(out=Cx[:, :], in0=k[:, :], in1=maps[cn][:, :])
                        v.tensor_mul(out=Rx[:, 0:GT], in0=q[:, :], in1=maps[rn][:, :])
                        v.tensor_mul(out=Rx[:, GT:2 * GT], in0=k[:, :], in1=maps[rn][:, :])
                    ph.inc(v.engine_nop())
                V(optiles)

                def scoresb(t):
                    for ch in range(NCH):
                        c0 = ch * 128
                        def rhs2(Rt, lo, hi):
                            r3 = Rt[:, :].rearrange("p (h t) -> p h t", h=2)
                            return r3[:, :, c0 + lo:c0 + hi]
                        ps3 = pScoreB[:, ch * 256:(ch + 1) * 256].rearrange(
                            "p (h t) -> p h t", h=2)
                        t.matmul(out=ps3[:, :, :], lhsT=C1[:, c0:c0 + 128],
                                 rhs=rhs2(RBe, 0, 128), start=True, stop=False)
                        t.matmul(out=ps3[:, :, :], lhsT=C2[:, c0:c0 + 128],
                                 rhs=rhs2(RBo, 0, 128), start=False, stop=False)
                        t.matmul(out=ps3[:, :, :], lhsT=C3[:, c0:c0 + 128],
                                 rhs=rhs2(R8, 0, 128), start=False, stop=True)
                        for p in range(3):
                            t.matmul(out=ps3[32 * p:32 * p + 16, :, 32 * p + 16:32 * p + 32],
                                     lhsT=C4[:, c0 + 32 * p:c0 + 32 * p + 16],
                                     rhs=rhs2(R16, 32 * p + 16, 32 * p + 32),
                                     start=True, stop=True)
                        st3 = pStageB[:, ch * 32:(ch + 1) * 32].rearrange(
                            "p (h t) -> p h t", h=2)
                        t.matmul(out=st3[0:16, :, 0:16], lhsT=C4[:, c0 + 96:c0 + 112],
                                 rhs=rhs2(R16, 112, 128), start=True, stop=True)
                        for p in range(2):
                            t.matmul(out=ps3[64 * p:64 * p + 32, :, 64 * p + 32:64 * p + 64],
                                     lhsT=C5[:, c0 + 64 * p:c0 + 64 * p + 32],
                                     rhs=rhs2(R32, 64 * p + 32, 64 * p + 64),
                                     start=True, stop=True)
                        ins = t.matmul(out=ps3[0:64, :, 64:128], lhsT=C6[:, c0:c0 + 64],
                                       rhs=rhs2(R64, 64, 128), start=True, stop=True)
                    ph.inc(ins)
                T(scoresb)

                def evacb(v, ht=ht):
                    for ch in range(NCH):
                        cA = ch * 128
                        v.tensor_tensor(out=mskd[:, :],
                                        in0=pScoreB[:, ch * 256:ch * 256 + 128]
                                        .bitcast(mybir.dt.int32),
                                        in1=cs["maskAb"][:, :], op=ALU.bitwise_and)
                        v.engine_nop()
                        v.tensor_copy(out=Aqt4[:, cA:cA + 128], in_=mskd[:, :].bitcast(F32))
                        v.tensor_tensor(out=mskd[:, :],
                                        in0=pScoreB[:, ch * 256 + 128:ch * 256 + 256]
                                        .bitcast(mybir.dt.int32),
                                        in1=cs["maskMb"][:, :], op=ALU.bitwise_and)
                        v.engine_nop()
                        v.tensor_scalar_mul(out=Ntn4[:, cA:cA + 128],
                                            in0=mskd[:, :].bitcast(F32),
                                            scalar1=beta_fn[:, ch * 2 + ht:ch * 2 + ht + 1])
                        v.tensor_copy(out=Aqt4[96:112, cA + 112:cA + 128],
                                      in_=pStageB[0:16, ch * 32:ch * 32 + 16])
                        ins = v.tensor_scalar_mul(
                            out=Ntn4[96:112, cA + 112:cA + 128],
                            in0=pStageB[0:16, ch * 32 + 16:ch * 32 + 32],
                            scalar1=beta_fn[96:112, ch * 2 + ht:ch * 2 + ht + 1])
                    ph.inc(ins)
                V(evacb)

                def trz(t):
                    for ch in range(NCH):
                        ins = t.transpose(out=pT4[:, ch * 128:(ch + 1) * 128],
                                          in_=Ntn4[:, ch * 128:(ch + 1) * 128],
                                          identity=cs["identb"][:, :])
                    ph.inc(ins)
                T(trz)
                def cpz(v):
                    for ch in range(NCH):
                        ins = v.tensor_copy(out=pown4[0][:, ch * 128:(ch + 1) * 128],
                                            in_=pT4[:, ch * 128:(ch + 1) * 128])
                    ph.inc(ins)
                V(cpz)
                for j in range(1, 7):
                    def sqrb(t, j=j):
                        for ch in range(NCH):
                            sl = slice(ch * 128, (ch + 1) * 128)
                            lastT = Ntn4[:, sl] if j == 1 else powT4[j - 2][:, sl]
                            lastN = pown4[j - 1][:, sl]
                            ins = t.matmul(out=pA[:, sl], lhsT=lastN, rhs=lastT,
                                           start=True, stop=True)
                            if j < 6:
                                ins = t.matmul(out=pSolve[:, sl], lhsT=lastT, rhs=lastN,
                                               start=True, stop=True)
                        ph.inc(ins)
                    T(sqrb)
                    def sqevb(v, j=j):
                        for ch in range(NCH):
                            sl = slice(ch * 128, (ch + 1) * 128)
                            ins = v.tensor_copy(out=powT4[j - 1][:, sl], in_=pA[:, sl])
                            if j < 6:
                                ins = v.tensor_copy(out=pown4[j][:, sl], in_=pSolve[:, sl])
                        ph.inc(ins)
                    V(sqevb)
                def trv(t, ht=ht):
                    for ch in range(NCH):
                        ins = t.transpose(out=pT4[:, ch * 128:(ch + 1) * 128],
                                          in_=vh[ht][:, ch * 128:(ch + 1) * 128],
                                          identity=cs["identb"][:, :])
                    ph.inc(ins)
                T(trv)
                def cpv(v):
                    for ch in range(NCH):
                        ins = v.tensor_copy(out=rhsb4[:, ch * 256:ch * 256 + 128],
                                            in_=pT4[:, ch * 128:(ch + 1) * 128])
                    ph.inc(ins)
                V(cpv)
                def trw(t):
                    for ch in range(NCH):
                        ins = t.transpose(out=pT4[:, ch * 128:(ch + 1) * 128],
                                          in_=Ws[:, ch * 128:(ch + 1) * 128],
                                          identity=cs["identb"][:, :])
                    ph.inc(ins)
                T(trw)
                def cpw(v):
                    for ch in range(NCH):
                        ins = v.tensor_copy(out=rhsb4[:, ch * 256 + 128:ch * 256 + 256],
                                            in_=pT4[:, ch * 128:(ch + 1) * 128])
                    ph.inc(ins)
                V(cpw)
                ydl = spb[:, :].bitcast(BF16)
                ycur = rhsb4
                for j in range(6, -1, -1):
                    ynext = ybuf4[j % 2]
                    def chmm(t, j=j, ycur=ycur):
                        for ch in range(NCH):
                            p = Ntn4 if j == 0 else powT4[j - 1]
                            ins = t.matmul(out=pSolve[:, ch * 256:(ch + 1) * 256],
                                           lhsT=p[:, ch * 128:(ch + 1) * 128],
                                           rhs=ycur[:, ch * 256:(ch + 1) * 256],
                                           start=True, stop=True)
                        ph.inc(ins)
                    T(chmm)
                    def chadd(v, ycur=ycur, ynext=ynext):
                        for ch in range(NCH):
                            v.tensor_copy(out=ydl[:, ch * 256:(ch + 1) * 256],
                                          in_=pSolve[:, ch * 256:(ch + 1) * 256])
                        v.engine_nop()
                        for ch in range(NCH):
                            ins = v.tensor_add(out=ynext[:, ch * 256:(ch + 1) * 256],
                                               in0=ycur[:, ch * 256:(ch + 1) * 256],
                                               in1=ydl[:, ch * 256:(ch + 1) * 256])
                        ph.inc(ins)
                    V(chadd)
                    ycur = ynext
                U4 = ycur

                for ch in range(NCH):
                    c0 = ch * 128
                    def trs(t, ch=ch, c0=c0, U4=U4):
                        t.transpose(out=pT4[:, 0:128], in_=U4[:, ch * 256 + 128:ch * 256 + 256],
                                    identity=cs["identb"][:, :])
                        ph.inc(t.transpose(out=pT4[:, 128:256], in_=KD[:, c0:c0 + 128],
                                           identity=cs["identb"][:, :]))
                    T(trs)
                    def cps(v):
                        v.tensor_copy(out=UWc[:, :], in_=pT4[:, 0:128])
                        ph.inc(v.tensor_copy(out=KDt[:, :], in_=pT4[:, 128:256]))
                    V(cps)
                    T(lambda t, ht=ht: ph.inc(t.matmul(out=pSolve[:, 0:128], lhsT=UWc[:, :],
                                                       rhs=S_b[ht][:, :],
                                                       start=True, stop=True)))
                    def useqf(v, U4=U4, ch=ch, ht=ht):
                        v.tensor_copy(out=xdel[:, :], in_=pSolve[:, 0:128])
                        v.engine_nop()
                        v.tensor_sub(out=spb[:, 0:128], in0=U4[:, ch * 256:ch * 256 + 128],
                                     in1=xdel[:, :])
                        v.engine_nop()
                        ph.inc(v.tensor_scalar_mul(out=useq[:, :], in0=spb[:, 0:128],
                                                   scalar1=beta_f[:, ch * 2 + ht:ch * 2 + ht + 1]))
                    V(useqf)
                    def omm3(t, c0=c0, ht=ht):
                        t.matmul(out=pSolve[:, 256:384], lhsT=Qs[:, c0:c0 + 128],
                                 rhs=S_b[ht][:, :], start=True, stop=False)
                        t.matmul(out=pSolve[:, 256:384], lhsT=Aqt4[:, c0:c0 + 128],
                                 rhs=useq[:, :], start=False, stop=True)
                        ph.inc(t.matmul(out=pSolve[:, 128:256], lhsT=KDt[:, :],
                                        rhs=useq[:, :], start=True, stop=True))
                    T(omm3)
                    def supdf(v, ch=ch, ht=ht):
                        v.scalar_tensor_tensor(out=S_f[ht][:, :], in0=S_f[ht][:, :],
                                               scalar=eGend[:, ch:ch + 1],
                                               in1=pSolve[:, 128:256],
                                               op0=ALU.mult, op1=ALU.add)
                        v.tensor_copy(out=S_b[ht][:, :], in_=S_f[ht][:, :])
                        ph.inc(v.tensor_copy(out=sp1[:, c0:c0 + 128],
                                             in_=pSolve[:, 256:384]))
                    V(supdf)

                def episq(s):
                    for ch in range(NCH):
                        ins = s.activation(out=spb[:, ch * 128:(ch + 1) * 128],
                                           in_=sp1[:, ch * 128:(ch + 1) * 128],
                                           func=AF.Square, accum_out=ssq[:, ch:ch + 1])
                    ph.inc(ins)
                SC(episq)
                V(lambda v: ph.inc(v.tensor_scalar(out=ssq[:, 4:8], in0=ssq[:, 0:4],
                                                   scalar1=1.0 / 128, scalar2=1e-5,
                                                   op0=ALU.mult, op1=ALU.add)))
                SC(lambda s: ph.inc(s.activation(out=ssq[:, 8:12], in_=ssq[:, 4:8],
                                                 func=AF.Sqrt)))
                V(lambda v: ph.inc(v.reciprocal(out=ssq[:, 8:12], in_=ssq[:, 8:12])))
                def ofinb(v, ht=ht):
                    for ch in range(NCH):
                        v.tensor_scalar_mul(out=spb2[:, ch * 128:(ch + 1) * 128],
                                            in0=sp1[:, ch * 128:(ch + 1) * 128],
                                            scalar1=ssq[:, 8 + ch:9 + ch])
                    v.engine_nop()
                    for ch in range(NCH):
                        dst = obuf[:, ch * 256 + ht * 128:ch * 256 + ht * 128 + 128]
                        ins = v.tensor_mul(out=dst, in0=spb2[:, ch * 128:(ch + 1) * 128],
                                           in1=gate_tm[:, ch * 256 + ht * 128:
                                                       ch * 256 + ht * 128 + 128])
                    ph.inc(ins)
                V(ofinb)

            for fc in range(2):
                def trob(t, fc=fc):
                    for ch in range(NCH):
                        ins = t.transpose(
                            out=pT4[:, ch * 128:(ch + 1) * 128],
                            in_=obuf[:, ch * 256 + fc * 128:ch * 256 + fc * 128 + 128],
                            identity=cs["identb"][:, :])
                    ph.inc(ins)
                T(trob)
                def cpob(v, fc=fc):
                    for ch in range(NCH):
                        ins = v.tensor_scalar_mul(
                            out=ofcm[:, fc * GT + ch * 128:fc * GT + (ch + 1) * 128],
                            in0=pT4[:, ch * 128:(ch + 1) * 128], scalar1=onws[:, 0:1])
                    ph.inc(ins)
                V(cpob)
            dstb = (g * GT) // TPC
            cb0 = (g * GT) % TPC
            for jp in range(NFC // 2):
                def pmm(t, jp=jp):
                    for q in range(2):
                        jo = jp * 2 + q
                        pdst = pA[:, 0:GT] if q == 0 else pScoreB[:, 0:GT]
                        t.matmul(out=pdst,
                                 lhsT=ows2[:, 0 * D + jo * 128:0 * D + jo * 128 + 128],
                                 rhs=ofcm[:, 0:GT], start=True, stop=False)
                        ins = t.matmul(out=pdst,
                                       lhsT=ows2[:, 1 * D + jo * 128:1 * D + jo * 128 + 128],
                                       rhs=ofcm[:, GT:2 * GT], start=False, stop=True)
                    ph.inc(ins)
                T(pmm)
                def cppat(v):
                    v.tensor_copy(out=patb2[:, 0:GT], in_=pA[:, 0:GT])
                    ph.inc(v.tensor_copy(out=patb2[:, GT:2 * GT], in_=pScoreB[:, 0:GT]))
                V(cppat)
                def stp(gq, jp=jp, dstb=dstb, cb0=cb0):
                    for q in range(2):
                        jo = jp * 2 + q
                        ins = gq.dma_start(
                            out=pattn[dstb * D + jo * 128:dstb * D + (jo + 1) * 128,
                                      cb0:cb0 + GT],
                            in_=patb2[:, q * GT:(q + 1) * GT])
                        ph.inc(ins, 16)
                G3(stp)

        @block.gpsimd
        def _(gq):
            ph.wait(gq)


def emit_mid_f(C, nc, attn_sl, xns, xs, x1o, xn1o):
    debug = False
    TT = 512
    NT = TPC // TT
    E, I = 8, 1408
    NI = I // 128
    rout = nc.dram_tensor("rout", [D, 8], F32, kind="ExternalInput").ap()
    wug = nc.dram_tensor("wug", [E, D, 2 * I], F32, kind="ExternalInput").ap()
    wdn = nc.dram_tensor("wdn", [E, I, D], F32, kind="ExternalInput").ap()
    swu = nc.dram_tensor("swu", [D, 2 * 5632], F32, kind="ExternalInput").ap()
    swd = nc.dram_tensor("swd", [5632, D], F32, kind="ExternalInput").ap()
    identf_in = nc.dram_tensor("identf_m", [128, 128], F32, kind="ExternalInput").ap()
    esel_in = nc.dram_tensor("esel", [8, 8 * 128], BF16, kind="ExternalInput").ap()
    if debug:
        dbg_w8 = nc.dram_tensor("dbg_w8", [8, TPC], F32, kind="ExternalOutput").ap()
        dbg_y = nc.dram_tensor("dbg_y", [128, TPC], F32, kind="ExternalOutput").ap()
        dbg_lg = nc.dram_tensor("dbg_lg", [8, TPC], F32, kind="ExternalOutput").ap()
        dbg_h = nc.dram_tensor("dbg_h", [128, TPC], F32, kind="ExternalOutput").ap()

    with ExitStack() as st:
        def sbuf(n, sh, dt=F32):
            return st.enter_context(nc.sbuf_tensor("m_" + n, sh, dt))
        ph = C.ph
        block = C.block

        def T(fn):
            @block.tensor
            def _(t):
                ph.wait(t)
                fn(t)
        def V(fn):
            @block.vector
            def _(v):
                ph.wait(v)
                fn(v)
        def SC(fn):
            @block.scalar
            def _(s):
                ph.wait(s)
                fn(s)
        def SY(fn):
            @block.sync
            def _(sy):
                ph.wait(sy)
                fn(sy)
        def GQ(fn):
            @block.gpsimd
            def _(g):
                ph.wait(g)
                fn(g)

        y_sb = sbuf("y_sb", [128, NFC * TPC], BF16)
        idf = sbuf("idf", [128, 128], F32)
        onesc = sbuf("onesc", [1, 128], F32)
        onecol = sbuf("onecol", [128, 1], F32)
        epsc = sbuf("epsc", [1, 1], F32)
        gt2 = [sbuf(f"gtile{i}", [128, TT], BF16) for i in range(2)]
        util = sbuf("util", [128, TT], BF16)
        wbt = sbuf("wbt", [128, TT], BF16)
        pA6 = [st.enter_context(nc.psum_tensor(f"m_pA{i}", [128, TT], F32))
               for i in range(6)]
        pL8 = st.enter_context(nc.psum_tensor("m_pL8", [8, TT], F32))
        s_w = st.enter_context(nc.semaphore("m_sw"))
        s_t = st.enter_context(nc.semaphore("m_st"))
        s_sc = st.enter_context(nc.semaphore("m_ssc"))
        s_v = st.enter_context(nc.semaphore("m_sv"))
        s_wd = st.enter_context(nc.semaphore("m_swd"))
        s_dt = st.enter_context(nc.semaphore("m_sdt"))
        s_dv = st.enter_context(nc.semaphore("m_sdv"))

        def setup(v):
            v.memset(onesc[:, :], 1.0)
            v.memset(onecol[:, :], 1.0)
            v.memset(epsc[:, :], 1e-5)
            for j in range(NFC):
                v.memset(y_sb[:, j * TPC:(j + 1) * TPC], 0.0)
            ph.inc(v.engine_nop())
        V(setup)

        with ExitStack() as stB:
            xnsb = stB.enter_context(nc.sbuf_tensor("m_xnsb", [128, NFC * TPC], BF16))
            wga0 = stB.enter_context(nc.sbuf_tensor("m_wga0", [128, NFC * 128], BF16))
            wgb0 = stB.enter_context(nc.sbuf_tensor("m_wgb0", [128, NFC * 128], BF16))
            wga1 = stB.enter_context(nc.sbuf_tensor("m_wga1", [128, NFC * 128], BF16))
            wgb1 = stB.enter_context(nc.sbuf_tensor("m_wgb1", [128, NFC * 128], BF16))
            wdb_a = stB.enter_context(nc.sbuf_tensor("m_wdb", [128, NI * 128], BF16))
            wdb_b = stB.enter_context(nc.sbuf_tensor("m_wdb2", [128, NI * 128], BF16))
            w8cm = stB.enter_context(nc.sbuf_tensor("m_w8cm", [8, TPC], BF16))
            esel = stB.enter_context(nc.sbuf_tensor("m_esel_sb", [8, 8 * 128], BF16))
            stR = stB.enter_context(ExitStack())
            routs = stR.enter_context(nc.sbuf_tensor("m_routs", [128, NFC * 8], BF16))
            lgt = stR.enter_context(nc.sbuf_tensor("m_lgt", [128, 8], F32))
            msk8 = stR.enter_context(nc.sbuf_tensor("m_msk8", [128, 8], F32))
            negbig = stR.enter_context(nc.sbuf_tensor("m_negbig", [128, 1], F32))
            lg2 = stR.enter_context(nc.sbuf_tensor("m_lg2", [128, 8], F32))
            sm = stR.enter_context(nc.sbuf_tensor("m_sm", [128, 6], F32))
            lgcm = stR.enter_context(nc.sbuf_tensor("m_lgcm", [8, TPC], F32))
            def ldxn(sy):
                for j in range(NFC):
                    ph.inc(sy.dma_start(out=xnsb[:, j * TPC:(j + 1) * TPC],
                                        in_=xns[j * 128:(j + 1) * 128, :]), 16)
            SY(ldxn)
            def ldid(sy):
                ph.inc(sy.dma_start(out=idf[:, :], in_=identf_in), 16)
            SY(ldid)
            def ldrt(g):
                src = rout.rearrange("(j p) n -> p j n", p=128)
                dst = routs[:, :].rearrange("p (j n) -> p j n", n=8)
                ph.inc(g.dma_start(out=dst, in_=src), 16)
            GQ(ldrt)
            SY(lambda sy: ph.inc(sy.dma_start(out=esel[:, :], in_=esel_in), 16))
            V(lambda v: ph.inc(v.memset(negbig[:, :], -1e30)))
            for tt in range(NT):
                def mmlg(t, tt=tt):
                    for j in range(NFC):
                        ins = t.matmul(out=pL8[:, :], lhsT=routs[:, j * 8:(j + 1) * 8],
                                       rhs=xnsb[:, j * TPC + tt * TT:j * TPC + (tt + 1) * TT],
                                       start=(j == 0), stop=(j == NFC - 1))
                    ph.inc(ins)
                T(mmlg)
                V(lambda v, tt=tt: ph.inc(v.tensor_copy(out=lgcm[:, tt * TT:(tt + 1) * TT],
                                                  in_=pL8[:, :])))
                for cq in range(TT // 128):
                    def trlg(t, tt=tt, cq=cq):
                        ph.inc(t.transpose(
                            out=pA6[4][:, 0:8],
                            in_=lgcm[:, tt * TT + cq * 128:tt * TT + (cq + 1) * 128],
                            identity=idf[0:8, 0:8]))
                    T(trlg)
                    V(lambda v: ph.inc(v.tensor_copy(out=lgt[:, :], in_=pA6[4][:, 0:8])))
                    V(lambda v: ph.inc(v.reduce_max(out=sm[:, 0:1], in_=lgt[:, :],
                                                    axis=mybir.AxisListType.X)))
                    V(lambda v: ph.inc(v.tensor_scalar(out=msk8[:, :], in0=lgt[:, :],
                                                       scalar1=sm[:, 0:1], scalar2=None,
                                                       op0=ALU.is_ge)))
                    V(lambda v: ph.inc(v.scalar_tensor_tensor(out=lg2[:, :], in0=msk8[:, :],
                                                              scalar=negbig[:, 0:1],
                                                              in1=lgt[:, :], op0=ALU.mult,
                                                              op1=ALU.add)))
                    V(lambda v: ph.inc(v.reduce_max(out=sm[:, 1:2], in_=lg2[:, :],
                                                    axis=mybir.AxisListType.X)))
                    V(lambda v: ph.inc(v.tensor_sub(out=sm[:, 2:3], in0=sm[:, 1:2],
                                                    in1=sm[:, 0:1])))
                    V(lambda v: ph.inc(v.tensor_scalar_mul(out=sm[:, 3:4], in0=sm[:, 0:1],
                                                           scalar1=-1.0)))
                    def top2b(s):
                        s.activation(out=lg2[:, :], in_=lgt[:, :], func=AF.Exp, bias=sm[:, 3:4])
                        ph.inc(s.activation(out=sm[:, 4:5], in_=sm[:, 2:3], func=AF.Exp))
                    SC(top2b)
                    V(lambda v: ph.inc(v.tensor_scalar_add(out=sm[:, 4:5], in0=sm[:, 4:5],
                                                           scalar1=1.0)))
                    V(lambda v: ph.inc(v.reciprocal(out=sm[:, 5:6], in_=sm[:, 4:5])))
                    V(lambda v: ph.inc(v.tensor_scalar(out=lg2[:, :], in0=lg2[:, :],
                                                       scalar1=sm[:, 5:6], scalar2=None,
                                                       op0=ALU.mult)))
                    V(lambda v: ph.inc(v.tensor_scalar(out=lgt[:, :], in0=lgt[:, :],
                                                       scalar1=sm[:, 1:2], scalar2=None,
                                                       op0=ALU.is_ge)))
                    V(lambda v: ph.inc(v.tensor_mul(out=lg2[:, :], in0=lg2[:, :],
                                                    in1=lgt[:, :])))
                    def trw(t):
                        ph.inc(t.transpose(out=pA6[4][0:8, 0:128], in_=lg2[:, :],
                                           identity=idf[:, :]))
                    T(trw)
                    V(lambda v, tt=tt, cq=cq: ph.inc(
                        v.tensor_copy(out=w8cm[:, tt * TT + cq * 128:tt * TT + (cq + 1) * 128],
                                      in_=pA6[4][0:8, 0:128])))

            if debug:
                def dmpw(g):
                    for tt in range(NT):
                        ph.inc(g.dma_start(out=dbg_w8[:, tt * TT:(tt + 1) * TT].bitcast(BF16)[:, 0:TT] if False else dbg_w8[:, tt * TT:(tt + 1) * TT],
                                           in_=w8cm[:, tt * TT:(tt + 1) * TT]), 16)
                    ph.inc(g.dma_start(out=dbg_lg[:, :], in_=lgcm[:, :]), 16)
                GQ(dmpw)
            stR.close()
            hprime = stB.enter_context(nc.sbuf_tensor("m_hprime", [128, NI * TPC], BF16))
            passes = [("e", e) for e in range(E)] + [("s", g_) for g_ in range(4)]
            UPT = NI * NT
            DNT = NFC * NT
            wgp = [(wga0, wgb0), (wga1, wgb1)]
            wdp = [wdb_a, wdb_b]
            for pidx, (kind, ei) in enumerate(passes):
                for icp in range(NI):
                    wl = pidx * NI + icp

                    @block.gpsimd
                    def _(g, kind=kind, ei=ei, icp=icp, wl=wl):
                        if wl == 0:
                            ph.wait(g)
                        if wl >= 2:
                            g.wait_ge(s_t, 4 * (wl - 1))
                        if kind == "e":
                            gsrc = wug[ei, :, icp * 128:(icp + 1) * 128]
                            usrc = wug[ei, :, I + icp * 128:I + (icp + 1) * 128]
                        else:
                            base = ei * NI + icp
                            gsrc = swu[:, base * 128:(base + 1) * 128]
                            usrc = swu[:, 5632 + base * 128:5632 + (base + 1) * 128]
                        wga, wgb = wgp[wl % 2]
                        g.dma_start(out=wga[:, :].rearrange("p (j n) -> p j n", n=128),
                                    in_=gsrc.rearrange("(j p) n -> p j n", p=128)
                                    ).then_inc(s_w, 16)
                        g.dma_start(out=wgb[:, :].rearrange("p (j n) -> p j n", n=128),
                                    in_=usrc.rearrange("(j p) n -> p j n", p=128)
                                    ).then_inc(s_w, 16)
                    for tt in range(NT):
                        k = pidx * UPT + icp * NT + tt
                        q = k % 2

                        @block.tensor
                        def _(t, kind=kind, ei=ei, tt=tt, wl=wl, k=k, q=q, pidx=pidx):
                            if k == 0:
                                ph.wait(t)
                            t.wait_ge(s_w, 32 * (wl + 1))
                            if k >= 2:
                                t.wait_ge(s_v, k - 1)
                            if pidx >= 1:
                                t.wait_ge(s_dv, 64 * pidx)
                            wga, wgb = wgp[wl % 2]
                            for j in range(NFC):
                                t.matmul(out=pA6[2 * q][:, :],
                                         lhsT=wga[:, j * 128:(j + 1) * 128],
                                         rhs=xnsb[:, j * TPC + tt * TT:j * TPC + (tt + 1) * TT],
                                         start=(j == 0), stop=(j == NFC - 1))
                            for j in range(NFC):
                                ins = t.matmul(out=pA6[2 * q + 1][:, :],
                                               lhsT=wgb[:, j * 128:(j + 1) * 128],
                                               rhs=xnsb[:, j * TPC + tt * TT:
                                                        j * TPC + (tt + 1) * TT],
                                               start=(j == 0), stop=(j == NFC - 1))
                            if kind == "e":
                                ins = t.matmul(out=pA6[4 + q][:, :],
                                               lhsT=esel[:, ei * 128:(ei + 1) * 128],
                                               rhs=w8cm[0:8, tt * TT:(tt + 1) * TT],
                                               start=True, stop=True)
                            ins.then_inc(s_t, 1)

                        @block.scalar
                        def _(s, k=k, q=q):
                            if k == 0:
                                ph.wait(s)
                            s.wait_ge(s_t, k + 1)
                            if k >= 2:
                                s.wait_ge(s_v, k - 1)
                            s.activation(out=gt2[q][:, :], in_=pA6[2 * q][:, :],
                                         func=AF.Silu).then_inc(s_sc, 1)

                        @block.vector
                        def _(v, kind=kind, icp=icp, tt=tt, k=k, q=q, pidx=pidx):
                            if k == 0:
                                ph.wait(v)
                            v.wait_ge(s_sc, k + 1)
                            if pidx >= 1:
                                v.wait_ge(s_dt, 64 * pidx)
                            v.tensor_copy(out=util[:, :], in_=pA6[2 * q + 1][:, :])
                            if kind == "e":
                                v.tensor_copy(out=wbt[:, :], in_=pA6[4 + q][:, :])
                            v.engine_nop()
                            dst = hprime[:, icp * TPC + tt * TT:icp * TPC + (tt + 1) * TT]
                            if kind == "e":
                                v.tensor_mul(out=gt2[q][:, :], in0=gt2[q][:, :],
                                             in1=util[:, :])
                                v.tensor_mul(out=dst, in0=gt2[q][:, :],
                                             in1=wbt[:, :]).then_inc(s_v, 1)
                            else:
                                v.tensor_mul(out=dst, in0=gt2[q][:, :],
                                             in1=util[:, :]).then_inc(s_v, 1)
                for jo in range(NFC):
                    dl = pidx * NFC + jo

                    @block.gpsimd
                    def _(g, kind=kind, ei=ei, jo=jo, dl=dl):
                        if dl >= 2:
                            g.wait_ge(s_dt, 4 * (dl - 1))
                        if kind == "e":
                            src = wdn[ei, :, jo * 128:(jo + 1) * 128]
                        else:
                            src = swd[ei * I:(ei + 1) * I, jo * 128:(jo + 1) * 128]
                        wdb = wdp[dl % 2]
                        g.dma_start(out=wdb[:, :].rearrange("p (j n) -> p j n", n=128),
                                    in_=src.rearrange("(j p) n -> p j n", p=128)
                                    ).then_inc(s_wd, 16)
                    for tt in range(NT):
                        m = pidx * DNT + jo * NT + tt

                        @block.tensor
                        def _(t, tt=tt, dl=dl, m=m, pidx=pidx):
                            t.wait_ge(s_wd, 16 * (dl + 1))
                            if m >= 4:
                                t.wait_ge(s_dv, m - 3)
                            t.wait_ge(s_v, UPT * (pidx + 1))
                            t.wait_ge(s_sc, UPT * (pidx + 1))
                            wdb = wdp[dl % 2]
                            for ic in range(NI):
                                ins = t.matmul(out=pA6[m % 4][:, :],
                                               lhsT=wdb[:, ic * 128:(ic + 1) * 128],
                                               rhs=hprime[:, ic * TPC + tt * TT:
                                                          ic * TPC + (tt + 1) * TT],
                                               start=(ic == 0), stop=(ic == NI - 1))
                            ins.then_inc(s_dt, 1)

                        @block.vector
                        def _(v, jo=jo, tt=tt, m=m):
                            v.wait_ge(s_dt, m + 1)
                            v.tensor_copy(out=util[:, :], in_=pA6[m % 4][:, :])
                            v.engine_nop()
                            dst = y_sb[:, jo * TPC + tt * TT:jo * TPC + (tt + 1) * TT]
                            v.tensor_add(out=dst, in0=dst, in1=util[:, :]).then_inc(s_dv, 1)

            @block.vector
            def _(v):
                ph.inc(v.engine_nop())

            if debug:
                def dmph(g):
                    ph.inc(g.dma_start(out=dbg_h[:, :], in_=hprime[:, 0:TPC]), 16)
                GQ(dmph)
        if debug:
            def dmpy(g):
                ph.inc(g.dma_start(out=dbg_y[:, :], in_=y_sb[:, 0:TPC]), 16)
            GQ(dmpy)
        with nc.sbuf_tensor("m_x1b", [128, NFC * TT], F32) as x1b, \
             nc.sbuf_tensor("m_sqf", [128, TT], F32) as sqf, \
             nc.sbuf_tensor("m_statm", [1, 6 * TT], F32) as statm, \
             nc.sbuf_tensor("m_xn1b", [128, TT], BF16) as xn1b, \
             nc.sbuf_tensor("m_f32t", [128, TT], F32) as f32t, \
             nc.sbuf_tensor("m_atb", [128, TT], BF16) as atb:
            for tt in range(NT):
                def ldx(sy, tt=tt):
                    for j in range(NFC):
                        ph.inc(sy.dma_start(out=x1b[:, j * TT:(j + 1) * TT],
                                            in_=xs[j * 128:(j + 1) * 128,
                                                   tt * TT:(tt + 1) * TT]), 16)
                SY(ldx)
                for j in range(NFC):
                    def ldat(sy, tt=tt, j=j):
                        ph.inc(sy.dma_start(out=atb[:, :],
                                            in_=attn_sl[j * 128:(j + 1) * 128,
                                                        tt * TT:(tt + 1) * TT]), 16)
                    SY(ldat)
                    V(lambda v, j=j: ph.inc(v.tensor_copy(out=f32t[:, :], in_=atb[:, :])))
                    V(lambda v, j=j: ph.inc(v.tensor_add(out=x1b[:, j * TT:(j + 1) * TT],
                                                         in0=x1b[:, j * TT:(j + 1) * TT],
                                                         in1=f32t[:, :])))
                    V(lambda v, tt=tt, j=j: ph.inc(
                        v.tensor_copy(out=f32t[:, :],
                                      in_=y_sb[:, j * TPC + tt * TT:j * TPC + (tt + 1) * TT])))
                    V(lambda v, j=j: ph.inc(v.tensor_add(out=x1b[:, j * TT:(j + 1) * TT],
                                                         in0=x1b[:, j * TT:(j + 1) * TT],
                                                         in1=f32t[:, :])))
                def stx1(g, tt=tt):
                    for j in range(NFC):
                        ph.inc(g.dma_start(out=x1o[j * 128:(j + 1) * 128,
                                                   tt * TT:(tt + 1) * TT],
                                           in_=x1b[:, j * TT:(j + 1) * TT]), 16)
                GQ(stx1)
                def mmsum(t, tt=tt):
                    for j in range(NFC):
                        ins = t.matmul(out=pA6[0][0:1, :], lhsT=onecol[:, 0:1],
                                       rhs=x1b[:, j * TT:(j + 1) * TT],
                                       start=(j == 0), stop=(j == NFC - 1))
                    ph.inc(ins)
                T(mmsum)
                for j in range(NFC):
                    SC(lambda s, j=j: ph.inc(s.activation(out=sqf[:, :],
                                                          in_=x1b[:, j * TT:(j + 1) * TT],
                                                          func=AF.Square)))
                    def mmsq(t, j=j):
                        ph.inc(t.matmul(out=pA6[1][0:1, :], lhsT=onecol[:, 0:1], rhs=sqf[:, :],
                                        start=(j == 0), stop=(j == NFC - 1)))
                    T(mmsq)
                def stats(v):
                    v.tensor_scalar_mul(out=statm[:, 0 * TT:1 * TT], in0=pA6[0][0:1, :], scalar1=1.0 / D)
                    v.tensor_scalar_mul(out=statm[:, 1 * TT:2 * TT], in0=pA6[1][0:1, :], scalar1=1.0 / D)
                    v.tensor_mul(out=statm[:, 2 * TT:3 * TT], in0=statm[:, 0 * TT:1 * TT],
                                 in1=statm[:, 0 * TT:1 * TT])
                    ph.inc(v.tensor_sub(out=statm[:, 3 * TT:4 * TT], in0=statm[:, 1 * TT:2 * TT],
                                        in1=statm[:, 2 * TT:3 * TT]))
                V(stats)
                SC(lambda s: ph.inc(s.activation(out=statm[:, 4 * TT:5 * TT],
                                                 in_=statm[:, 3 * TT:4 * TT],
                                                 func=AF.Sqrt, bias=epsc[0:1, 0:1])))
                V(lambda v: ph.inc(v.reciprocal(out=statm[:, 5 * TT:6 * TT],
                                                in_=statm[:, 4 * TT:5 * TT])))
                def bcst(t):
                    t.matmul(out=pA6[2][:, :], lhsT=onesc[:, :], rhs=statm[:, 0 * TT:1 * TT],
                             start=True, stop=True)
                    ph.inc(t.matmul(out=pA6[3][:, :], lhsT=onesc[:, :],
                                    rhs=statm[:, 5 * TT:6 * TT], start=True, stop=True))
                T(bcst)
                for j in range(NFC):
                    def norm(v, j=j):
                        v.tensor_sub(out=sqf[:, :], in0=x1b[:, j * TT:(j + 1) * TT],
                                     in1=pA6[2][:, :])
                        ph.inc(v.tensor_mul(out=xn1b[:, :], in0=sqf[:, :], in1=pA6[3][:, :]))
                    V(norm)
                    def stxn(g, tt=tt, j=j):
                        ph.inc(g.dma_start(out=xn1o[j * 128:(j + 1) * 128,
                                                    tt * TT:(tt + 1) * TT],
                                           in_=xn1b[:, :]), 16)
                    GQ(stxn)

        @block.gpsimd
        def _(g):
            ph.wait(g)


def emit_l1attn_f(C, nc, xn1_g, pattn2):
    core = 0
    S = 4096
    NB = 4
    NQT = S // 128
    wq = nc.dram_tensor("wq1", [D, 256], F32, kind="ExternalInput").ap()
    wk = nc.dram_tensor("wk1", [D, 64], F32, kind="ExternalInput").ap()
    wv = nc.dram_tensor("wv1", [D, 64], F32, kind="ExternalInput").ap()
    CST = l1attn_consts(core)
    cin = {}
    for k, v in CST.items():
        dt = BF16 if v.dtype == BF16NP else F32
        cin[k] = nc.dram_tensor(k, list(v.shape), dt, kind="ExternalInput").ap()
    wop = nc.dram_tensor("wop", [256, D], F32, kind="ExternalInput").ap()
    qtm_d = nc.dram_tensor("l_qtm_d", [S, 384], BF16, kind="Internal").ap()

    with ExitStack() as st:
        def sbuf(n, sh, dt=F32):
            return st.enter_context(nc.sbuf_tensor("l_" + n, sh, dt))
        ph = C.ph
        block = C.block

        def T(fn):
            @block.tensor
            def _(t):
                ph.wait(t)
                fn(t)
        def V(fn):
            @block.vector
            def _(v):
                ph.wait(v)
                fn(v)
        def SC(fn):
            @block.scalar
            def _(s):
                ph.wait(s)
                fn(s)
        def SY(fn):
            @block.sync
            def _(sy):
                ph.wait(sy)
                fn(sy)
        def GQ(fn):
            @block.gpsimd
            def _(g):
                ph.wait(g)
                fn(g)

        wqs = sbuf("wqs", [128, NFC * 256], BF16)
        wkv = sbuf("wkv", [128, NFC * 128], BF16)
        cs = {}
        for k, v in CST.items():
            dt = BF16 if v.dtype == BF16NP else F32
            cs[k] = sbuf("c3_" + k, list(v.shape), dt)
        xnt = sbuf("xnt", [128, NFC * 128], BF16)
        qtmb = sbuf("qtmb", [128, 256], BF16)
        ktmb = sbuf("ktmb", [128, 64], BF16)
        rt1 = sbuf("rt1", [128, 128], BF16)
        rt2 = sbuf("rt2", [128, 128], BF16)
        vaug = sbuf("vaug", [128, NQT * 65], BF16)
        q4cm = sbuf("q4cm", [64, 4 * S], BF16)
        qx2 = sbuf("qx2", [128, S], BF16)
        kcmx = sbuf("kcmx", [128, S], BF16)
        st1 = sbuf("st1", [128, 512], F32)
        ptile = sbuf("ptile", [128, 512], BF16)
        neg30 = sbuf("neg30", [128, 1], F32)
        ctxb = sbuf("ctxb", [65, 512], BF16)
        rnb = sbuf("rnb", [64, 512], BF16)
        rsum = sbuf("rsum", [1, 512], BF16)
        ones64 = sbuf("ones64", [1, 64], BF16)
        ctxf = sbuf("ctxf", [64, 512], BF16)
        ctx2g = sbuf("ctx2g", [128, 2 * 512], BF16)
        wos2 = sbuf("wos2", [128, 2 * D], BF16)
        pat2 = sbuf("pat2", [128, 512], BF16)
        pPt = st.enter_context(nc.psum_tensor("l_pPt", [128, 512], F32))
        pP = st.enter_context(nc.psum_tensor("l_pP", [128, 256], F32))
        pKV = st.enter_context(nc.psum_tensor("l_pKV", [128, 128], F32))
        pS2 = [st.enter_context(nc.psum_tensor(f"l_pSc{i}", [128, 512], F32))
               for i in range(2)]
        pC2 = [st.enter_context(nc.psum_tensor(f"l_pCtx{i}", [65, 512], F32))
               for i in range(2)]
        pBc = st.enter_context(nc.psum_tensor("l_pBc", [64, 512], F32))
        pt2 = [sbuf(f"ptile{i}", [128, 512], BF16) for i in range(2)]
        pat2p = [sbuf(f"pat2_{i}", [128, 512], BF16) for i in range(2)]
        a_t = st.enter_context(nc.semaphore("l_at"))
        a_sc = st.enter_context(nc.semaphore("l_asc"))
        a_v1 = st.enter_context(nc.semaphore("l_av1"))
        a_pv = st.enter_context(nc.semaphore("l_apv"))
        a_cv = st.enter_context(nc.semaphore("l_acv"))
        a_nv = st.enter_context(nc.semaphore("l_anv"))
        a_nt = st.enter_context(nc.semaphore("l_ant"))
        a_av = st.enter_context(nc.semaphore("l_aav"))
        a_pt = st.enter_context(nc.semaphore("l_apt"))
        a_pc = st.enter_context(nc.semaphore("l_apc"))
        a_ps = st.enter_context(nc.semaphore("l_aps"))

        def ld0(g):
            srcw = wop.rearrange("(j p) n -> p j n", p=128)
            dstw = wos2[:, :].rearrange("p (j n) -> p j n", n=D)
            ph.inc(g.dma_start(out=dstw, in_=srcw), 16)
            src = wq.rearrange("(j p) n -> p j n", p=128)
            ph.inc(g.dma_start(out=wqs[:, :].rearrange("p (j n) -> p j n", n=256), in_=src), 16)
            ph.inc(g.dma_start(out=wkv[:, :].rearrange("p (j n) -> p j n", n=128)[:, :, 0:64],
                               in_=wk.rearrange("(j p) n -> p j n", p=128)), 16)
            ph.inc(g.dma_start(out=wkv[:, :].rearrange("p (j n) -> p j n", n=128)[:, :, 64:128],
                               in_=wv.rearrange("(j p) n -> p j n", p=128)), 16)
        GQ(ld0)
        def ld1(sy):
            for k in CST:
                ph.inc(sy.dma_start(out=cs[k][:, :], in_=cin[k]), 16)
        SY(ld1)
        def st0(v):
            v.memset(neg30[:, :], -30.0)
            ph.inc(v.memset(ones64[:, :], 1.0))
        V(st0)

        PIC = [0]
        PJC = [0]
        PBASE = [bb * (NQT * (NQT + 1) // 2) for bb in range(NB)]
        for b in range(NB):
            if b >= 1:
                @block.gpsimd
                def _(g, b=b):
                    g.wait_ge(a_t, PBASE[b])
                    g.wait_ge(a_av, NQT * b)
                    ph.inc(g.engine_nop())
            for qt in range(NQT):
                tok0 = b * S + qt * 128
                def ldxn(sy, tok0=tok0):
                    rk = tok0 // TPC
                    cb = tok0 % TPC
                    for j in range(NFC):
                        ph.inc(sy.dma_start(out=xnt[:, j * 128:(j + 1) * 128],
                                            in_=xn1_g[rk * D + j * 128:rk * D + (j + 1) * 128,
                                                      cb:cb + 128]), 16)
                SY(ldxn)
                def mmq(t):
                    for j in range(NFC):
                        t.matmul(out=pP[:, :], lhsT=xnt[:, j * 128:(j + 1) * 128],
                                 rhs=wqs[:, j * 256:(j + 1) * 256],
                                 start=(j == 0), stop=(j == NFC - 1))
                    for j in range(NFC):
                        ins = t.matmul(out=pKV[:, :], lhsT=xnt[:, j * 128:(j + 1) * 128],
                                       rhs=wkv[:, j * 128:(j + 1) * 128],
                                       start=(j == 0), stop=(j == NFC - 1))
                    ph.inc(ins)
                T(mmq)
                def ropeq(v, qt=qt):
                    _lp = st.enter_context(nc.allow_low_precision(
                        reason="bf16 rope elementwise, blessed tolerance")) if False else None
                    cosb = cs["cos_t"][:, qt * 32:(qt + 1) * 32]
                    sinb = cs["sin_t"][:, qt * 32:(qt + 1) * 32]
                    q3 = pP[:, :].rearrange("p (h d) -> p h d", h=4)
                    o3 = qtmb[:, :].rearrange("p (h d) -> p h d", h=4)
                    c3 = cosb.unsqueeze(1).broadcast_to([128, 4, 32])
                    s3 = sinb.unsqueeze(1).broadcast_to([128, 4, 32])
                    v.tensor_mul(out=o3[:, :, 0::2], in0=q3[:, :, 1::2], in1=s3)
                    v.engine_nop()
                    v.tensor_mul(out=rt1[:, :].rearrange("p (h d) -> p h d", h=4)[:, :, 0:32],
                                 in0=q3[:, :, 0::2], in1=c3)
                    v.engine_nop()
                    v.tensor_sub(out=o3[:, :, 0::2],
                                 in0=rt1[:, :].rearrange("p (h d) -> p h d", h=4)[:, :, 0:32],
                                 in1=o3[:, :, 0::2])
                    v.tensor_mul(out=o3[:, :, 1::2], in0=q3[:, :, 0::2], in1=s3)
                    v.engine_nop()
                    v.tensor_mul(out=rt1[:, :].rearrange("p (h d) -> p h d", h=4)[:, :, 0:32],
                                 in0=q3[:, :, 1::2], in1=c3)
                    v.engine_nop()
                    ph.inc(v.tensor_add(out=o3[:, :, 1::2],
                                        in0=rt1[:, :].rearrange("p (h d) -> p h d", h=4)[:, :, 0:32],
                                        in1=o3[:, :, 1::2]))
                V(ropeq)
                def ropek(v, qt=qt):
                    cosb = cs["cos_t"][:, qt * 32:(qt + 1) * 32]
                    sinb = cs["sin_t"][:, qt * 32:(qt + 1) * 32]
                    k2 = pKV[:, 0:64]
                    o2 = ktmb[:, :]
                    v.tensor_mul(out=o2[:, 0::2], in0=k2[:, 1::2], in1=sinb)
                    v.engine_nop()
                    v.tensor_mul(out=rt2[:, 0:32], in0=k2[:, 0::2], in1=cosb)
                    v.engine_nop()
                    v.tensor_sub(out=o2[:, 0::2], in0=rt2[:, 0:32], in1=o2[:, 0::2])
                    v.tensor_mul(out=o2[:, 1::2], in0=k2[:, 0::2], in1=sinb)
                    v.engine_nop()
                    v.tensor_mul(out=rt2[:, 0:32], in0=k2[:, 1::2], in1=cosb)
                    v.engine_nop()
                    v.tensor_add(out=o2[:, 1::2], in0=rt2[:, 0:32], in1=o2[:, 1::2])
                    v.tensor_copy(out=vaug[:, qt * 65:qt * 65 + 64], in_=pKV[:, 64:128])
                    ph.inc(v.memset(vaug[:, qt * 65 + 64:qt * 65 + 65], 1.0))
                V(ropek)
                def stqk(g, qt=qt):
                    ph.inc(g.dma_start(out=qtm_d[qt * 128:(qt + 1) * 128, 0:256],
                                       in_=qtmb[:, :]), 16)
                    ph.inc(g.dma_start(out=qtm_d[qt * 128:(qt + 1) * 128, 256:320],
                                       in_=ktmb[:, :]), 16)
                GQ(stqk)
            for hp in range(2):
                def trq(sy, hp=hp):
                    ph.inc(sy.dma_start_transpose(out=qx2[:, :],
                                                  in_=qtm_d[:, hp * 128:(hp + 1) * 128]), 16)
                SY(trq)
                def spl(v, hp=hp):
                    v.tensor_copy(out=q4cm[:, (2 * hp) * S:(2 * hp + 1) * S], in_=qx2[0:64, :])
                    ph.inc(v.tensor_copy(out=q4cm[:, (2 * hp + 1) * S:(2 * hp + 2) * S],
                                         in_=qx2[64:128, :]))
                V(spl)
            def trk(sy):
                ph.inc(sy.dma_start_transpose(out=kcmx[:, :],
                                              in_=qtm_d[:, 256:384]), 16)
            SY(trk)
            pending_bcn = []

            def emit_bcn(g2):
                @block.tensor
                def _(t, g2=g2):
                    t.wait_ge(a_nv, g2 + 1)
                    if g2 >= 1:
                        t.wait_ge(a_av, g2)
                    t.matmul(out=pBc[:, :], lhsT=ones64[:, :],
                             rhs=rsum[:, :], start=True,
                             stop=True).then_inc(a_nt, 1)

            def emit_mmsc(qt, kt, pi):
                @block.tensor
                def _(t, qt=qt, kt=kt, pi=pi):
                    if pi == PBASE[b]:
                        ph.wait(t)
                    if pi >= 2:
                        t.wait_ge(a_sc, pi - 1)
                    q3 = q4cm[:, :].rearrange("p (h t) -> p h t", h=4)
                    ps3 = pS2[pi % 2][:, :].rearrange("p (h t) -> p h t", h=4)
                    t.matmul(out=ps3[:, :, :], lhsT=kcmx[0:64, kt * 128:(kt + 1) * 128],
                             rhs=q3[:, :, qt * 128:(qt + 1) * 128], start=True, stop=False)
                    qa3 = cs["qaug"][:, :].rearrange("p (h t) -> p h t", h=4)
                    t.matmul(out=ps3[:, :, :],
                             lhsT=cs["kpos"][:, kt * 128:(kt + 1) * 128],
                             rhs=qa3[:, :, qt * 128:(qt + 1) * 128],
                             start=False, stop=True,
                             skip_group_check=True).then_inc(a_t, 1)

            def emit_capexp(pi):
                @block.scalar
                def _(s, pi=pi):
                    if pi == PBASE[b]:
                        ph.wait(s)
                    s.wait_ge(a_t, pi + 1)
                    if pi >= 2:
                        s.wait_ge(a_pv, pi - 1)
                    s.activation(out=st1[:, :], in_=pS2[pi % 2][:, :], func=AF.Tanh,
                                 scale=1.0 / 240.0)
                    s.activation(out=pt2[pi % 2][:, :], in_=st1[:, :], func=AF.Exp,
                                 scale=30.0, bias=neg30[:, 0:1]).then_inc(a_sc, 1)

            def emit_dmask(pi, gq):
                @block.vector
                def _(v, pi=pi, gq=gq):
                    if gq == 0:
                        ph.wait(v)
                    v.wait_ge(a_sc, pi + 1)
                    p3 = pt2[pi % 2][:, :].rearrange("p (h t) -> p h t", h=4)
                    m3 = cs["causal"][:, :].unsqueeze(1).broadcast_to([128, 4, 128])
                    v.tensor_tensor(out=p3, in0=p3, in1=m3,
                                    op=ALU.mult).then_inc(a_v1, 1)

            def emit_mmpv(qt, kt, pi, gq):
                @block.tensor
                def _(t, qt=qt, kt=kt, pi=pi, gq=gq):
                    t.wait_ge(a_sc, pi + 1)
                    if kt == qt:
                        t.wait_ge(a_v1, gq + 1)
                    if kt == 0 and gq >= 2:
                        t.wait_ge(a_cv, gq - 1)
                    t.matmul(out=pC2[gq % 2][:, :], lhsT=vaug[:, kt * 65:(kt + 1) * 65],
                             rhs=pt2[pi % 2][:, :], start=(kt == 0), stop=(kt == qt),
                             skip_group_check=True).then_inc(a_pv, 1)

            for qt in range(NQT):
                gq = b * NQT + qt
                pi_base = PIC[0]
                prev = None
                for kt in range(qt + 1):
                    pi = pi_base + kt
                    emit_mmsc(qt, kt, pi)
                    if kt == min(3, qt) and pending_bcn:
                        for g2 in pending_bcn:
                            emit_bcn(g2)
                        pending_bcn = []
                    emit_capexp(pi)
                    if kt == qt:
                        emit_dmask(pi, gq)
                    if prev is not None:
                        emit_mmpv(qt, prev, pi_base + prev, gq)
                    prev = kt
                emit_mmpv(qt, prev, pi_base + prev, gq)
                PIC[0] = pi_base + qt + 1

                @block.vector
                def _(v, gq=gq):
                    v.wait_ge(a_pv, PIC[0])
                    v.tensor_copy(out=ctxb[:, :],
                                  in_=pC2[gq % 2][:, :]).then_inc(a_cv, 1)

                @block.vector
                def _(v):
                    v.tensor_copy(out=rsum[:, :], in_=ctxb[64:65, :])

                @block.vector
                def _(v):
                    v.reciprocal(out=rsum[:, :], in_=rsum[:, :]).then_inc(a_nv, 1)

                pending_bcn.append(gq)
                if qt % 4 == 3 or qt == NQT - 1:
                    for g2 in pending_bcn:
                        emit_bcn(g2)
                    pending_bcn = []

                @block.vector
                def _(v, qt=qt, gq=gq):
                    v.wait_ge(a_nt, gq + 1)
                    if PJC[0] > 0:
                        v.wait_ge(a_pt, PJC[0] * NFC)
                    v.tensor_copy(out=rnb[:, :], in_=pBc[:, :])
                    v.engine_nop()
                    v.tensor_mul(out=ctxf[:, :], in0=ctxb[0:64, :], in1=rnb[:, :])
                    v.engine_nop()
                    q4 = qt % 4
                    ins = None
                    for hp in range(2):
                        v.tensor_copy(out=ctx2g[0:64, hp * 512 + q4 * 128:
                                                hp * 512 + (q4 + 1) * 128],
                                      in_=ctxf[:, (2 * hp) * 128:(2 * hp + 1) * 128])
                        ins = v.tensor_copy(out=ctx2g[64:128, hp * 512 + q4 * 128:
                                                      hp * 512 + (q4 + 1) * 128],
                                            in_=ctxf[:, (2 * hp + 1) * 128:(2 * hp + 2) * 128])
                    ins.then_inc(a_av, 1)

                if qt % 4 == 3:
                    tok0 = b * S + (qt - 3) * 128
                    dstb = tok0 // TPC
                    cb0 = tok0 % TPC
                    pc_base = PJC[0] * NFC
                    for jo in range(NFC):
                        @block.tensor
                        def _(t, jo=jo, gq=gq, pc_base=pc_base):
                            if jo == 0:
                                t.wait_ge(a_av, gq + 1)
                            if pc_base + jo >= 1:
                                t.wait_ge(a_pc, pc_base + jo)
                            t.matmul(out=pPt[:, :],
                                     lhsT=wos2[:, 0 * D + jo * 128:0 * D + jo * 128 + 128],
                                     rhs=ctx2g[:, 0:512], start=True, stop=False)
                            t.matmul(out=pPt[:, :],
                                     lhsT=wos2[:, 1 * D + jo * 128:1 * D + jo * 128 + 128],
                                     rhs=ctx2g[:, 512:1024], start=False,
                                     stop=True).then_inc(a_pt, 1)

                        @block.vector
                        def _(v, jo=jo, pc_base=pc_base):
                            v.wait_ge(a_pt, pc_base + jo + 1)
                            if pc_base + jo >= 2:
                                v.wait_ge(a_ps, 16 * (pc_base + jo - 1))
                            v.tensor_copy(out=pat2p[jo % 2][:, :],
                                          in_=pPt[:, :]).then_inc(a_pc, 1)

                        @block.gpsimd
                        def _(g, jo=jo, dstb=dstb, cb0=cb0, pc_base=pc_base):
                            g.wait_ge(a_pc, pc_base + jo + 1)
                            g.dma_start(
                                out=pattn2[dstb * D + jo * 128:dstb * D + (jo + 1) * 128,
                                           cb0:cb0 + 512],
                                in_=pat2p[jo % 2][:, :]).then_inc(a_ps, 16)
                    PJC[0] += 1

        @block.gpsimd
        def _(g):
            g.wait_ge(a_ps, 16 * NFC * (NQT // 4) * NB)
            g.wait_ge(a_av, NQT * NB)
            ph.inc(g.engine_nop())

        @block.gpsimd
        def _(g):
            ph.wait(g)


def emit_fin_f(C, nc, attn2_sl, xn1s, x1s, x2o):
    TT = 512
    NT = TPC // TT
    NIC = 8192 // 128
    bo = nc.dram_tensor("bo", [128, NFC], F32, kind="ExternalInput").ap()
    wg8 = nc.dram_tensor("wg8", [D, 8192], I8, kind="ExternalInput").ap()
    wu8 = nc.dram_tensor("wu8", [D, 8192], I8, kind="ExternalInput").ap()
    wd8 = nc.dram_tensor("wd8", [8192, D], I8, kind="ExternalInput").ap()
    sg = nc.dram_tensor("sg", [128, NIC], F32, kind="ExternalInput").ap()
    su = nc.dram_tensor("su", [128, NIC], F32, kind="ExternalInput").ap()
    sd = nc.dram_tensor("sd", [128, NFC], F32, kind="ExternalInput").ap()

    with ExitStack() as st:
        def sbuf(n, sh, dt=F32):
            return st.enter_context(nc.sbuf_tensor("n_" + n, sh, dt))
        ph = C.ph
        block = C.block
        def T(fn):
            @block.tensor
            def _(t):
                ph.wait(t)
                fn(t)
        def V(fn):
            @block.vector
            def _(v):
                ph.wait(v)
                fn(v)
        def SC(fn):
            @block.scalar
            def _(s):
                ph.wait(s)
                fn(s)
        def SY(fn):
            @block.sync
            def _(sy):
                ph.wait(sy)
                fn(sy)
        def GQ(fn):
            @block.gpsimd
            def _(g):
                ph.wait(g)
                fn(g)

        sgs = sbuf("sgs", [128, NIC], F32)
        sus = sbuf("sus", [128, NIC], F32)
        sds = sbuf("sds", [128, NFC], F32)
        bos = sbuf("bos", [128, NFC], F32)
        f32t = sbuf("f32t", [128, TT], F32)
        f32u = sbuf("f32u", [128, TT], F32)
        gt2f = [sbuf(f"gt{i}", [128, TT], BF16) for i in range(2)]
        at2p = [sbuf(f"at2b{i}", [128, TT], BF16) for i in range(2)]
        x2p = [sbuf(f"x2b{i}", [128, TT], F32) for i in range(2)]
        ut = sbuf("ut", [128, TT], BF16)
        pF6 = [st.enter_context(nc.psum_tensor(f"n_pF{i}", [128, TT], F32))
               for i in range(6)]
        f_sw = st.enter_context(nc.semaphore("n_sw"))
        f_st = st.enter_context(nc.semaphore("n_st"))
        f_ssc = st.enter_context(nc.semaphore("n_ssc"))
        f_sv = st.enter_context(nc.semaphore("n_sv"))
        f_swd = st.enter_context(nc.semaphore("n_swd"))
        f_sdt = st.enter_context(nc.semaphore("n_sdt"))
        f_sdv = st.enter_context(nc.semaphore("n_sdv"))
        f_slx = st.enter_context(nc.semaphore("n_slx"))
        f_sst = st.enter_context(nc.semaphore("n_sst"))

        def lds(sy):
            for src, dst in ((sg, sgs), (su, sus), (sd, sds), (bo, bos)):
                ph.inc(sy.dma_start(out=dst[:, :], in_=src), 16)
        SY(lds)

        with nc.sbuf_tensor("n_xn1sb", [128, NFC * TPC], BF16) as xn1sb, \
             nc.sbuf_tensor("n_hst", [128, NIC * TT], BF16) as hst, \
             nc.sbuf_tensor("n_wgc0", [128, NFC * 128], BF16) as wgc0, \
             nc.sbuf_tensor("n_wuc0", [128, NFC * 128], BF16) as wuc0, \
             nc.sbuf_tensor("n_wgc1", [128, NFC * 128], BF16) as wgc1, \
             nc.sbuf_tensor("n_wuc1", [128, NFC * 128], BF16) as wuc1, \
             nc.sbuf_tensor("n_wdc0", [128, NIC * 128], BF16) as wdc0, \
             nc.sbuf_tensor("n_wdc1", [128, NIC * 128], BF16) as wdc1:
            def ldxn(sy):
                for j in range(NFC):
                    ph.inc(sy.dma_start(out=xn1sb[:, j * TPC:(j + 1) * TPC],
                                        in_=xn1s[j * 128:(j + 1) * 128, :]), 16)
            SY(ldxn)
            UPT, DNT = NIC, NFC
            wgup = [(wgc0, wuc0), (wgc1, wuc1)]
            wdnp = [wdc0, wdc1]
            for p in range(NT):
                for ic in range(NIC):
                    wl = p * UPT + ic

                    @block.gpsimd
                    def _(g, ic=ic, wl=wl):
                        if wl == 0:
                            ph.wait(g)
                        if wl >= 2:
                            g.wait_ge(f_st, wl - 1)
                        wgc, wuc = wgup[wl % 2]
                        g.dma_start(out=wgc[:, :].rearrange("p (j n) -> p j n", n=128),
                                    in_=wg8[:, ic * 128:(ic + 1) * 128]
                                    .rearrange("(j p) n -> p j n", p=128)).then_inc(f_sw, 16)
                        g.dma_start(out=wuc[:, :].rearrange("p (j n) -> p j n", n=128),
                                    in_=wu8[:, ic * 128:(ic + 1) * 128]
                                    .rearrange("(j p) n -> p j n", p=128)).then_inc(f_sw, 16)
                    k = wl
                    q = k % 2

                    @block.tensor
                    def _(t, tt=p, wl=wl, k=k, q=q, p=p):
                        if k == 0:
                            ph.wait(t)
                        t.wait_ge(f_sw, 32 * (wl + 1))
                        if k >= 2:
                            t.wait_ge(f_sv, k - 1)
                        if p >= 1:
                            t.wait_ge(f_sdv, DNT * p)
                        wgc, wuc = wgup[wl % 2]
                        for j in range(NFC):
                            t.matmul(out=pF6[2 * q][:, :], lhsT=wgc[:, j * 128:(j + 1) * 128],
                                     rhs=xn1sb[:, j * TPC + tt * TT:j * TPC + (tt + 1) * TT],
                                     start=(j == 0), stop=(j == NFC - 1))
                        for j in range(NFC):
                            ins = t.matmul(out=pF6[2 * q + 1][:, :],
                                           lhsT=wuc[:, j * 128:(j + 1) * 128],
                                           rhs=xn1sb[:, j * TPC + tt * TT:
                                                    j * TPC + (tt + 1) * TT],
                                           start=(j == 0), stop=(j == NFC - 1))
                        ins.then_inc(f_st, 1)

                    @block.scalar
                    def _(s, ic=ic, k=k, q=q):
                        if k == 0:
                            ph.wait(s)
                        s.wait_ge(f_st, k + 1)
                        if k >= 2:
                            s.wait_ge(f_sv, k - 1)
                        s.activation(out=gt2f[q][:, :], in_=pF6[2 * q][:, :],
                                     func=AF.Silu,
                                     scale=sgs[:, ic:ic + 1]).then_inc(f_ssc, 1)

                    @block.vector
                    def _(v, ic=ic, k=k, q=q, p=p):
                        if k == 0:
                            ph.wait(v)
                        v.wait_ge(f_ssc, k + 1)
                        if p >= 1:
                            v.wait_ge(f_sdt, DNT * p)
                        v.tensor_scalar_mul(out=ut[:, :], in0=pF6[2 * q + 1][:, :],
                                            scalar1=sus[:, ic:ic + 1])
                        v.engine_nop()
                        v.tensor_mul(out=hst[:, ic * TT:(ic + 1) * TT], in0=gt2f[q][:, :],
                                     in1=ut[:, :]).then_inc(f_sv, 1)
                for jo in range(NFC):
                    m = p * DNT + jo

                    @block.gpsimd
                    def _(g, jo=jo, m=m):
                        if m >= 2:
                            g.wait_ge(f_sdt, m - 1)
                        wdc = wdnp[m % 2]
                        g.dma_start(out=wdc[:, :].rearrange("p (j n) -> p j n", n=128),
                                    in_=wd8[:, jo * 128:(jo + 1) * 128]
                                    .rearrange("(j p) n -> p j n", p=128)).then_inc(f_swd, 16)

                    @block.sync
                    def _(sy, tt=p, jo=jo, m=m):
                        if m == 0:
                            ph.wait(sy)
                        if m >= 2:
                            sy.wait_ge(f_sst, 16 * (m - 1))
                        sy.dma_start(out=x2p[m % 2][:, :],
                                     in_=x1s[jo * 128:(jo + 1) * 128,
                                             tt * TT:(tt + 1) * TT]).then_inc(f_slx, 16)
                        sy.dma_start(out=at2p[m % 2][:, :],
                                     in_=attn2_sl[jo * 128:(jo + 1) * 128,
                                                  tt * TT:(tt + 1) * TT]).then_inc(f_slx, 16)

                    @block.tensor
                    def _(t, m=m, p=p):
                        t.wait_ge(f_swd, 16 * (m + 1))
                        if m >= 4:
                            t.wait_ge(f_sdv, m - 3)
                        t.wait_ge(f_sv, UPT * (p + 1))
                        wdc = wdnp[m % 2]
                        for ic in range(NIC):
                            ins = t.matmul(out=pF6[m % 4][:, :],
                                           lhsT=wdc[:, ic * 128:(ic + 1) * 128],
                                           rhs=hst[:, ic * TT:(ic + 1) * TT],
                                           start=(ic == 0), stop=(ic == NIC - 1))
                        ins.then_inc(f_sdt, 1)

                    @block.vector
                    def _(v, jo=jo, m=m):
                        v.wait_ge(f_sdt, m + 1)
                        v.wait_ge(f_slx, 32 * (m + 1))
                        x2b = x2p[m % 2]
                        v.tensor_scalar_mul(out=f32t[:, :], in0=pF6[m % 4][:, :],
                                            scalar1=sds[:, jo:jo + 1])
                        v.tensor_scalar_add(out=f32u[:, :], in0=at2p[m % 2][:, :],
                                            scalar1=bos[:, jo:jo + 1])
                        v.engine_nop()
                        v.tensor_add(out=x2b[:, :], in0=x2b[:, :], in1=f32u[:, :])
                        v.engine_nop()
                        v.tensor_add(out=x2b[:, :], in0=x2b[:, :],
                                     in1=f32t[:, :]).then_inc(f_sdv, 1)

                    @block.gpsimd
                    def _(g, tt=p, jo=jo, m=m):
                        g.wait_ge(f_sdv, m + 1)
                        g.dma_start(out=x2o[jo * 128:(jo + 1) * 128,
                                            tt * TT:(tt + 1) * TT],
                                    in_=x2p[m % 2][:, :]).then_inc(f_sst, 16)

            @block.gpsimd
            def _(g):
                g.wait_ge(f_sst, 16 * NT * NFC)
                ph.inc(g.engine_nop())

        @block.gpsimd
        def _(g):
            ph.wait(g)



def fused_inmap(core, ins):
    c = core
    x = ins['L0_attn_hidden_states'].reshape(TOK, D)
    x_fm = np.ascontiguousarray(np.asarray(x, np.float32).T)
    m = {"x_sl": np.ascontiguousarray(x_fm[:, c * TPC:(c + 1) * TPC])}
    m0 = l0attn_inmap(c, None, ins)
    m0.pop("xn", None)
    m.update(m0)
    m["owp"] = np.ascontiguousarray(ins["L0_attn_o_w"][256 * c:256 * (c + 1), :])
    m["onw"] = np.ascontiguousarray(ins["L0_attn_o_norm_w"][:, None])
    m1 = mid_inmap(c, None, None, None, ins)
    for k in ("ofm", "xns", "xs", "ow", "onw"):
        m1.pop(k, None)
    m1["identf_m"] = m1.pop("identf")
    m.update(m1)
    m2 = l1attn_inmap(c, None, ins)
    m2.pop("xn1", None)
    m2["wq1"] = m2.pop("wq"); m2["wk1"] = m2.pop("wk"); m2["wv1"] = m2.pop("wv")
    m.update(m2)
    m["wop"] = np.ascontiguousarray(ins["L1_attn_Wo"][256 * c:256 * (c + 1), :])
    m3 = fin_inmap(c, None, None, None, ins)
    for k in ("ctx", "xn1s", "x1s", "wo"):
        m3.pop(k, None)
    m.update(m3)
    return m


def kernel_fused(upto=99, **inputs):
    nc = build_fused(upto=upto)
    maps = [fused_inmap(c, inputs) for c in range(NCORES)]
    res = run_bass_kernel_spmd(nc, maps, core_ids=list(range(NCORES)))
    if upto < 8:
        return [np.asarray(r["dbgo"]) for r in res.results]
    x2_fm = np.concatenate([np.asarray(r["x2_sl"]) for r in res.results], axis=1)
    return np.ascontiguousarray(x2_fm.T).reshape(4, 4096, D)



def kernel(**inputs):
    nc = build_fused(upto=8)
    maps = [fused_inmap(c, inputs) for c in range(NCORES)]
    res = run_bass_kernel_spmd(nc, maps, core_ids=list(range(NCORES)))
    x2_fm = np.concatenate([np.asarray(r["x2_sl"]) for r in res.results], axis=1)
    return np.ascontiguousarray(x2_fm.T).reshape(4, 4096, D)


# version 11
# speedup vs baseline: 1.3752x; 1.0077x over previous
import numpy as np
import ml_dtypes
from contextlib import ExitStack
from types import SimpleNamespace
import concourse.bass as bass
import concourse.mybir as mybir
from concourse.bass_utils import run_bass_kernel_spmd

F32 = mybir.dt.float32
BF16 = mybir.dt.bfloat16
I8 = mybir.dt.int8
AF = mybir.ActivationFunctionType
ALU = mybir.AluOpType
BF16NP = ml_dtypes.bfloat16

NCORES = 8
D = 2048
TOK = 16384
TPC = TOK // NCORES
NFC = D // 128
CLAMP = 21.0


class Ctr:
    def __init__(self, handle):
        self.h = handle
        self.n = 0

    def inc(self, instr, k=1):
        self.n += k
        instr.then_inc(self.h, k)
        return instr

    def wait(self, eng, n=None):
        eng.wait_ge(self.h, self.n if n is None else n)



def l0attn_consts():
    GT = 512
    col = np.arange(GT)
    c = {}
    for m in (8, 16, 32, 64, 128):
        c[f"rm{m}"] = np.broadcast_to((col % m != 0).astype(np.float32), (128, GT)).copy()
    c["evn8"] = np.broadcast_to(((col // 8) % 2 == 0).astype(BF16NP), (128, GT)).copy()
    c["odd8"] = np.broadcast_to(((col // 8) % 2 == 1).astype(BF16NP), (128, GT)).copy()
    s = np.arange(128)[:, None]; t = np.arange(128)[None, :]
    c["maskAb"] = np.where(s <= t, -1, 0).astype(np.int32)
    c["maskMb"] = np.where(s < t, -1, 0).astype(np.int32)
    c["identb"] = np.eye(128, dtype=BF16NP)
    c["identf"] = np.eye(128, dtype=np.float32)
    return c


def l0attn_inmap(core, xn_fm_bf16, ins):
    c = core
    hs = slice(256 * c, 256 * c + 256)
    CST = l0attn_consts()
    cwq = ins["L0_attn_q_conv_w"][hs].reshape(2, 128, 4)
    cwk = ins["L0_attn_k_conv_w"][hs].reshape(2, 128, 4)
    cwv = ins["L0_attn_v_conv_w"][hs].reshape(2, 128, 4)
    cw = np.concatenate([cwq[0], cwq[1], cwk[0], cwk[1], cwv[0], cwv[1]],
                        axis=1)
    m = {
        "wq": np.ascontiguousarray(ins["L0_attn_q_w"][:, hs]),
        "wk": np.ascontiguousarray(ins["L0_attn_k_w"][:, hs]),
        "wv": np.ascontiguousarray(ins["L0_attn_v_w"][:, hs]),
        "cw": np.ascontiguousarray(cw),
        "fw1": ins["L0_attn_f_w1"],
        "fw2": np.ascontiguousarray(ins["L0_attn_f_w2"][:, hs]),
        "gw1": ins["L0_attn_g_w1"],
        "gw2": np.ascontiguousarray(ins["L0_attn_g_w2"][:, hs]),
        "gb2": np.ascontiguousarray(ins["L0_attn_g_b2"][hs][None, :]),
        "bw": np.ascontiguousarray(ins["L0_attn_b_w"][:, 2 * c:2 * c + 2]),
        "alog": np.ascontiguousarray(ins["L0_attn_A_log"][2 * c:2 * c + 2][None, :]),
        "dtb": np.ascontiguousarray(ins["L0_attn_dt_bias"].reshape(16, 128)
                                    [2 * c:2 * c + 2].T),
    }
    for k, v in CST.items():
        m[k] = v
    return m


def mid_inmap(core, ofeat_full_bf16, xn_fm_bf16, x_fm_f32, ins):
    c = core
    sl = slice(TPC * c, TPC * (c + 1))
    CST = l0attn_consts()
    m_ = {}
    if ofeat_full_bf16 is not None:
        m_["ofm"] = np.ascontiguousarray(ofeat_full_bf16.T[:, sl])
        m_["xns"] = np.ascontiguousarray(xn_fm_bf16[:, sl])
        m_["xs"] = np.ascontiguousarray(x_fm_f32[:, sl])
    return {
        **m_,
        "onw": np.ascontiguousarray(ins["L0_attn_o_norm_w"][:, None]),
        "ow": ins["L0_attn_o_w"],
        "rout": ins["L0_ffn_router"],
        "wug": ins["L0_ffn_w_up_gate"],
        "wdn": ins["L0_ffn_w_down"],
        "swu": ins["L0_ffn_shared_w_up_gate"],
        "swd": ins["L0_ffn_shared_w_down"],
        "identf": CST["identf"],
        "esel": np.ascontiguousarray(
            np.concatenate([np.tile((np.arange(8) == e).astype(BF16NP)[:, None],
                                    (1, 128)) for e in range(8)], axis=1)),
    }


def l1attn_consts(core):
    S = 4096
    t = np.arange(S, dtype=np.float32)
    inv = 1.0 / 10000.0 ** (np.arange(0, 64, 2) / 64.0)
    fr = t[:, None] * inv[None, :]
    cos = np.cos(fr).astype(np.float32)
    sin = np.sin(fr).astype(np.float32)
    cos_t = np.ascontiguousarray(cos.reshape(32, 128, 32).transpose(1, 0, 2).reshape(128, 32 * 32))
    sin_t = np.ascontiguousarray(sin.reshape(32, 128, 32).transpose(1, 0, 2).reshape(128, 32 * 32))
    slopes = 2.0 ** (-(np.arange(1, 33) * 8.0 / 32))
    qaug = np.zeros((2, 4 * S), np.float32)
    for h in range(4):
        sl = slopes[4 * core + h]
        qaug[0, h * S:(h + 1) * S] = -sl * t
        qaug[1, h * S:(h + 1) * S] = sl
    kpos = np.stack([np.ones(S, np.float32), t]).astype(np.float32)
    sq = np.arange(128)[:, None]; tq = np.arange(128)[None, :]
    causal = (sq <= tq).astype(BF16NP)
    return {"cos_t": cos_t, "sin_t": sin_t, "qaug": qaug, "kpos": kpos, "causal": causal}


def l1attn_inmap(core, xn1_fm_bf16, ins):
    c = core
    CST = l1attn_consts(c)
    m = {
        "wq": np.ascontiguousarray(ins["L1_attn_Wq"][:, 256 * c:256 * (c + 1)]),
        "wk": np.ascontiguousarray(ins["L1_attn_Wk"][:, 64 * c:64 * (c + 1)]),
        "wv": np.ascontiguousarray(ins["L1_attn_Wv"][:, 64 * c:64 * (c + 1)]),
    }
    m.update(CST)
    if xn1_fm_bf16 is not None:
        m["xn1"] = xn1_fm_bf16
    return m


def fin_inmap(core, ctx_fm_bf16, xn1_fm_bf16, x1_fm_f32, ins):
    c = core
    sl = slice(TPC * c, TPC * (c + 1))
    m_ = {}
    if ctx_fm_bf16 is not None:
        m_["ctx"] = np.ascontiguousarray(ctx_fm_bf16[:, sl])
        m_["xn1s"] = np.ascontiguousarray(xn1_fm_bf16[:, sl])
        m_["x1s"] = np.ascontiguousarray(x1_fm_f32[:, sl])
    return {
        **m_,
        "wo": ins["L1_attn_Wo"],
        "bo": np.ascontiguousarray(ins["L1_attn_bo"].reshape(NFC, 128).T),
        "wg8": ins["L1_ffn_w_gate"],
        "wu8": ins["L1_ffn_w_up"],
        "wd8": ins["L1_ffn_w_down"],
        "sg": np.ascontiguousarray(ins["L1_ffn_s_gate"].reshape(64, 128).T),
        "su": np.ascontiguousarray(ins["L1_ffn_s_up"].reshape(64, 128).T),
        "sd": np.ascontiguousarray(ins["L1_ffn_s_down"].reshape(NFC, 128).T),
    }


I32F = mybir.dt.int32


def mkctx(nc, st, block, ph, pfx):
    C = SimpleNamespace(nc=nc, st=st, block=block, ph=ph, pfx=pfx)

    def sbuf(n, sh, dt=F32):
        return st.enter_context(nc.sbuf_tensor(pfx + n, sh, dt))

    def psum(n, sh, dt=F32):
        return st.enter_context(nc.psum_tensor(pfx + n, sh, dt))

    def T(fn):
        @block.tensor
        def _(t):
            ph.wait(t)
            fn(t)

    def V(fn):
        @block.vector
        def _(v):
            ph.wait(v)
            fn(v)

    def SC(fn):
        @block.scalar
        def _(s):
            ph.wait(s)
            fn(s)

    def SY(fn):
        @block.sync
        def _(sy):
            ph.wait(sy)
            fn(sy)

    def GQ(fn):
        @block.gpsimd
        def _(g):
            ph.wait(g)
            fn(g)

    C.sbuf, C.psum = sbuf, psum
    C.T, C.V, C.SC, C.SY, C.GQ = T, V, SC, SY, GQ
    return C


def emit_coll(C, kind, in_ap, out_ap):
    ph = C.ph
    op = ALU.add if kind == "ReduceScatter" else ALU.bypass

    @C.block.gpsimd
    def _(g):
        ph.wait(g)
        ins = g.collective_compute(
            kind, op,
            replica_groups=[list(range(NCORES))],
            ins=[in_ap.opt()], outs=[out_ap.opt()])
        ph.inc(ins, 1)


def emit_ln(C, x_in, xn_out, eps=1e-5):
    TT = 512
    NT = TPC // TT
    nc, ph = C.nc, C.ph
    with ExitStack() as st2:
        def sb(n, sh, dt=F32):
            return st2.enter_context(nc.sbuf_tensor(C.pfx + n, sh, dt))
        xb = sb("xb", [128, NFC * TT])
        sqf = sb("sqf", [128, TT])
        onecol = sb("onecol", [128, 1])
        onesc = sb("onesc", [1, 128])
        epsc = sb("epsc", [1, 1])
        statm = sb("statm", [1, 6 * TT])
        xnb = sb("xnb", [128, TT], BF16)
        pS = st2.enter_context(nc.psum_tensor(C.pfx + "pS", [1, TT], F32))
        pQ = st2.enter_context(nc.psum_tensor(C.pfx + "pQ", [1, TT], F32))
        pM = st2.enter_context(nc.psum_tensor(C.pfx + "pM", [128, TT], F32))
        pR = st2.enter_context(nc.psum_tensor(C.pfx + "pR", [128, TT], F32))

        def setup(v):
            v.memset(onecol[:, :], 1.0)
            v.memset(onesc[:, :], 1.0)
            ph.inc(v.memset(epsc[:, :], eps))
        C.V(setup)
        for tt in range(NT):
            def ldx(sy, tt=tt):
                for j in range(NFC):
                    ph.inc(sy.dma_start(out=xb[:, j * TT:(j + 1) * TT],
                                        in_=x_in[j * 128:(j + 1) * 128,
                                                 tt * TT:(tt + 1) * TT]), 16)
            C.SY(ldx)
            def mmsum(t):
                for j in range(NFC):
                    ins = t.matmul(out=pS[:, :], lhsT=onecol[:, 0:1],
                                   rhs=xb[:, j * TT:(j + 1) * TT],
                                   start=(j == 0), stop=(j == NFC - 1))
                ph.inc(ins)
            C.T(mmsum)
            for j in range(NFC):
                C.SC(lambda s, j=j: ph.inc(s.activation(out=sqf[:, :],
                                                        in_=xb[:, j * TT:(j + 1) * TT],
                                                        func=AF.Square)))
                C.T(lambda t, j=j: ph.inc(t.matmul(out=pQ[:, :], lhsT=onecol[:, 0:1],
                                                   rhs=sqf[:, :], start=(j == 0),
                                                   stop=(j == NFC - 1))))
            def stats(v):
                v.tensor_scalar_mul(out=statm[:, 0 * TT:1 * TT], in0=pS[:, :], scalar1=1.0 / D)
                v.tensor_scalar_mul(out=statm[:, 1 * TT:2 * TT], in0=pQ[:, :], scalar1=1.0 / D)
                v.tensor_mul(out=statm[:, 2 * TT:3 * TT], in0=statm[:, 0 * TT:1 * TT],
                             in1=statm[:, 0 * TT:1 * TT])
                ph.inc(v.tensor_sub(out=statm[:, 3 * TT:4 * TT], in0=statm[:, 1 * TT:2 * TT],
                                    in1=statm[:, 2 * TT:3 * TT]))
            C.V(stats)
            C.SC(lambda s: ph.inc(s.activation(out=statm[:, 4 * TT:5 * TT],
                                               in_=statm[:, 3 * TT:4 * TT],
                                               func=AF.Sqrt, bias=epsc[0:1, 0:1])))
            C.V(lambda v: ph.inc(v.reciprocal(out=statm[:, 5 * TT:6 * TT],
                                              in_=statm[:, 4 * TT:5 * TT])))
            def bcst(t):
                t.matmul(out=pM[:, :], lhsT=onesc[:, :], rhs=statm[:, 0 * TT:1 * TT],
                         start=True, stop=True)
                ph.inc(t.matmul(out=pR[:, :], lhsT=onesc[:, :],
                                rhs=statm[:, 5 * TT:6 * TT], start=True, stop=True))
            C.T(bcst)
            for j in range(NFC):
                def norm(v, j=j):
                    v.tensor_sub(out=sqf[:, :], in0=xb[:, j * TT:(j + 1) * TT], in1=pM[:, :])
                    ph.inc(v.tensor_mul(out=xnb[:, :], in0=sqf[:, :], in1=pR[:, :]))
                C.V(norm)
                def stxn(g, tt=tt, j=j):
                    ph.inc(g.dma_start(out=xn_out[j * 128:(j + 1) * 128,
                                                  tt * TT:(tt + 1) * TT],
                                       in_=xnb[:, :]), 16)
                C.GQ(stxn)


def build_fused(core_check=None, upto=99):
    nc = bass.Bass("TRN2", target_bir_lowering=False, debug=False)
    EI = lambda n, sh, dt=F32: nc.dram_tensor(n, sh, dt, kind="ExternalInput").ap()
    IN = lambda n, sh, dt=F32: nc.dram_tensor(n, sh, dt, kind="Internal").ap()
    SH = lambda n, sh, dt=F32: nc.dram_tensor(n, sh, dt, kind="Internal", addr_space="Shared").ap()
    x_sl = EI("x_sl", [D, TPC])
    xn_sl = IN("xn_sl", [D, TPC], BF16)
    xn_g = IN("xn_g", [NCORES * D, TPC], BF16)
    pattn = IN("pattn", [NCORES * D, TPC], BF16)
    attn_sl = IN("attn_sl", [D, TPC], BF16)
    x1_sl = IN("x1_sl", [D, TPC], F32)
    xn1_sl = IN("xn1_sl", [D, TPC], BF16)
    xn1_g = IN("xn1_g", [NCORES * D, TPC], BF16)
    pattn2 = IN("pattn2", [NCORES * D, TPC], BF16)
    attn2_sl = IN("attn2_sl", [D, TPC], BF16)
    x2_sl = nc.dram_tensor("x2_sl", [D, TPC], F32, kind="ExternalOutput").ap()

    with ExitStack() as st:
        ph = Ctr(st.enter_context(nc.semaphore("ph")))
        block = st.enter_context(nc.Block())
        st.enter_context(nc.allow_low_precision(
            reason="bf16 elementwise per problem statement (honest bf16 blessed)"))
        C = mkctx(nc, st, block, ph, "f_")

        emit_ln(C, x_sl, xn_sl)
        if upto >= 1:
            emit_coll(C, "AllGather", xn_sl, xn_g)
        if upto >= 2:
            emit_l0attn_f(C, nc, xn_g, pattn)
        if upto >= 3:
            emit_coll(C, "ReduceScatter", pattn, attn_sl)
        if upto >= 4:
            emit_mid_f(C, nc, attn_sl, xn_sl, x_sl, x1_sl, xn1_sl)
        if upto >= 5:
            emit_coll(C, "AllGather", xn1_sl, xn1_g)
        if upto >= 6:
            emit_l1attn_f(C, nc, xn1_g, pattn2)
        if upto >= 7:
            emit_coll(C, "ReduceScatter", pattn2, attn2_sl)
        if upto >= 8:
            emit_fin_f(C, nc, attn2_sl, xn1_sl, x1_sl, x2_sl)
        if upto < 8:
            fin_map = {0: xn_sl, 1: xn_g, 2: pattn, 3: attn_sl, 4: xn1_sl,
                       5: xn1_g, 6: pattn2, 7: attn2_sl}
            srcap = fin_map[upto]
            dbgo = nc.dram_tensor("dbgo", list(srcap.shape), srcap.dtype,
                                  kind="ExternalOutput").ap()

            @block.gpsimd
            def _(g):
                ph.wait(g)
                rows = srcap.shape[0]
                step = min(2048, rows)
                for r0 in range(0, rows, step):
                    ph.inc(g.dma_start(out=dbgo[r0:r0 + step, :],
                                       in_=srcap[r0:r0 + step, :]), 16)

        @block.gpsimd
        def _(g):
            ph.wait(g)
    return nc


def emit_l0attn_f(C, nc, xn_g, pattn):
    GT = 512
    NG = TOK // GT
    NCH = GT // 128
    wq = nc.dram_tensor("wq", [D, 256], F32, kind="ExternalInput").ap()
    wk = nc.dram_tensor("wk", [D, 256], F32, kind="ExternalInput").ap()
    wv = nc.dram_tensor("wv", [D, 256], F32, kind="ExternalInput").ap()
    cw = nc.dram_tensor("cw", [128, 24], F32, kind="ExternalInput").ap()
    fw1 = nc.dram_tensor("fw1", [D, 128], F32, kind="ExternalInput").ap()
    fw2 = nc.dram_tensor("fw2", [128, 256], F32, kind="ExternalInput").ap()
    gw1 = nc.dram_tensor("gw1", [D, 128], F32, kind="ExternalInput").ap()
    gw2 = nc.dram_tensor("gw2", [128, 256], F32, kind="ExternalInput").ap()
    gb2 = nc.dram_tensor("gb2", [1, 256], F32, kind="ExternalInput").ap()
    bw = nc.dram_tensor("bw", [D, 2], F32, kind="ExternalInput").ap()
    alog = nc.dram_tensor("alog", [1, 2], F32, kind="ExternalInput").ap()
    dtb = nc.dram_tensor("dtb", [128, 2], F32, kind="ExternalInput").ap()
    owp = nc.dram_tensor("owp", [256, D], F32, kind="ExternalInput").ap()
    onw = nc.dram_tensor("onw", [128, 1], F32, kind="ExternalInput").ap()
    CST = l0attn_consts()
    I32 = mybir.dt.int32
    cin = {}
    for k, v in CST.items():
        dt = BF16 if v.dtype == BF16NP else (I32 if v.dtype == np.int32 else F32)
        cin[k] = nc.dram_tensor(k, list(v.shape), dt, kind="ExternalInput").ap()

    with ExitStack() as st:
        def sbuf(n, sh, dt=F32):
            return st.enter_context(nc.sbuf_tensor("a_" + n, sh, dt))
        wqs = sbuf("wqs", [128, NFC * 256], BF16)
        wks = sbuf("wks", [128, NFC * 256], BF16)
        wvs = sbuf("wvs", [128, NFC * 256], BF16)
        fw1s = sbuf("fw1s", [128, NFC * 128], BF16)
        gw1s = sbuf("gw1s", [128, NFC * 128], BF16)
        fw2s = sbuf("fw2s", [128, 256], BF16)
        gw2s = sbuf("gw2s", [128, 256], BF16)
        gb2s = sbuf("gb2s", [1, 256], F32)
        bws = sbuf("bws", [128, NFC * 2], BF16)
        cws = sbuf("cws", [128, 24], F32)
        alogs = sbuf("alogs", [1, 2], F32)
        negA = sbuf("negA", [128, 2], F32)
        dts = sbuf("dts", [128, 2], F32)
        onesc = sbuf("onesc", [1, 128], F32)
        ones1b = sbuf("ones1b", [1, 128], BF16)
        onecol = sbuf("onecol", [128, 1], F32)
        onebias = sbuf("onebias", [128, 1], F32)
        epsl2 = sbuf("epsl2", [1, 1], F32)
        cs = {}
        for k, v in CST.items():
            dt = BF16 if v.dtype == BF16NP else (mybir.dt.int32 if v.dtype == np.int32 else F32)
            cs[k] = sbuf("c2_" + k, list(v.shape), dt)
        mskd = sbuf("mskd", [128, 128], mybir.dt.int32)
        xng = sbuf("xng0", [128, NFC * GT], BF16)
        raw = {nm: sbuf("raw" + nm, [128, GT + 3], BF16)
               for nm in ("q0", "q1", "k0", "k1", "v0", "v1")}
        qh = [sbuf(f"qh{h}", [128, GT], BF16) for h in range(2)]
        kh = [sbuf(f"kh{h}", [128, GT], BF16) for h in range(2)]
        vh = [sbuf(f"vh{h}", [128, GT], BF16) for h in range(2)]
        innf = sbuf("innf", [128, GT], BF16)
        inng = sbuf("inng", [128, GT], BF16)
        betac = sbuf("betac", [2, GT], F32)
        nsc = sbuf("nsc", [1, 4 * 512], F32)
        beta_f = sbuf("beta_f", [128, NCH * 2], F32)
        beta_fn = sbuf("beta_fn", [128, NCH * 2], F32)
        gate_tm = sbuf("gate_tm", [128, NCH * 256], BF16)
        Gf = sbuf("Gf", [128, GT], F32)
        RDf = sbuf("RDf", [128, GT], F32)
        SDf = sbuf("SDf", [128, GT], F32)
        ERf = sbuf("ERf", [128, GT], F32)
        gkf = sbuf("gkf", [128, GT], F32)
        sp1 = sbuf("sp1", [128, GT], F32)
        spb = sbuf("spb", [128, GT], F32)
        spc = sbuf("spc", [128, GT], BF16)
        spb2 = sbuf("spb2", [128, GT], BF16)
        mapn = ["EQ", "ED", "rowb", "colb", "r8", "c8", "r16", "c16", "r32", "c32", "r64", "c64"]
        maps = {k: sbuf("m_" + k, [128, GT], BF16) for k in mapn}
        C1 = sbuf("C1", [128, GT], BF16); C2 = sbuf("C2", [128, GT], BF16)
        C3 = sbuf("C3", [128, GT], BF16); C4 = sbuf("C4", [128, GT], BF16)
        C5 = sbuf("C5", [128, GT], BF16); C6 = sbuf("C6", [128, GT], BF16)
        RBe = sbuf("RBe", [128, 2 * GT], BF16); RBo = sbuf("RBo", [128, 2 * GT], BF16)
        R8 = sbuf("R8", [128, 2 * GT], BF16); R16 = sbuf("R16", [128, 2 * GT], BF16)
        R32 = sbuf("R32", [128, 2 * GT], BF16); R64 = sbuf("R64", [128, 2 * GT], BF16)
        Qs = sbuf("Qs", [128, GT], BF16); Ws = sbuf("Ws", [128, GT], BF16)
        KD = sbuf("KD", [128, GT], BF16)
        eGend = sbuf("eGend", [128, NCH], F32)
        Aqt4 = sbuf("Aqt4", [128, NCH * 128], BF16)
        Ntn4 = sbuf("Ntn4", [128, NCH * 128], BF16)
        powT4 = [sbuf(f"powT4_{i}", [128, NCH * 128], BF16) for i in range(6)]
        pown4 = [sbuf(f"pown4_{i}", [128, NCH * 128], BF16) for i in range(6)]
        ybuf4 = [sbuf(f"ybuf4_{i}", [128, NCH * 256], BF16) for i in range(2)]
        rhsb4 = sbuf("rhsb4", [128, NCH * 256], BF16)
        S_f = [sbuf(f"S_f{h}", [128, 128], F32) for h in range(2)]
        S_b = [sbuf(f"S_b{h}", [128, 128], BF16) for h in range(2)]
        UWc4 = sbuf("UWc4", [128, NCH * 128], BF16)
        KDt4 = sbuf("KDt4", [128, NCH * 128], BF16)
        xdel = sbuf("xdel", [128, 128], BF16)
        useq = sbuf("useq", [128, 128], BF16)
        ssq = sbuf("ssq", [128, 12], F32)
        obuf = sbuf("obuf", [128, NCH * 256], BF16)
        ows2 = sbuf("ows2", [128, 2 * D], BF16)
        ofcm = sbuf("ofcm", [128, 2 * GT], BF16)
        patb2 = sbuf("patb2", [128, 2 * GT], BF16)
        onws = sbuf("onws", [128, 1], F32)
        pA = st.enter_context(nc.psum_tensor("a_pA", [128, GT], F32))
        pC = st.enter_context(nc.psum_tensor("a_pC", [2, GT], F32))
        pScoreB = st.enter_context(nc.psum_tensor("a_pScoreB", [128, NCH * 256], F32))
        pStageB = st.enter_context(nc.psum_tensor("a_pStageB", [16, NCH * 32], F32))
        pT4 = st.enter_context(nc.psum_tensor("a_pT4", [128, NCH * 128], BF16))
        pSolve = st.enter_context(nc.psum_tensor("a_pSolve", [128, NCH * 256], F32))
        ph = C.ph
        block = C.block

        def T(fn):
            @block.tensor
            def _(t):
                ph.wait(t)
                fn(t)
        def V(fn):
            @block.vector
            def _(v):
                ph.wait(v)
                fn(v)
        def SC(fn):
            @block.scalar
            def _(s):
                ph.wait(s)
                fn(s)
        def G3(fn):
            @block.gpsimd
            def _(g):
                ph.wait(g)
                fn(g)

        @block.gpsimd
        def _(g):
            ph.wait(g)
            for w_dram, w_sb, nch in ((wq, wqs, 256), (wk, wks, 256), (wv, wvs, 256),
                                      (fw1, fw1s, 128), (gw1, gw1s, 128), (bw, bws, 2)):
                src = w_dram.rearrange("(j p) n -> p j n", p=128)
                dst = w_sb[:, :].rearrange("p (j n) -> p j n", n=nch)
                ph.inc(g.dma_start(out=dst, in_=src), 16)
            for w_dram, w_sb in ((fw2, fw2s), (gw2, gw2s), (cw, cws)):
                ph.inc(g.dma_start(out=w_sb[:, :], in_=w_dram), 16)
            srcw = owp.rearrange("(j p) n -> p j n", p=128)
            dstw = ows2[:, :].rearrange("p (j n) -> p j n", n=D)
            ph.inc(g.dma_start(out=dstw, in_=srcw), 16)

        @block.sync
        def _(sy):
            ph.wait(sy)
            ph.inc(sy.dma_start(out=gb2s[:, :], in_=gb2), 16)
            ph.inc(sy.dma_start(out=alogs[:, :], in_=alog), 16)
            ph.inc(sy.dma_start(out=dts[:, :], in_=dtb), 16)
            ph.inc(sy.dma_start(out=onws[:, :], in_=onw), 16)
            for k in CST:
                ph.inc(sy.dma_start(out=cs[k][:, :], in_=cin[k]), 16)

        def setup1(v):
            v.memset(onesc[:, :], 1.0)
            v.memset(ones1b[:, :], 1.0)
            v.memset(onecol[:, :], 1.0)
            v.memset(onebias[:, :], 1.0)
            v.memset(epsl2[:, :], 1e-6)
            for h in range(2):
                v.memset(S_f[h][:, :], 0.0)
                v.memset(S_b[h][:, :], 0.0)
            ph.inc(v.engine_nop())
        V(setup1)
        SC(lambda s: ph.inc(s.activation(out=alogs[:, :], in_=alogs[:, :], func=AF.Exp)))
        V(lambda v: ph.inc(v.tensor_scalar_mul(out=alogs[:, :], in0=alogs[:, :], scalar1=-1.0)))
        T(lambda t: ph.inc(t.matmul(out=pA[:, 0:2], lhsT=onesc[:, :], rhs=alogs[:, :],
                                    start=True, stop=True)))
        V(lambda v: ph.inc(v.tensor_copy(out=negA[:, :], in_=pA[:, 0:2])))

        for g in range(NG):
            bstart = (g % max(NG // 4, 8) == 0)
            xbuf = xng

            def loadxn(sy, g=g, xbuf=xbuf):
                rk = (g * GT) // TPC
                cb = (g * GT) % TPC
                for j in range(NFC):
                    ins = sy.dma_start(out=xbuf[:, j * GT:(j + 1) * GT],
                                       in_=xn_g[rk * D + j * 128:rk * D + (j + 1) * 128,
                                                cb:cb + GT])
                    ph.inc(ins, 16)
            @block.sync
            def _(sy):
                ph.wait(sy)
                loadxn(sy)

            for pi, wsb in enumerate((wqs, wks, wvs)):
                for ht in range(2):
                    nm = ("q0q1", "k0k1", "v0v1")[pi][ht * 2:ht * 2 + 2]
                    rb = raw[nm]
                    def mmproj(t, wsb=wsb, ht=ht, xbuf=xbuf):
                        for j in range(NFC):
                            ins = t.matmul(out=pA[:, :],
                                           lhsT=wsb[:, j * 256 + ht * 128:j * 256 + ht * 128 + 128],
                                           rhs=xbuf[:, j * GT:(j + 1) * GT],
                                           start=(j == 0), stop=(j == NFC - 1))
                        ph.inc(ins)
                    T(mmproj)
                    def convf(v, rb=rb, pi=pi, ht=ht, bstart=bstart):
                        v.tensor_copy(out=rb[:, 3:3 + GT], in_=pA[:, :])
                        if bstart:
                            v.memset(rb[:, 0:3], 0.0)
                        v.engine_nop()
                        ci = (pi * 2 + ht) * 4
                        v.tensor_scalar(out=spc[:, :], in0=rb[:, 0:GT],
                                        scalar1=cws[:, ci:ci + 1], scalar2=None, op0=ALU.mult)
                        for i in range(1, 4):
                            v.scalar_tensor_tensor(out=spc[:, :], in0=rb[:, i:i + GT],
                                                   scalar=cws[:, ci + i:ci + i + 1],
                                                   in1=spc[:, :], op0=ALU.mult, op1=ALU.add)
                        ph.inc(v.tensor_copy(out=rb[:, 0:3], in_=rb[:, GT:GT + 3]))
                    V(convf)
                    dst = (qh, kh, vh)[pi][ht]
                    SC(lambda s: ph.inc(s.activation(out=sp1[:, :], in_=spc[:, :],
                                                     func=AF.Tanh, scale=0.5)))
                    def silu_v(v, dst=dst):
                        v.tensor_scalar(out=spb2[:, :], in0=sp1[:, :],
                                        scalar1=0.5, scalar2=0.5, op0=ALU.mult, op1=ALU.add)
                        v.engine_nop()
                        ph.inc(v.tensor_mul(out=dst[:, :], in0=spc[:, :], in1=spb2[:, :]))
                    V(silu_v)

            l2srcs = [qh[0], qh[1], kh[0], kh[1]]
            sumps = [pScoreB[0:1, 0:512], pScoreB[0:1, 512:1024],
                     pSolve[0:1, 0:512], pSolve[0:1, 512:1024]]
            bcps = [pA[:, :], pScoreB[:, 0:512], pScoreB[:, 512:1024], pSolve[:, 0:512]]
            for half in range(2):
                def sq2(s, half=half):
                    s.activation(out=sp1[:, :], in_=l2srcs[2 * half][:, :], func=AF.Square)
                    ph.inc(s.activation(out=spb[:, :], in_=l2srcs[2 * half + 1][:, :],
                                        func=AF.Square))
                SC(sq2)
                def sum2(t, half=half):
                    t.matmul(out=sumps[2 * half], lhsT=onecol[:, 0:1], rhs=sp1[:, :],
                             start=True, stop=True)
                    ph.inc(t.matmul(out=sumps[2 * half + 1], lhsT=onecol[:, 0:1],
                                    rhs=spb[:, :], start=True, stop=True))
                T(sum2)
            def sqrt4(s):
                for i in range(4):
                    ins = s.activation(out=nsc[0:1, i * 512:(i + 1) * 512], in_=sumps[i],
                                       func=AF.Sqrt, bias=epsl2[0:1, 0:1])
                ph.inc(ins)
            SC(sqrt4)
            V(lambda v: ph.inc(v.reciprocal(out=nsc[0:1, :], in_=nsc[0:1, :])))
            def bc4(t):
                for i in range(4):
                    ins = t.matmul(out=bcps[i], lhsT=onesc[:, :],
                                   rhs=nsc[0:1, i * 512:(i + 1) * 512],
                                   start=True, stop=True)
                ph.inc(ins)
            T(bc4)
            def l2fin(v):
                for i in range(4):
                    scl = (128.0 ** -0.5) if i < 2 else 1.0
                    v.tensor_scalar_mul(out=spb2[:, :], in0=bcps[i], scalar1=scl)
                    v.engine_nop()
                    ins = v.tensor_mul(out=l2srcs[i][:, :], in0=l2srcs[i][:, :],
                                       in1=spb2[:, :])
                ph.inc(ins)
            V(l2fin)

            for wsb, dst in ((fw1s, innf), (gw1s, inng)):
                def mminner(t, wsb=wsb, xbuf=xbuf):
                    for j in range(NFC):
                        ins = t.matmul(out=pA[:, :], lhsT=wsb[:, j * 128:(j + 1) * 128],
                                       rhs=xbuf[:, j * GT:(j + 1) * GT],
                                       start=(j == 0), stop=(j == NFC - 1))
                    ph.inc(ins)
                T(mminner)
                V(lambda v, dst=dst: ph.inc(v.tensor_copy(out=dst[:, :], in_=pA[:, :])))

            def mmbeta(t, xbuf=xbuf):
                for j in range(NFC):
                    ins = t.matmul(out=pC[:, :], lhsT=bws[:, j * 2:(j + 1) * 2],
                                   rhs=xbuf[:, j * GT:(j + 1) * GT],
                                   start=(j == 0), stop=(j == NFC - 1))
                ph.inc(ins)
            T(mmbeta)
            SC(lambda s: ph.inc(s.activation(out=betac[:, :], in_=pC[:, :],
                                             func=AF.Tanh, scale=0.5)))
            V(lambda v: ph.inc(v.tensor_scalar(out=betac[:, :], in0=betac[:, :],
                                               scalar1=0.5, scalar2=0.5,
                                               op0=ALU.mult, op1=ALU.add)))
            def trbeta(t):
                for ch in range(NCH):
                    ins = t.transpose(out=pSolve[:, ch * 256:ch * 256 + 2],
                                      in_=betac[:, ch * 128:(ch + 1) * 128],
                                      identity=cs["identf"][0:2, 0:2])
                ph.inc(ins)
            T(trbeta)
            def betaev(v):
                for ch in range(NCH):
                    v.tensor_copy(out=beta_f[:, ch * 2:(ch + 1) * 2],
                                  in_=pSolve[:, ch * 256:ch * 256 + 2])
                for ch in range(NCH):
                    ins = v.tensor_scalar_mul(out=beta_fn[:, ch * 2:(ch + 1) * 2],
                                              in0=pSolve[:, ch * 256:ch * 256 + 2],
                                              scalar1=-1.0)
                ph.inc(ins)
            V(betaev)

            for cp in range(NCH // 2):
                def mmgate(t, cp=cp):
                    for q in range(2):
                        ch = cp * 2 + q
                        t.matmul(out=pA[:, q * 256:q * 256 + 256],
                                 lhsT=inng[:, ch * 128:(ch + 1) * 128],
                                 rhs=gw2s[:, :], start=True, stop=False)
                        ins = t.matmul(out=pA[:, q * 256:q * 256 + 256], lhsT=onesc[:, :],
                                       rhs=gb2s[:, :], start=False, stop=True,
                                       skip_group_check=True)
                    ph.inc(ins)
                T(mmgate)
                SC(lambda s: ph.inc(s.activation(out=sp1[:, :], in_=pA[:, :],
                                                 func=AF.Tanh, scale=0.5)))
                def gatefin(v, cp=cp):
                    ph.inc(v.tensor_scalar(out=gate_tm[:, cp * 512:(cp + 1) * 512],
                                           in0=sp1[:, :], scalar1=0.5, scalar2=0.5,
                                           op0=ALU.mult, op1=ALU.add))
                V(gatefin)

            for ht in range(2):
                T(lambda t, ht=ht: ph.inc(t.matmul(out=pA[:, :],
                                                   lhsT=fw2s[:, ht * 128:(ht + 1) * 128],
                                                   rhs=innf[:, :], start=True, stop=True)))
                V(lambda v, ht=ht: ph.inc(v.tensor_scalar_add(out=sp1[:, :], in0=pA[:, :],
                                                              scalar1=dts[:, ht:ht + 1])))
                SC(lambda s: ph.inc(s.activation(out=gkf[:, :], in_=sp1[:, :], func=AF.Exp)))
                SC(lambda s: ph.inc(s.activation(out=sp1[:, :], in_=gkf[:, :], func=AF.Ln,
                                                 bias=onebias[:, 0:1])))
                V(lambda v, ht=ht: ph.inc(v.tensor_scalar(out=gkf[:, :], in0=sp1[:, :],
                                                          scalar1=negA[:, ht:ht + 1],
                                                          scalar2=-CLAMP,
                                                          op0=ALU.mult, op1=ALU.max)))

                def scan_map(rm, dst):
                    V(lambda v, rm=rm, dst=dst: ph.inc(
                        v.tensor_tensor_scan(out=dst[:, :], data0=cs[rm][:, :],
                                             data1=gkf[:, :], initial=0.0,
                                             op0=ALU.mult, op1=ALU.add)))
                def sufd(v, src, dst, last, step):
                    nb = GT // step
                    gath = src[:, last::step].unsqueeze(2).broadcast_to([128, nb, step])
                    s3 = src[:, :].rearrange("p (a b) -> p a b", b=step)
                    d3 = dst[:, :].rearrange("p (a b) -> p a b", b=step)
                    return v.tensor_sub(out=d3, in0=gath, in1=s3)
                def expm(s, src, dst, scale=1.0):
                    return s.activation(out=maps[dst][:, :], in_=src[:, :], func=AF.Exp,
                                        scale=scale)

                scan_map("rm128", Gf)
                V(lambda v: ph.inc(sufd(v, Gf, SDf, 127, 128)))
                def exps128(s):
                    expm(s, Gf, "EQ")
                    expm(s, SDf, "ED")
                    ph.inc(s.activation(out=eGend[:, 0:NCH], in_=Gf[:, 127::128],
                                        func=AF.Exp))
                SC(exps128)
                scan_map("rm8", RDf)
                def d8(v):
                    sufd(v, RDf, SDf, 7, 8)
                    gath = RDf[:, 3::8].unsqueeze(2).broadcast_to([128, GT // 8, 8])
                    s3 = RDf[:, :].rearrange("p (a b) -> p a b", b=8)
                    d3 = ERf[:, :].rearrange("p (a b) -> p a b", b=8)
                    ph.inc(v.tensor_sub(out=d3, in0=s3, in1=gath))
                V(d8)
                def exps8(s):
                    expm(s, RDf, "r8")
                    expm(s, SDf, "c8")
                    expm(s, ERf, "rowb")
                    ph.inc(expm(s, ERf, "colb", scale=-1.0))
                SC(exps8)
                for m, rn, cn in ((16, "r16", "c16"), (32, "r32", "c32"), (64, "r64", "c64")):
                    scan_map(f"rm{m}", RDf)
                    V(lambda v, m=m: ph.inc(sufd(v, RDf, SDf, m - 1, m)))
                    def expsm(s, rn=rn, cn=cn):
                        expm(s, RDf, rn)
                        ph.inc(expm(s, SDf, cn))
                    SC(expsm)

                def optiles(v, ht=ht):
                    q = qh[ht]; k = kh[ht]
                    v.tensor_mul(out=Qs[:, :], in0=q[:, :], in1=maps["EQ"][:, :])
                    v.tensor_mul(out=Ws[:, :], in0=k[:, :], in1=maps["EQ"][:, :])
                    v.tensor_mul(out=KD[:, :], in0=k[:, :], in1=maps["ED"][:, :])
                    v.tensor_mul(out=C1[:, :], in0=k[:, :], in1=maps["colb"][:, :])
                    v.tensor_mul(out=C2[:, :], in0=C1[:, :], in1=cs["odd8"][:, :])
                    v.tensor_mul(out=C1[:, :], in0=C1[:, :], in1=cs["evn8"][:, :])
                    v.tensor_mul(out=RBe[:, 0:GT], in0=q[:, :], in1=maps["rowb"][:, :])
                    v.tensor_mul(out=RBe[:, GT:2 * GT], in0=k[:, :], in1=maps["rowb"][:, :])
                    v.tensor_mul(out=RBo[:, 0:GT], in0=RBe[:, 0:GT], in1=cs["odd8"][:, :])
                    v.tensor_mul(out=RBo[:, GT:2 * GT], in0=RBe[:, GT:2 * GT], in1=cs["odd8"][:, :])
                    v.tensor_mul(out=RBe[:, 0:GT], in0=RBe[:, 0:GT], in1=cs["evn8"][:, :])
                    v.tensor_mul(out=RBe[:, GT:2 * GT], in0=RBe[:, GT:2 * GT], in1=cs["evn8"][:, :])
                    v.tensor_mul(out=C3[:, :], in0=k[:, :], in1=maps["c8"][:, :])
                    v.tensor_mul(out=C3[:, :], in0=C3[:, :], in1=cs["evn8"][:, :])
                    v.tensor_mul(out=R8[:, 0:GT], in0=q[:, :], in1=maps["r8"][:, :])
                    v.tensor_mul(out=R8[:, GT:2 * GT], in0=k[:, :], in1=maps["r8"][:, :])
                    v.tensor_mul(out=R8[:, 0:GT], in0=R8[:, 0:GT], in1=cs["odd8"][:, :])
                    v.tensor_mul(out=R8[:, GT:2 * GT], in0=R8[:, GT:2 * GT], in1=cs["odd8"][:, :])
                    for Cx, Rx, rn, cn in ((C4, R16, "r16", "c16"), (C5, R32, "r32", "c32"),
                                           (C6, R64, "r64", "c64")):
                        v.tensor_mul(out=Cx[:, :], in0=k[:, :], in1=maps[cn][:, :])
                        v.tensor_mul(out=Rx[:, 0:GT], in0=q[:, :], in1=maps[rn][:, :])
                        v.tensor_mul(out=Rx[:, GT:2 * GT], in0=k[:, :], in1=maps[rn][:, :])
                    ph.inc(v.engine_nop())
                V(optiles)

                def scoresb(t):
                    for ch in range(NCH):
                        c0 = ch * 128
                        def rhs2(Rt, lo, hi):
                            r3 = Rt[:, :].rearrange("p (h t) -> p h t", h=2)
                            return r3[:, :, c0 + lo:c0 + hi]
                        ps3 = pScoreB[:, ch * 256:(ch + 1) * 256].rearrange(
                            "p (h t) -> p h t", h=2)
                        t.matmul(out=ps3[:, :, :], lhsT=C1[:, c0:c0 + 128],
                                 rhs=rhs2(RBe, 0, 128), start=True, stop=False)
                        t.matmul(out=ps3[:, :, :], lhsT=C2[:, c0:c0 + 128],
                                 rhs=rhs2(RBo, 0, 128), start=False, stop=False)
                        t.matmul(out=ps3[:, :, :], lhsT=C3[:, c0:c0 + 128],
                                 rhs=rhs2(R8, 0, 128), start=False, stop=True)
                        for p in range(3):
                            t.matmul(out=ps3[32 * p:32 * p + 16, :, 32 * p + 16:32 * p + 32],
                                     lhsT=C4[:, c0 + 32 * p:c0 + 32 * p + 16],
                                     rhs=rhs2(R16, 32 * p + 16, 32 * p + 32),
                                     start=True, stop=True)
                        st3 = pStageB[:, ch * 32:(ch + 1) * 32].rearrange(
                            "p (h t) -> p h t", h=2)
                        t.matmul(out=st3[0:16, :, 0:16], lhsT=C4[:, c0 + 96:c0 + 112],
                                 rhs=rhs2(R16, 112, 128), start=True, stop=True)
                        for p in range(2):
                            t.matmul(out=ps3[64 * p:64 * p + 32, :, 64 * p + 32:64 * p + 64],
                                     lhsT=C5[:, c0 + 64 * p:c0 + 64 * p + 32],
                                     rhs=rhs2(R32, 64 * p + 32, 64 * p + 64),
                                     start=True, stop=True)
                        ins = t.matmul(out=ps3[0:64, :, 64:128], lhsT=C6[:, c0:c0 + 64],
                                       rhs=rhs2(R64, 64, 128), start=True, stop=True)
                    ph.inc(ins)
                T(scoresb)

                def evacb(v, ht=ht):
                    for ch in range(NCH):
                        cA = ch * 128
                        v.tensor_tensor(out=mskd[:, :],
                                        in0=pScoreB[:, ch * 256:ch * 256 + 128]
                                        .bitcast(mybir.dt.int32),
                                        in1=cs["maskAb"][:, :], op=ALU.bitwise_and)
                        v.engine_nop()
                        v.tensor_copy(out=Aqt4[:, cA:cA + 128], in_=mskd[:, :].bitcast(F32))
                        v.tensor_tensor(out=mskd[:, :],
                                        in0=pScoreB[:, ch * 256 + 128:ch * 256 + 256]
                                        .bitcast(mybir.dt.int32),
                                        in1=cs["maskMb"][:, :], op=ALU.bitwise_and)
                        v.engine_nop()
                        v.tensor_scalar_mul(out=Ntn4[:, cA:cA + 128],
                                            in0=mskd[:, :].bitcast(F32),
                                            scalar1=beta_fn[:, ch * 2 + ht:ch * 2 + ht + 1])
                        v.tensor_copy(out=Aqt4[96:112, cA + 112:cA + 128],
                                      in_=pStageB[0:16, ch * 32:ch * 32 + 16])
                        ins = v.tensor_scalar_mul(
                            out=Ntn4[96:112, cA + 112:cA + 128],
                            in0=pStageB[0:16, ch * 32 + 16:ch * 32 + 32],
                            scalar1=beta_fn[96:112, ch * 2 + ht:ch * 2 + ht + 1])
                    ph.inc(ins)
                V(evacb)

                def trz(t):
                    for ch in range(NCH):
                        ins = t.transpose(out=pT4[:, ch * 128:(ch + 1) * 128],
                                          in_=Ntn4[:, ch * 128:(ch + 1) * 128],
                                          identity=cs["identb"][:, :])
                    ph.inc(ins)
                T(trz)
                def cpz(v):
                    for ch in range(NCH):
                        ins = v.tensor_copy(out=pown4[0][:, ch * 128:(ch + 1) * 128],
                                            in_=pT4[:, ch * 128:(ch + 1) * 128])
                    ph.inc(ins)
                V(cpz)
                for j in range(1, 7):
                    def sqrb(t, j=j):
                        for ch in range(NCH):
                            sl = slice(ch * 128, (ch + 1) * 128)
                            lastT = Ntn4[:, sl] if j == 1 else powT4[j - 2][:, sl]
                            lastN = pown4[j - 1][:, sl]
                            ins = t.matmul(out=pA[:, sl], lhsT=lastN, rhs=lastT,
                                           start=True, stop=True)
                            if j < 6:
                                ins = t.matmul(out=pSolve[:, sl], lhsT=lastT, rhs=lastN,
                                               start=True, stop=True)
                        ph.inc(ins)
                    T(sqrb)
                    def sqevb(v, j=j):
                        for ch in range(NCH):
                            sl = slice(ch * 128, (ch + 1) * 128)
                            ins = v.tensor_copy(out=powT4[j - 1][:, sl], in_=pA[:, sl])
                            if j < 6:
                                ins = v.tensor_copy(out=pown4[j][:, sl], in_=pSolve[:, sl])
                        ph.inc(ins)
                    V(sqevb)
                def trv(t, ht=ht):
                    for ch in range(NCH):
                        ins = t.transpose(out=pT4[:, ch * 128:(ch + 1) * 128],
                                          in_=vh[ht][:, ch * 128:(ch + 1) * 128],
                                          identity=cs["identb"][:, :])
                    ph.inc(ins)
                T(trv)
                def cpv(v):
                    for ch in range(NCH):
                        ins = v.tensor_copy(out=rhsb4[:, ch * 256:ch * 256 + 128],
                                            in_=pT4[:, ch * 128:(ch + 1) * 128])
                    ph.inc(ins)
                V(cpv)
                def trw(t):
                    for ch in range(NCH):
                        ins = t.transpose(out=pT4[:, ch * 128:(ch + 1) * 128],
                                          in_=Ws[:, ch * 128:(ch + 1) * 128],
                                          identity=cs["identb"][:, :])
                    ph.inc(ins)
                T(trw)
                def cpw(v):
                    for ch in range(NCH):
                        ins = v.tensor_copy(out=rhsb4[:, ch * 256 + 128:ch * 256 + 256],
                                            in_=pT4[:, ch * 128:(ch + 1) * 128])
                    ph.inc(ins)
                V(cpw)
                ydl = spb[:, :].bitcast(BF16)
                ycur = rhsb4
                for j in range(6, -1, -1):
                    ynext = ybuf4[j % 2]
                    def chmm(t, j=j, ycur=ycur):
                        for ch in range(NCH):
                            p = Ntn4 if j == 0 else powT4[j - 1]
                            ins = t.matmul(out=pSolve[:, ch * 256:(ch + 1) * 256],
                                           lhsT=p[:, ch * 128:(ch + 1) * 128],
                                           rhs=ycur[:, ch * 256:(ch + 1) * 256],
                                           start=True, stop=True)
                        ph.inc(ins)
                    T(chmm)
                    def chadd(v, ycur=ycur, ynext=ynext):
                        for ch in range(NCH):
                            v.tensor_copy(out=ydl[:, ch * 256:(ch + 1) * 256],
                                          in_=pSolve[:, ch * 256:(ch + 1) * 256])
                        v.engine_nop()
                        for ch in range(NCH):
                            ins = v.tensor_add(out=ynext[:, ch * 256:(ch + 1) * 256],
                                               in0=ycur[:, ch * 256:(ch + 1) * 256],
                                               in1=ydl[:, ch * 256:(ch + 1) * 256])
                        ph.inc(ins)
                    V(chadd)
                    ycur = ynext
                U4 = ycur

                def trsb(t, U4=U4):
                    for ch in range(NCH):
                        ins = t.transpose(out=pT4[:, ch * 128:(ch + 1) * 128],
                                          in_=U4[:, ch * 256 + 128:ch * 256 + 256],
                                          identity=cs["identb"][:, :])
                    ph.inc(ins)
                T(trsb)
                def cpsb(v):
                    for ch in range(NCH):
                        ins = v.tensor_copy(out=UWc4[:, ch * 128:(ch + 1) * 128],
                                            in_=pT4[:, ch * 128:(ch + 1) * 128])
                    ph.inc(ins)
                V(cpsb)
                def trkb(t):
                    for ch in range(NCH):
                        ins = t.transpose(out=pT4[:, ch * 128:(ch + 1) * 128],
                                          in_=KD[:, ch * 128:(ch + 1) * 128],
                                          identity=cs["identb"][:, :])
                    ph.inc(ins)
                T(trkb)
                def cpkb(v):
                    for ch in range(NCH):
                        ins = v.tensor_copy(out=KDt4[:, ch * 128:(ch + 1) * 128],
                                            in_=pT4[:, ch * 128:(ch + 1) * 128])
                    ph.inc(ins)
                V(cpkb)
                for ch in range(NCH):
                    c0 = ch * 128
                    T(lambda t, ht=ht, c0=c0: ph.inc(t.matmul(out=pSolve[:, 0:128],
                                                              lhsT=UWc4[:, c0:c0 + 128],
                                                              rhs=S_b[ht][:, :],
                                                              start=True, stop=True)))
                    def useqf(v, U4=U4, ch=ch, ht=ht):
                        v.tensor_copy(out=xdel[:, :], in_=pSolve[:, 0:128])
                        v.engine_nop()
                        v.tensor_sub(out=spb[:, 0:128], in0=U4[:, ch * 256:ch * 256 + 128],
                                     in1=xdel[:, :])
                        v.engine_nop()
                        ph.inc(v.tensor_scalar_mul(out=useq[:, :], in0=spb[:, 0:128],
                                                   scalar1=beta_f[:, ch * 2 + ht:ch * 2 + ht + 1]))
                    V(useqf)
                    def omm3(t, c0=c0, ht=ht):
                        t.matmul(out=pSolve[:, 256:384], lhsT=Qs[:, c0:c0 + 128],
                                 rhs=S_b[ht][:, :], start=True, stop=False)
                        t.matmul(out=pSolve[:, 256:384], lhsT=Aqt4[:, c0:c0 + 128],
                                 rhs=useq[:, :], start=False, stop=True)
                        ph.inc(t.matmul(out=pSolve[:, 128:256],
                                        lhsT=KDt4[:, c0:c0 + 128],
                                        rhs=useq[:, :], start=True, stop=True))
                    T(omm3)
                    def supdf(v, ch=ch, ht=ht):
                        v.scalar_tensor_tensor(out=S_f[ht][:, :], in0=S_f[ht][:, :],
                                               scalar=eGend[:, ch:ch + 1],
                                               in1=pSolve[:, 128:256],
                                               op0=ALU.mult, op1=ALU.add)
                        v.tensor_copy(out=S_b[ht][:, :], in_=S_f[ht][:, :])
                        ph.inc(v.tensor_copy(out=sp1[:, c0:c0 + 128],
                                             in_=pSolve[:, 256:384]))
                    V(supdf)

                def episq(s):
                    for ch in range(NCH):
                        ins = s.activation(out=spb[:, ch * 128:(ch + 1) * 128],
                                           in_=sp1[:, ch * 128:(ch + 1) * 128],
                                           func=AF.Square, accum_out=ssq[:, ch:ch + 1])
                    ph.inc(ins)
                SC(episq)
                V(lambda v: ph.inc(v.tensor_scalar(out=ssq[:, 4:8], in0=ssq[:, 0:4],
                                                   scalar1=1.0 / 128, scalar2=1e-5,
                                                   op0=ALU.mult, op1=ALU.add)))
                SC(lambda s: ph.inc(s.activation(out=ssq[:, 8:12], in_=ssq[:, 4:8],
                                                 func=AF.Sqrt)))
                V(lambda v: ph.inc(v.reciprocal(out=ssq[:, 8:12], in_=ssq[:, 8:12])))
                def ofinb(v, ht=ht):
                    for ch in range(NCH):
                        v.tensor_scalar_mul(out=spb2[:, ch * 128:(ch + 1) * 128],
                                            in0=sp1[:, ch * 128:(ch + 1) * 128],
                                            scalar1=ssq[:, 8 + ch:9 + ch])
                    v.engine_nop()
                    for ch in range(NCH):
                        dst = obuf[:, ch * 256 + ht * 128:ch * 256 + ht * 128 + 128]
                        ins = v.tensor_mul(out=dst, in0=spb2[:, ch * 128:(ch + 1) * 128],
                                           in1=gate_tm[:, ch * 256 + ht * 128:
                                                       ch * 256 + ht * 128 + 128])
                    ph.inc(ins)
                V(ofinb)

            for fc in range(2):
                def trob(t, fc=fc):
                    for ch in range(NCH):
                        ins = t.transpose(
                            out=pT4[:, ch * 128:(ch + 1) * 128],
                            in_=obuf[:, ch * 256 + fc * 128:ch * 256 + fc * 128 + 128],
                            identity=cs["identb"][:, :])
                    ph.inc(ins)
                T(trob)
                def cpob(v, fc=fc):
                    for ch in range(NCH):
                        ins = v.tensor_scalar_mul(
                            out=ofcm[:, fc * GT + ch * 128:fc * GT + (ch + 1) * 128],
                            in0=pT4[:, ch * 128:(ch + 1) * 128], scalar1=onws[:, 0:1])
                    ph.inc(ins)
                V(cpob)
            dstb = (g * GT) // TPC
            cb0 = (g * GT) % TPC
            for jp in range(NFC // 2):
                def pmm(t, jp=jp):
                    for q in range(2):
                        jo = jp * 2 + q
                        pdst = pA[:, 0:GT] if q == 0 else pScoreB[:, 0:GT]
                        t.matmul(out=pdst,
                                 lhsT=ows2[:, 0 * D + jo * 128:0 * D + jo * 128 + 128],
                                 rhs=ofcm[:, 0:GT], start=True, stop=False)
                        ins = t.matmul(out=pdst,
                                       lhsT=ows2[:, 1 * D + jo * 128:1 * D + jo * 128 + 128],
                                       rhs=ofcm[:, GT:2 * GT], start=False, stop=True)
                    ph.inc(ins)
                T(pmm)
                def cppat(v):
                    v.tensor_copy(out=patb2[:, 0:GT], in_=pA[:, 0:GT])
                    ph.inc(v.tensor_copy(out=patb2[:, GT:2 * GT], in_=pScoreB[:, 0:GT]))
                V(cppat)
                def stp(gq, jp=jp, dstb=dstb, cb0=cb0):
                    for q in range(2):
                        jo = jp * 2 + q
                        ins = gq.dma_start(
                            out=pattn[dstb * D + jo * 128:dstb * D + (jo + 1) * 128,
                                      cb0:cb0 + GT],
                            in_=patb2[:, q * GT:(q + 1) * GT])
                        ph.inc(ins, 16)
                G3(stp)

        @block.gpsimd
        def _(gq):
            ph.wait(gq)


def emit_mid_f(C, nc, attn_sl, xns, xs, x1o, xn1o):
    debug = False
    TT = 512
    NT = TPC // TT
    E, I = 8, 1408
    NI = I // 128
    rout = nc.dram_tensor("rout", [D, 8], F32, kind="ExternalInput").ap()
    wug = nc.dram_tensor("wug", [E, D, 2 * I], F32, kind="ExternalInput").ap()
    wdn = nc.dram_tensor("wdn", [E, I, D], F32, kind="ExternalInput").ap()
    swu = nc.dram_tensor("swu", [D, 2 * 5632], F32, kind="ExternalInput").ap()
    swd = nc.dram_tensor("swd", [5632, D], F32, kind="ExternalInput").ap()
    identf_in = nc.dram_tensor("identf_m", [128, 128], F32, kind="ExternalInput").ap()
    esel_in = nc.dram_tensor("esel", [8, 8 * 128], BF16, kind="ExternalInput").ap()
    if debug:
        dbg_w8 = nc.dram_tensor("dbg_w8", [8, TPC], F32, kind="ExternalOutput").ap()
        dbg_y = nc.dram_tensor("dbg_y", [128, TPC], F32, kind="ExternalOutput").ap()
        dbg_lg = nc.dram_tensor("dbg_lg", [8, TPC], F32, kind="ExternalOutput").ap()
        dbg_h = nc.dram_tensor("dbg_h", [128, TPC], F32, kind="ExternalOutput").ap()

    with ExitStack() as st:
        def sbuf(n, sh, dt=F32):
            return st.enter_context(nc.sbuf_tensor("m_" + n, sh, dt))
        ph = C.ph
        block = C.block

        def T(fn):
            @block.tensor
            def _(t):
                ph.wait(t)
                fn(t)
        def V(fn):
            @block.vector
            def _(v):
                ph.wait(v)
                fn(v)
        def SC(fn):
            @block.scalar
            def _(s):
                ph.wait(s)
                fn(s)
        def SY(fn):
            @block.sync
            def _(sy):
                ph.wait(sy)
                fn(sy)
        def GQ(fn):
            @block.gpsimd
            def _(g):
                ph.wait(g)
                fn(g)

        y_sb = sbuf("y_sb", [128, NFC * TPC], BF16)
        idf = sbuf("idf", [128, 128], F32)
        onesc = sbuf("onesc", [1, 128], F32)
        onecol = sbuf("onecol", [128, 1], F32)
        epsc = sbuf("epsc", [1, 1], F32)
        gt2 = [sbuf(f"gtile{i}", [128, TT], BF16) for i in range(2)]
        util = sbuf("util", [128, TT], BF16)
        wbt = sbuf("wbt", [128, TT], BF16)
        pA6 = [st.enter_context(nc.psum_tensor(f"m_pA{i}", [128, TT], F32))
               for i in range(6)]
        pL8 = st.enter_context(nc.psum_tensor("m_pL8", [8, TT], F32))
        s_w = st.enter_context(nc.semaphore("m_sw"))
        s_t = st.enter_context(nc.semaphore("m_st"))
        s_sc = st.enter_context(nc.semaphore("m_ssc"))
        s_v = st.enter_context(nc.semaphore("m_sv"))
        s_wd = st.enter_context(nc.semaphore("m_swd"))
        s_dt = st.enter_context(nc.semaphore("m_sdt"))
        s_dv = st.enter_context(nc.semaphore("m_sdv"))

        def setup(v):
            v.memset(onesc[:, :], 1.0)
            v.memset(onecol[:, :], 1.0)
            v.memset(epsc[:, :], 1e-5)
            for j in range(NFC):
                v.memset(y_sb[:, j * TPC:(j + 1) * TPC], 0.0)
            ph.inc(v.engine_nop())
        V(setup)

        with ExitStack() as stB:
            xnsb = stB.enter_context(nc.sbuf_tensor("m_xnsb", [128, NFC * TPC], BF16))
            wga0 = stB.enter_context(nc.sbuf_tensor("m_wga0", [128, NFC * 128], BF16))
            wgb0 = stB.enter_context(nc.sbuf_tensor("m_wgb0", [128, NFC * 128], BF16))
            wga1 = stB.enter_context(nc.sbuf_tensor("m_wga1", [128, NFC * 128], BF16))
            wgb1 = stB.enter_context(nc.sbuf_tensor("m_wgb1", [128, NFC * 128], BF16))
            wdb_a = stB.enter_context(nc.sbuf_tensor("m_wdb", [128, NI * 128], BF16))
            wdb_b = stB.enter_context(nc.sbuf_tensor("m_wdb2", [128, NI * 128], BF16))
            w8cm = stB.enter_context(nc.sbuf_tensor("m_w8cm", [8, TPC], BF16))
            esel = stB.enter_context(nc.sbuf_tensor("m_esel_sb", [8, 8 * 128], BF16))
            stR = stB.enter_context(ExitStack())
            routs = stR.enter_context(nc.sbuf_tensor("m_routs", [128, NFC * 8], BF16))
            lgt = stR.enter_context(nc.sbuf_tensor("m_lgt", [128, 8], F32))
            msk8 = stR.enter_context(nc.sbuf_tensor("m_msk8", [128, 8], F32))
            negbig = stR.enter_context(nc.sbuf_tensor("m_negbig", [128, 1], F32))
            lg2 = stR.enter_context(nc.sbuf_tensor("m_lg2", [128, 8], F32))
            sm = stR.enter_context(nc.sbuf_tensor("m_sm", [128, 6], F32))
            lgcm = stR.enter_context(nc.sbuf_tensor("m_lgcm", [8, TPC], F32))
            def ldxn(sy):
                for j in range(NFC):
                    ph.inc(sy.dma_start(out=xnsb[:, j * TPC:(j + 1) * TPC],
                                        in_=xns[j * 128:(j + 1) * 128, :]), 16)
            SY(ldxn)
            def ldid(sy):
                ph.inc(sy.dma_start(out=idf[:, :], in_=identf_in), 16)
            SY(ldid)
            def ldrt(g):
                src = rout.rearrange("(j p) n -> p j n", p=128)
                dst = routs[:, :].rearrange("p (j n) -> p j n", n=8)
                ph.inc(g.dma_start(out=dst, in_=src), 16)
            GQ(ldrt)
            SY(lambda sy: ph.inc(sy.dma_start(out=esel[:, :], in_=esel_in), 16))
            V(lambda v: ph.inc(v.memset(negbig[:, :], -1e30)))
            for tt in range(NT):
                def mmlg(t, tt=tt):
                    for j in range(NFC):
                        ins = t.matmul(out=pL8[:, :], lhsT=routs[:, j * 8:(j + 1) * 8],
                                       rhs=xnsb[:, j * TPC + tt * TT:j * TPC + (tt + 1) * TT],
                                       start=(j == 0), stop=(j == NFC - 1))
                    ph.inc(ins)
                T(mmlg)
                V(lambda v, tt=tt: ph.inc(v.tensor_copy(out=lgcm[:, tt * TT:(tt + 1) * TT],
                                                  in_=pL8[:, :])))
                for cq in range(TT // 128):
                    def trlg(t, tt=tt, cq=cq):
                        ph.inc(t.transpose(
                            out=pA6[4][:, 0:8],
                            in_=lgcm[:, tt * TT + cq * 128:tt * TT + (cq + 1) * 128],
                            identity=idf[0:8, 0:8]))
                    T(trlg)
                    V(lambda v: ph.inc(v.tensor_copy(out=lgt[:, :], in_=pA6[4][:, 0:8])))
                    V(lambda v: ph.inc(v.reduce_max(out=sm[:, 0:1], in_=lgt[:, :],
                                                    axis=mybir.AxisListType.X)))
                    V(lambda v: ph.inc(v.tensor_scalar(out=msk8[:, :], in0=lgt[:, :],
                                                       scalar1=sm[:, 0:1], scalar2=None,
                                                       op0=ALU.is_ge)))
                    V(lambda v: ph.inc(v.scalar_tensor_tensor(out=lg2[:, :], in0=msk8[:, :],
                                                              scalar=negbig[:, 0:1],
                                                              in1=lgt[:, :], op0=ALU.mult,
                                                              op1=ALU.add)))
                    V(lambda v: ph.inc(v.reduce_max(out=sm[:, 1:2], in_=lg2[:, :],
                                                    axis=mybir.AxisListType.X)))
                    V(lambda v: ph.inc(v.tensor_sub(out=sm[:, 2:3], in0=sm[:, 1:2],
                                                    in1=sm[:, 0:1])))
                    V(lambda v: ph.inc(v.tensor_scalar_mul(out=sm[:, 3:4], in0=sm[:, 0:1],
                                                           scalar1=-1.0)))
                    def top2b(s):
                        s.activation(out=lg2[:, :], in_=lgt[:, :], func=AF.Exp, bias=sm[:, 3:4])
                        ph.inc(s.activation(out=sm[:, 4:5], in_=sm[:, 2:3], func=AF.Exp))
                    SC(top2b)
                    V(lambda v: ph.inc(v.tensor_scalar_add(out=sm[:, 4:5], in0=sm[:, 4:5],
                                                           scalar1=1.0)))
                    V(lambda v: ph.inc(v.reciprocal(out=sm[:, 5:6], in_=sm[:, 4:5])))
                    V(lambda v: ph.inc(v.tensor_scalar(out=lg2[:, :], in0=lg2[:, :],
                                                       scalar1=sm[:, 5:6], scalar2=None,
                                                       op0=ALU.mult)))
                    V(lambda v: ph.inc(v.tensor_scalar(out=lgt[:, :], in0=lgt[:, :],
                                                       scalar1=sm[:, 1:2], scalar2=None,
                                                       op0=ALU.is_ge)))
                    V(lambda v: ph.inc(v.tensor_mul(out=lg2[:, :], in0=lg2[:, :],
                                                    in1=lgt[:, :])))
                    def trw(t):
                        ph.inc(t.transpose(out=pA6[4][0:8, 0:128], in_=lg2[:, :],
                                           identity=idf[:, :]))
                    T(trw)
                    V(lambda v, tt=tt, cq=cq: ph.inc(
                        v.tensor_copy(out=w8cm[:, tt * TT + cq * 128:tt * TT + (cq + 1) * 128],
                                      in_=pA6[4][0:8, 0:128])))

            if debug:
                def dmpw(g):
                    for tt in range(NT):
                        ph.inc(g.dma_start(out=dbg_w8[:, tt * TT:(tt + 1) * TT].bitcast(BF16)[:, 0:TT] if False else dbg_w8[:, tt * TT:(tt + 1) * TT],
                                           in_=w8cm[:, tt * TT:(tt + 1) * TT]), 16)
                    ph.inc(g.dma_start(out=dbg_lg[:, :], in_=lgcm[:, :]), 16)
                GQ(dmpw)
            stR.close()
            hprime = stB.enter_context(nc.sbuf_tensor("m_hprime", [128, NI * TPC], BF16))
            passes = [("e", e) for e in range(E)] + [("s", g_) for g_ in range(4)]
            UPT = NI * NT
            DNT = NFC * NT
            wgp = [(wga0, wgb0), (wga1, wgb1)]
            wdp = [wdb_a, wdb_b]
            for pidx, (kind, ei) in enumerate(passes):
                for icp in range(NI):
                    wl = pidx * NI + icp

                    @block.gpsimd
                    def _(g, kind=kind, ei=ei, icp=icp, wl=wl):
                        if wl == 0:
                            ph.wait(g)
                        if wl >= 2:
                            g.wait_ge(s_t, 4 * (wl - 1))
                        if kind == "e":
                            gsrc = wug[ei, :, icp * 128:(icp + 1) * 128]
                            usrc = wug[ei, :, I + icp * 128:I + (icp + 1) * 128]
                        else:
                            base = ei * NI + icp
                            gsrc = swu[:, base * 128:(base + 1) * 128]
                            usrc = swu[:, 5632 + base * 128:5632 + (base + 1) * 128]
                        wga, wgb = wgp[wl % 2]
                        g.dma_start(out=wga[:, :].rearrange("p (j n) -> p j n", n=128),
                                    in_=gsrc.rearrange("(j p) n -> p j n", p=128)
                                    ).then_inc(s_w, 16)
                        g.dma_start(out=wgb[:, :].rearrange("p (j n) -> p j n", n=128),
                                    in_=usrc.rearrange("(j p) n -> p j n", p=128)
                                    ).then_inc(s_w, 16)
                    for tt in range(NT):
                        k = pidx * UPT + icp * NT + tt
                        q = k % 2

                        @block.tensor
                        def _(t, kind=kind, ei=ei, tt=tt, wl=wl, k=k, q=q, pidx=pidx):
                            if k == 0:
                                ph.wait(t)
                            t.wait_ge(s_w, 32 * (wl + 1))
                            if k >= 2:
                                t.wait_ge(s_v, k - 1)
                            if pidx >= 1:
                                t.wait_ge(s_dv, 64 * pidx)
                            wga, wgb = wgp[wl % 2]
                            for j in range(NFC):
                                t.matmul(out=pA6[2 * q][:, :],
                                         lhsT=wga[:, j * 128:(j + 1) * 128],
                                         rhs=xnsb[:, j * TPC + tt * TT:j * TPC + (tt + 1) * TT],
                                         start=(j == 0), stop=(j == NFC - 1))
                            for j in range(NFC):
                                ins = t.matmul(out=pA6[2 * q + 1][:, :],
                                               lhsT=wgb[:, j * 128:(j + 1) * 128],
                                               rhs=xnsb[:, j * TPC + tt * TT:
                                                        j * TPC + (tt + 1) * TT],
                                               start=(j == 0), stop=(j == NFC - 1))
                            if kind == "e":
                                ins = t.matmul(out=pA6[4 + q][:, :],
                                               lhsT=esel[:, ei * 128:(ei + 1) * 128],
                                               rhs=w8cm[0:8, tt * TT:(tt + 1) * TT],
                                               start=True, stop=True)
                            ins.then_inc(s_t, 1)

                        @block.scalar
                        def _(s, k=k, q=q):
                            if k == 0:
                                ph.wait(s)
                            s.wait_ge(s_t, k + 1)
                            if k >= 2:
                                s.wait_ge(s_v, k - 1)
                            s.activation(out=gt2[q][:, :], in_=pA6[2 * q][:, :],
                                         func=AF.Silu).then_inc(s_sc, 1)

                        @block.vector
                        def _(v, kind=kind, icp=icp, tt=tt, k=k, q=q, pidx=pidx):
                            if k == 0:
                                ph.wait(v)
                            v.wait_ge(s_sc, k + 1)
                            if pidx >= 1:
                                v.wait_ge(s_dt, 64 * pidx)
                            v.tensor_copy(out=util[:, :], in_=pA6[2 * q + 1][:, :])
                            if kind == "e":
                                v.tensor_copy(out=wbt[:, :], in_=pA6[4 + q][:, :])
                            v.engine_nop()
                            dst = hprime[:, icp * TPC + tt * TT:icp * TPC + (tt + 1) * TT]
                            if kind == "e":
                                v.tensor_mul(out=gt2[q][:, :], in0=gt2[q][:, :],
                                             in1=util[:, :])
                                v.tensor_mul(out=dst, in0=gt2[q][:, :],
                                             in1=wbt[:, :]).then_inc(s_v, 1)
                            else:
                                v.tensor_mul(out=dst, in0=gt2[q][:, :],
                                             in1=util[:, :]).then_inc(s_v, 1)
                for jo in range(NFC):
                    dl = pidx * NFC + jo

                    @block.gpsimd
                    def _(g, kind=kind, ei=ei, jo=jo, dl=dl):
                        if dl >= 2:
                            g.wait_ge(s_dt, 4 * (dl - 1))
                        if kind == "e":
                            src = wdn[ei, :, jo * 128:(jo + 1) * 128]
                        else:
                            src = swd[ei * I:(ei + 1) * I, jo * 128:(jo + 1) * 128]
                        wdb = wdp[dl % 2]
                        g.dma_start(out=wdb[:, :].rearrange("p (j n) -> p j n", n=128),
                                    in_=src.rearrange("(j p) n -> p j n", p=128)
                                    ).then_inc(s_wd, 16)
                    for tt in range(NT):
                        m = pidx * DNT + jo * NT + tt

                        @block.tensor
                        def _(t, tt=tt, dl=dl, m=m, pidx=pidx):
                            t.wait_ge(s_wd, 16 * (dl + 1))
                            if m >= 4:
                                t.wait_ge(s_dv, m - 3)
                            t.wait_ge(s_v, UPT * (pidx + 1))
                            t.wait_ge(s_sc, UPT * (pidx + 1))
                            wdb = wdp[dl % 2]
                            for ic in range(NI):
                                ins = t.matmul(out=pA6[m % 4][:, :],
                                               lhsT=wdb[:, ic * 128:(ic + 1) * 128],
                                               rhs=hprime[:, ic * TPC + tt * TT:
                                                          ic * TPC + (tt + 1) * TT],
                                               start=(ic == 0), stop=(ic == NI - 1))
                            ins.then_inc(s_dt, 1)

                        @block.vector
                        def _(v, jo=jo, tt=tt, m=m):
                            v.wait_ge(s_dt, m + 1)
                            v.tensor_copy(out=util[:, :], in_=pA6[m % 4][:, :])
                            v.engine_nop()
                            dst = y_sb[:, jo * TPC + tt * TT:jo * TPC + (tt + 1) * TT]
                            v.tensor_add(out=dst, in0=dst, in1=util[:, :]).then_inc(s_dv, 1)

            @block.vector
            def _(v):
                ph.inc(v.engine_nop())

            if debug:
                def dmph(g):
                    ph.inc(g.dma_start(out=dbg_h[:, :], in_=hprime[:, 0:TPC]), 16)
                GQ(dmph)
        if debug:
            def dmpy(g):
                ph.inc(g.dma_start(out=dbg_y[:, :], in_=y_sb[:, 0:TPC]), 16)
            GQ(dmpy)
        with nc.sbuf_tensor("m_x1b", [128, NFC * TT], F32) as x1b, \
             nc.sbuf_tensor("m_sqf", [128, TT], F32) as sqf, \
             nc.sbuf_tensor("m_statm", [1, 6 * TT], F32) as statm, \
             nc.sbuf_tensor("m_xn1b", [128, TT], BF16) as xn1b, \
             nc.sbuf_tensor("m_f32t", [128, TT], F32) as f32t, \
             nc.sbuf_tensor("m_atb", [128, TT], BF16) as atb:
            for tt in range(NT):
                def ldx(sy, tt=tt):
                    for j in range(NFC):
                        ph.inc(sy.dma_start(out=x1b[:, j * TT:(j + 1) * TT],
                                            in_=xs[j * 128:(j + 1) * 128,
                                                   tt * TT:(tt + 1) * TT]), 16)
                SY(ldx)
                for j in range(NFC):
                    def ldat(sy, tt=tt, j=j):
                        ph.inc(sy.dma_start(out=atb[:, :],
                                            in_=attn_sl[j * 128:(j + 1) * 128,
                                                        tt * TT:(tt + 1) * TT]), 16)
                    SY(ldat)
                    V(lambda v, j=j: ph.inc(v.tensor_copy(out=f32t[:, :], in_=atb[:, :])))
                    V(lambda v, j=j: ph.inc(v.tensor_add(out=x1b[:, j * TT:(j + 1) * TT],
                                                         in0=x1b[:, j * TT:(j + 1) * TT],
                                                         in1=f32t[:, :])))
                    V(lambda v, tt=tt, j=j: ph.inc(
                        v.tensor_copy(out=f32t[:, :],
                                      in_=y_sb[:, j * TPC + tt * TT:j * TPC + (tt + 1) * TT])))
                    V(lambda v, j=j: ph.inc(v.tensor_add(out=x1b[:, j * TT:(j + 1) * TT],
                                                         in0=x1b[:, j * TT:(j + 1) * TT],
                                                         in1=f32t[:, :])))
                def stx1(g, tt=tt):
                    for j in range(NFC):
                        ph.inc(g.dma_start(out=x1o[j * 128:(j + 1) * 128,
                                                   tt * TT:(tt + 1) * TT],
                                           in_=x1b[:, j * TT:(j + 1) * TT]), 16)
                GQ(stx1)
                def mmsum(t, tt=tt):
                    for j in range(NFC):
                        ins = t.matmul(out=pA6[0][0:1, :], lhsT=onecol[:, 0:1],
                                       rhs=x1b[:, j * TT:(j + 1) * TT],
                                       start=(j == 0), stop=(j == NFC - 1))
                    ph.inc(ins)
                T(mmsum)
                for j in range(NFC):
                    SC(lambda s, j=j: ph.inc(s.activation(out=sqf[:, :],
                                                          in_=x1b[:, j * TT:(j + 1) * TT],
                                                          func=AF.Square)))
                    def mmsq(t, j=j):
                        ph.inc(t.matmul(out=pA6[1][0:1, :], lhsT=onecol[:, 0:1], rhs=sqf[:, :],
                                        start=(j == 0), stop=(j == NFC - 1)))
                    T(mmsq)
                def stats(v):
                    v.tensor_scalar_mul(out=statm[:, 0 * TT:1 * TT], in0=pA6[0][0:1, :], scalar1=1.0 / D)
                    v.tensor_scalar_mul(out=statm[:, 1 * TT:2 * TT], in0=pA6[1][0:1, :], scalar1=1.0 / D)
                    v.tensor_mul(out=statm[:, 2 * TT:3 * TT], in0=statm[:, 0 * TT:1 * TT],
                                 in1=statm[:, 0 * TT:1 * TT])
                    ph.inc(v.tensor_sub(out=statm[:, 3 * TT:4 * TT], in0=statm[:, 1 * TT:2 * TT],
                                        in1=statm[:, 2 * TT:3 * TT]))
                V(stats)
                SC(lambda s: ph.inc(s.activation(out=statm[:, 4 * TT:5 * TT],
                                                 in_=statm[:, 3 * TT:4 * TT],
                                                 func=AF.Sqrt, bias=epsc[0:1, 0:1])))
                V(lambda v: ph.inc(v.reciprocal(out=statm[:, 5 * TT:6 * TT],
                                                in_=statm[:, 4 * TT:5 * TT])))
                def bcst(t):
                    t.matmul(out=pA6[2][:, :], lhsT=onesc[:, :], rhs=statm[:, 0 * TT:1 * TT],
                             start=True, stop=True)
                    ph.inc(t.matmul(out=pA6[3][:, :], lhsT=onesc[:, :],
                                    rhs=statm[:, 5 * TT:6 * TT], start=True, stop=True))
                T(bcst)
                for j in range(NFC):
                    def norm(v, j=j):
                        v.tensor_sub(out=sqf[:, :], in0=x1b[:, j * TT:(j + 1) * TT],
                                     in1=pA6[2][:, :])
                        ph.inc(v.tensor_mul(out=xn1b[:, :], in0=sqf[:, :], in1=pA6[3][:, :]))
                    V(norm)
                    def stxn(g, tt=tt, j=j):
                        ph.inc(g.dma_start(out=xn1o[j * 128:(j + 1) * 128,
                                                    tt * TT:(tt + 1) * TT],
                                           in_=xn1b[:, :]), 16)
                    GQ(stxn)

        @block.gpsimd
        def _(g):
            ph.wait(g)


def emit_l1attn_f(C, nc, xn1_g, pattn2):
    core = 0
    S = 4096
    NB = 4
    NQT = S // 128
    wq = nc.dram_tensor("wq1", [D, 256], F32, kind="ExternalInput").ap()
    wk = nc.dram_tensor("wk1", [D, 64], F32, kind="ExternalInput").ap()
    wv = nc.dram_tensor("wv1", [D, 64], F32, kind="ExternalInput").ap()
    CST = l1attn_consts(core)
    cin = {}
    for k, v in CST.items():
        dt = BF16 if v.dtype == BF16NP else F32
        cin[k] = nc.dram_tensor(k, list(v.shape), dt, kind="ExternalInput").ap()
    wop = nc.dram_tensor("wop", [256, D], F32, kind="ExternalInput").ap()
    qtm_d = nc.dram_tensor("l_qtm_d", [S, 384], BF16, kind="Internal").ap()

    with ExitStack() as st:
        def sbuf(n, sh, dt=F32):
            return st.enter_context(nc.sbuf_tensor("l_" + n, sh, dt))
        ph = C.ph
        block = C.block

        def T(fn):
            @block.tensor
            def _(t):
                ph.wait(t)
                fn(t)
        def V(fn):
            @block.vector
            def _(v):
                ph.wait(v)
                fn(v)
        def SC(fn):
            @block.scalar
            def _(s):
                ph.wait(s)
                fn(s)
        def SY(fn):
            @block.sync
            def _(sy):
                ph.wait(sy)
                fn(sy)
        def GQ(fn):
            @block.gpsimd
            def _(g):
                ph.wait(g)
                fn(g)

        wqs = sbuf("wqs", [128, NFC * 256], BF16)
        wkv = sbuf("wkv", [128, NFC * 128], BF16)
        cs = {}
        for k, v in CST.items():
            dt = BF16 if v.dtype == BF16NP else F32
            cs[k] = sbuf("c3_" + k, list(v.shape), dt)
        xnt = sbuf("xnt", [128, NFC * 128], BF16)
        qtmb = sbuf("qtmb", [128, 256], BF16)
        ktmb = sbuf("ktmb", [128, 64], BF16)
        rt1 = sbuf("rt1", [128, 128], BF16)
        rt2 = sbuf("rt2", [128, 128], BF16)
        vaug = sbuf("vaug", [128, NQT * 65], BF16)
        q4cm = sbuf("q4cm", [64, 4 * S], BF16)
        qx2 = sbuf("qx2", [128, S], BF16)
        kcmx = sbuf("kcmx", [128, S], BF16)
        st1 = sbuf("st1", [128, 512], F32)
        ptile = sbuf("ptile", [128, 512], BF16)
        neg30 = sbuf("neg30", [128, 1], F32)
        ctxb = sbuf("ctxb", [65, 512], BF16)
        rnb = sbuf("rnb", [64, 512], BF16)
        rsum = sbuf("rsum", [1, 512], BF16)
        ones64 = sbuf("ones64", [1, 64], BF16)
        ctxf = sbuf("ctxf", [64, 512], BF16)
        ctx2g = sbuf("ctx2g", [128, 2 * 512], BF16)
        wos2 = sbuf("wos2", [128, 2 * D], BF16)
        pat2 = sbuf("pat2", [128, 512], BF16)
        pPt = st.enter_context(nc.psum_tensor("l_pPt", [128, 512], F32))
        pP = st.enter_context(nc.psum_tensor("l_pP", [128, 256], F32))
        pKV = st.enter_context(nc.psum_tensor("l_pKV", [128, 128], F32))
        pS2 = [st.enter_context(nc.psum_tensor(f"l_pSc{i}", [128, 512], F32))
               for i in range(2)]
        pC2 = [st.enter_context(nc.psum_tensor(f"l_pCtx{i}", [65, 512], F32))
               for i in range(2)]
        pBc = st.enter_context(nc.psum_tensor("l_pBc", [64, 512], F32))
        pt2 = [sbuf(f"ptile{i}", [128, 512], BF16) for i in range(2)]
        pat2p = [sbuf(f"pat2_{i}", [128, 512], BF16) for i in range(2)]
        a_t = st.enter_context(nc.semaphore("l_at"))
        a_sc = st.enter_context(nc.semaphore("l_asc"))
        a_v1 = st.enter_context(nc.semaphore("l_av1"))
        a_pv = st.enter_context(nc.semaphore("l_apv"))
        a_cv = st.enter_context(nc.semaphore("l_acv"))
        a_nv = st.enter_context(nc.semaphore("l_anv"))
        a_nt = st.enter_context(nc.semaphore("l_ant"))
        a_av = st.enter_context(nc.semaphore("l_aav"))
        a_pt = st.enter_context(nc.semaphore("l_apt"))
        a_pc = st.enter_context(nc.semaphore("l_apc"))
        a_ps = st.enter_context(nc.semaphore("l_aps"))

        def ld0(g):
            srcw = wop.rearrange("(j p) n -> p j n", p=128)
            dstw = wos2[:, :].rearrange("p (j n) -> p j n", n=D)
            ph.inc(g.dma_start(out=dstw, in_=srcw), 16)
            src = wq.rearrange("(j p) n -> p j n", p=128)
            ph.inc(g.dma_start(out=wqs[:, :].rearrange("p (j n) -> p j n", n=256), in_=src), 16)
            ph.inc(g.dma_start(out=wkv[:, :].rearrange("p (j n) -> p j n", n=128)[:, :, 0:64],
                               in_=wk.rearrange("(j p) n -> p j n", p=128)), 16)
            ph.inc(g.dma_start(out=wkv[:, :].rearrange("p (j n) -> p j n", n=128)[:, :, 64:128],
                               in_=wv.rearrange("(j p) n -> p j n", p=128)), 16)
        GQ(ld0)
        def ld1(sy):
            for k in CST:
                ph.inc(sy.dma_start(out=cs[k][:, :], in_=cin[k]), 16)
        SY(ld1)
        def st0(v):
            v.memset(neg30[:, :], -30.0)
            ph.inc(v.memset(ones64[:, :], 1.0))
        V(st0)

        PIC = [0]
        PJC = [0]
        PBASE = [bb * (NQT * (NQT + 1) // 2) for bb in range(NB)]
        for b in range(NB):
            if b >= 1:
                @block.gpsimd
                def _(g, b=b):
                    g.wait_ge(a_t, PBASE[b])
                    g.wait_ge(a_av, NQT * b)
                    ph.inc(g.engine_nop())
            for qt in range(NQT):
                tok0 = b * S + qt * 128
                def ldxn(sy, tok0=tok0):
                    rk = tok0 // TPC
                    cb = tok0 % TPC
                    for j in range(NFC):
                        ph.inc(sy.dma_start(out=xnt[:, j * 128:(j + 1) * 128],
                                            in_=xn1_g[rk * D + j * 128:rk * D + (j + 1) * 128,
                                                      cb:cb + 128]), 16)
                SY(ldxn)
                def mmq(t):
                    for j in range(NFC):
                        t.matmul(out=pP[:, :], lhsT=xnt[:, j * 128:(j + 1) * 128],
                                 rhs=wqs[:, j * 256:(j + 1) * 256],
                                 start=(j == 0), stop=(j == NFC - 1))
                    for j in range(NFC):
                        ins = t.matmul(out=pKV[:, :], lhsT=xnt[:, j * 128:(j + 1) * 128],
                                       rhs=wkv[:, j * 128:(j + 1) * 128],
                                       start=(j == 0), stop=(j == NFC - 1))
                    ph.inc(ins)
                T(mmq)
                def ropeq(v, qt=qt):
                    _lp = st.enter_context(nc.allow_low_precision(
                        reason="bf16 rope elementwise, blessed tolerance")) if False else None
                    cosb = cs["cos_t"][:, qt * 32:(qt + 1) * 32]
                    sinb = cs["sin_t"][:, qt * 32:(qt + 1) * 32]
                    q3 = pP[:, :].rearrange("p (h d) -> p h d", h=4)
                    o3 = qtmb[:, :].rearrange("p (h d) -> p h d", h=4)
                    c3 = cosb.unsqueeze(1).broadcast_to([128, 4, 32])
                    s3 = sinb.unsqueeze(1).broadcast_to([128, 4, 32])
                    v.tensor_mul(out=o3[:, :, 0::2], in0=q3[:, :, 1::2], in1=s3)
                    v.engine_nop()
                    v.tensor_mul(out=rt1[:, :].rearrange("p (h d) -> p h d", h=4)[:, :, 0:32],
                                 in0=q3[:, :, 0::2], in1=c3)
                    v.engine_nop()
                    v.tensor_sub(out=o3[:, :, 0::2],
                                 in0=rt1[:, :].rearrange("p (h d) -> p h d", h=4)[:, :, 0:32],
                                 in1=o3[:, :, 0::2])
                    v.tensor_mul(out=o3[:, :, 1::2], in0=q3[:, :, 0::2], in1=s3)
                    v.engine_nop()
                    v.tensor_mul(out=rt1[:, :].rearrange("p (h d) -> p h d", h=4)[:, :, 0:32],
                                 in0=q3[:, :, 1::2], in1=c3)
                    v.engine_nop()
                    ph.inc(v.tensor_add(out=o3[:, :, 1::2],
                                        in0=rt1[:, :].rearrange("p (h d) -> p h d", h=4)[:, :, 0:32],
                                        in1=o3[:, :, 1::2]))
                V(ropeq)
                def ropek(v, qt=qt):
                    cosb = cs["cos_t"][:, qt * 32:(qt + 1) * 32]
                    sinb = cs["sin_t"][:, qt * 32:(qt + 1) * 32]
                    k2 = pKV[:, 0:64]
                    o2 = ktmb[:, :]
                    v.tensor_mul(out=o2[:, 0::2], in0=k2[:, 1::2], in1=sinb)
                    v.engine_nop()
                    v.tensor_mul(out=rt2[:, 0:32], in0=k2[:, 0::2], in1=cosb)
                    v.engine_nop()
                    v.tensor_sub(out=o2[:, 0::2], in0=rt2[:, 0:32], in1=o2[:, 0::2])
                    v.tensor_mul(out=o2[:, 1::2], in0=k2[:, 0::2], in1=sinb)
                    v.engine_nop()
                    v.tensor_mul(out=rt2[:, 0:32], in0=k2[:, 1::2], in1=cosb)
                    v.engine_nop()
                    v.tensor_add(out=o2[:, 1::2], in0=rt2[:, 0:32], in1=o2[:, 1::2])
                    v.tensor_copy(out=vaug[:, qt * 65:qt * 65 + 64], in_=pKV[:, 64:128])
                    ph.inc(v.memset(vaug[:, qt * 65 + 64:qt * 65 + 65], 1.0))
                V(ropek)
                def stqk(g, qt=qt):
                    ph.inc(g.dma_start(out=qtm_d[qt * 128:(qt + 1) * 128, 0:256],
                                       in_=qtmb[:, :]), 16)
                    ph.inc(g.dma_start(out=qtm_d[qt * 128:(qt + 1) * 128, 256:320],
                                       in_=ktmb[:, :]), 16)
                GQ(stqk)
            for hp in range(2):
                def trq(sy, hp=hp):
                    ph.inc(sy.dma_start_transpose(out=qx2[:, :],
                                                  in_=qtm_d[:, hp * 128:(hp + 1) * 128]), 16)
                SY(trq)
                def spl(v, hp=hp):
                    v.tensor_copy(out=q4cm[:, (2 * hp) * S:(2 * hp + 1) * S], in_=qx2[0:64, :])
                    ph.inc(v.tensor_copy(out=q4cm[:, (2 * hp + 1) * S:(2 * hp + 2) * S],
                                         in_=qx2[64:128, :]))
                V(spl)
            def trk(sy):
                ph.inc(sy.dma_start_transpose(out=kcmx[:, :],
                                              in_=qtm_d[:, 256:384]), 16)
            SY(trk)
            pending_bcn = []

            def emit_bcn(g2):
                @block.tensor
                def _(t, g2=g2):
                    t.wait_ge(a_nv, g2 + 1)
                    if g2 >= 1:
                        t.wait_ge(a_av, g2)
                    t.matmul(out=pBc[:, :], lhsT=ones64[:, :],
                             rhs=rsum[:, :], start=True,
                             stop=True).then_inc(a_nt, 1)

            def emit_mmsc(qt, kt, pi):
                @block.tensor
                def _(t, qt=qt, kt=kt, pi=pi):
                    if pi == PBASE[b]:
                        ph.wait(t)
                    if pi >= 2:
                        t.wait_ge(a_sc, pi - 1)
                    q3 = q4cm[:, :].rearrange("p (h t) -> p h t", h=4)
                    ps3 = pS2[pi % 2][:, :].rearrange("p (h t) -> p h t", h=4)
                    t.matmul(out=ps3[:, :, :], lhsT=kcmx[0:64, kt * 128:(kt + 1) * 128],
                             rhs=q3[:, :, qt * 128:(qt + 1) * 128], start=True, stop=False)
                    qa3 = cs["qaug"][:, :].rearrange("p (h t) -> p h t", h=4)
                    t.matmul(out=ps3[:, :, :],
                             lhsT=cs["kpos"][:, kt * 128:(kt + 1) * 128],
                             rhs=qa3[:, :, qt * 128:(qt + 1) * 128],
                             start=False, stop=True,
                             skip_group_check=True).then_inc(a_t, 1)

            def emit_capexp(pi):
                @block.scalar
                def _(s, pi=pi):
                    if pi == PBASE[b]:
                        ph.wait(s)
                    s.wait_ge(a_t, pi + 1)
                    if pi >= 2:
                        s.wait_ge(a_pv, pi - 1)
                    s.activation(out=st1[:, :], in_=pS2[pi % 2][:, :], func=AF.Tanh,
                                 scale=1.0 / 240.0)
                    s.activation(out=pt2[pi % 2][:, :], in_=st1[:, :], func=AF.Exp,
                                 scale=30.0, bias=neg30[:, 0:1]).then_inc(a_sc, 1)

            def emit_dmask(pi, gq):
                @block.vector
                def _(v, pi=pi, gq=gq):
                    if gq == 0:
                        ph.wait(v)
                    v.wait_ge(a_sc, pi + 1)
                    p3 = pt2[pi % 2][:, :].rearrange("p (h t) -> p h t", h=4)
                    m3 = cs["causal"][:, :].unsqueeze(1).broadcast_to([128, 4, 128])
                    v.tensor_tensor(out=p3, in0=p3, in1=m3,
                                    op=ALU.mult).then_inc(a_v1, 1)

            def emit_mmpv(qt, kt, pi, gq):
                @block.tensor
                def _(t, qt=qt, kt=kt, pi=pi, gq=gq):
                    t.wait_ge(a_sc, pi + 1)
                    if kt == qt:
                        t.wait_ge(a_v1, gq + 1)
                    if kt == 0 and gq >= 2:
                        t.wait_ge(a_cv, gq - 1)
                    t.matmul(out=pC2[gq % 2][:, :], lhsT=vaug[:, kt * 65:(kt + 1) * 65],
                             rhs=pt2[pi % 2][:, :], start=(kt == 0), stop=(kt == qt),
                             skip_group_check=True).then_inc(a_pv, 1)

            for qt in range(NQT):
                gq = b * NQT + qt
                pi_base = PIC[0]
                prev = None
                for kt in range(qt + 1):
                    pi = pi_base + kt
                    emit_mmsc(qt, kt, pi)
                    if kt == min(3, qt) and pending_bcn:
                        for g2 in pending_bcn:
                            emit_bcn(g2)
                        pending_bcn = []
                    emit_capexp(pi)
                    if kt == qt:
                        emit_dmask(pi, gq)
                    if prev is not None:
                        emit_mmpv(qt, prev, pi_base + prev, gq)
                    prev = kt
                emit_mmpv(qt, prev, pi_base + prev, gq)
                PIC[0] = pi_base + qt + 1

                @block.vector
                def _(v, gq=gq):
                    v.wait_ge(a_pv, PIC[0])
                    v.tensor_copy(out=ctxb[:, :],
                                  in_=pC2[gq % 2][:, :]).then_inc(a_cv, 1)

                @block.vector
                def _(v):
                    v.tensor_copy(out=rsum[:, :], in_=ctxb[64:65, :])

                @block.vector
                def _(v):
                    v.reciprocal(out=rsum[:, :], in_=rsum[:, :]).then_inc(a_nv, 1)

                pending_bcn.append(gq)
                if qt % 4 == 3 or qt == NQT - 1:
                    for g2 in pending_bcn:
                        emit_bcn(g2)
                    pending_bcn = []

                @block.vector
                def _(v, qt=qt, gq=gq):
                    v.wait_ge(a_nt, gq + 1)
                    if PJC[0] > 0:
                        v.wait_ge(a_pt, PJC[0] * NFC)
                    v.tensor_copy(out=rnb[:, :], in_=pBc[:, :])
                    v.engine_nop()
                    v.tensor_mul(out=ctxf[:, :], in0=ctxb[0:64, :], in1=rnb[:, :])
                    v.engine_nop()
                    q4 = qt % 4
                    ins = None
                    for hp in range(2):
                        v.tensor_copy(out=ctx2g[0:64, hp * 512 + q4 * 128:
                                                hp * 512 + (q4 + 1) * 128],
                                      in_=ctxf[:, (2 * hp) * 128:(2 * hp + 1) * 128])
                        ins = v.tensor_copy(out=ctx2g[64:128, hp * 512 + q4 * 128:
                                                      hp * 512 + (q4 + 1) * 128],
                                            in_=ctxf[:, (2 * hp + 1) * 128:(2 * hp + 2) * 128])
                    ins.then_inc(a_av, 1)

                if qt % 4 == 3:
                    tok0 = b * S + (qt - 3) * 128
                    dstb = tok0 // TPC
                    cb0 = tok0 % TPC
                    pc_base = PJC[0] * NFC
                    for jo in range(NFC):
                        @block.tensor
                        def _(t, jo=jo, gq=gq, pc_base=pc_base):
                            if jo == 0:
                                t.wait_ge(a_av, gq + 1)
                            if pc_base + jo >= 1:
                                t.wait_ge(a_pc, pc_base + jo)
                            t.matmul(out=pPt[:, :],
                                     lhsT=wos2[:, 0 * D + jo * 128:0 * D + jo * 128 + 128],
                                     rhs=ctx2g[:, 0:512], start=True, stop=False)
                            t.matmul(out=pPt[:, :],
                                     lhsT=wos2[:, 1 * D + jo * 128:1 * D + jo * 128 + 128],
                                     rhs=ctx2g[:, 512:1024], start=False,
                                     stop=True).then_inc(a_pt, 1)

                        @block.vector
                        def _(v, jo=jo, pc_base=pc_base):
                            v.wait_ge(a_pt, pc_base + jo + 1)
                            if pc_base + jo >= 2:
                                v.wait_ge(a_ps, 16 * (pc_base + jo - 1))
                            v.tensor_copy(out=pat2p[jo % 2][:, :],
                                          in_=pPt[:, :]).then_inc(a_pc, 1)

                        @block.gpsimd
                        def _(g, jo=jo, dstb=dstb, cb0=cb0, pc_base=pc_base):
                            g.wait_ge(a_pc, pc_base + jo + 1)
                            g.dma_start(
                                out=pattn2[dstb * D + jo * 128:dstb * D + (jo + 1) * 128,
                                           cb0:cb0 + 512],
                                in_=pat2p[jo % 2][:, :]).then_inc(a_ps, 16)
                    PJC[0] += 1

        @block.gpsimd
        def _(g):
            g.wait_ge(a_ps, 16 * NFC * (NQT // 4) * NB)
            g.wait_ge(a_av, NQT * NB)
            ph.inc(g.engine_nop())

        @block.gpsimd
        def _(g):
            ph.wait(g)


def emit_fin_f(C, nc, attn2_sl, xn1s, x1s, x2o):
    TT = 512
    NT = TPC // TT
    NIC = 8192 // 128
    bo = nc.dram_tensor("bo", [128, NFC], F32, kind="ExternalInput").ap()
    wg8 = nc.dram_tensor("wg8", [D, 8192], I8, kind="ExternalInput").ap()
    wu8 = nc.dram_tensor("wu8", [D, 8192], I8, kind="ExternalInput").ap()
    wd8 = nc.dram_tensor("wd8", [8192, D], I8, kind="ExternalInput").ap()
    sg = nc.dram_tensor("sg", [128, NIC], F32, kind="ExternalInput").ap()
    su = nc.dram_tensor("su", [128, NIC], F32, kind="ExternalInput").ap()
    sd = nc.dram_tensor("sd", [128, NFC], F32, kind="ExternalInput").ap()

    with ExitStack() as st:
        def sbuf(n, sh, dt=F32):
            return st.enter_context(nc.sbuf_tensor("n_" + n, sh, dt))
        ph = C.ph
        block = C.block
        def T(fn):
            @block.tensor
            def _(t):
                ph.wait(t)
                fn(t)
        def V(fn):
            @block.vector
            def _(v):
                ph.wait(v)
                fn(v)
        def SC(fn):
            @block.scalar
            def _(s):
                ph.wait(s)
                fn(s)
        def SY(fn):
            @block.sync
            def _(sy):
                ph.wait(sy)
                fn(sy)
        def GQ(fn):
            @block.gpsimd
            def _(g):
                ph.wait(g)
                fn(g)

        sgs = sbuf("sgs", [128, NIC], F32)
        sus = sbuf("sus", [128, NIC], F32)
        sds = sbuf("sds", [128, NFC], F32)
        bos = sbuf("bos", [128, NFC], F32)
        f32t = sbuf("f32t", [128, TT], F32)
        f32u = sbuf("f32u", [128, TT], F32)
        gt2f = [sbuf(f"gt{i}", [128, TT], BF16) for i in range(2)]
        at2p = [sbuf(f"at2b{i}", [128, TT], BF16) for i in range(2)]
        x2p = [sbuf(f"x2b{i}", [128, TT], F32) for i in range(2)]
        ut = sbuf("ut", [128, TT], BF16)
        pF6 = [st.enter_context(nc.psum_tensor(f"n_pF{i}", [128, TT], F32))
               for i in range(6)]
        f_sw = st.enter_context(nc.semaphore("n_sw"))
        f_st = st.enter_context(nc.semaphore("n_st"))
        f_ssc = st.enter_context(nc.semaphore("n_ssc"))
        f_sv = st.enter_context(nc.semaphore("n_sv"))
        f_swd = st.enter_context(nc.semaphore("n_swd"))
        f_sdt = st.enter_context(nc.semaphore("n_sdt"))
        f_sdv = st.enter_context(nc.semaphore("n_sdv"))
        f_slx = st.enter_context(nc.semaphore("n_slx"))
        f_sst = st.enter_context(nc.semaphore("n_sst"))

        def lds(sy):
            for src, dst in ((sg, sgs), (su, sus), (sd, sds), (bo, bos)):
                ph.inc(sy.dma_start(out=dst[:, :], in_=src), 16)
        SY(lds)

        with nc.sbuf_tensor("n_xn1sb", [128, NFC * TPC], BF16) as xn1sb, \
             nc.sbuf_tensor("n_hst", [128, NIC * TT], BF16) as hst, \
             nc.sbuf_tensor("n_wgc0", [128, NFC * 128], BF16) as wgc0, \
             nc.sbuf_tensor("n_wuc0", [128, NFC * 128], BF16) as wuc0, \
             nc.sbuf_tensor("n_wgc1", [128, NFC * 128], BF16) as wgc1, \
             nc.sbuf_tensor("n_wuc1", [128, NFC * 128], BF16) as wuc1, \
             nc.sbuf_tensor("n_wdc0", [128, NIC * 128], BF16) as wdc0, \
             nc.sbuf_tensor("n_wdc1", [128, NIC * 128], BF16) as wdc1:
            def ldxn(sy):
                for j in range(NFC):
                    ph.inc(sy.dma_start(out=xn1sb[:, j * TPC:(j + 1) * TPC],
                                        in_=xn1s[j * 128:(j + 1) * 128, :]), 16)
            SY(ldxn)
            UPT, DNT = NIC, NFC
            wgup = [(wgc0, wuc0), (wgc1, wuc1)]
            wdnp = [wdc0, wdc1]
            for p in range(NT):
                for ic in range(NIC):
                    wl = p * UPT + ic

                    @block.gpsimd
                    def _(g, ic=ic, wl=wl):
                        if wl == 0:
                            ph.wait(g)
                        if wl >= 2:
                            g.wait_ge(f_st, wl - 1)
                        wgc, wuc = wgup[wl % 2]
                        g.dma_start(out=wgc[:, :].rearrange("p (j n) -> p j n", n=128),
                                    in_=wg8[:, ic * 128:(ic + 1) * 128]
                                    .rearrange("(j p) n -> p j n", p=128)).then_inc(f_sw, 16)
                        g.dma_start(out=wuc[:, :].rearrange("p (j n) -> p j n", n=128),
                                    in_=wu8[:, ic * 128:(ic + 1) * 128]
                                    .rearrange("(j p) n -> p j n", p=128)).then_inc(f_sw, 16)
                    k = wl
                    q = k % 2

                    @block.tensor
                    def _(t, tt=p, wl=wl, k=k, q=q, p=p):
                        if k == 0:
                            ph.wait(t)
                        t.wait_ge(f_sw, 32 * (wl + 1))
                        if k >= 2:
                            t.wait_ge(f_sv, k - 1)
                        if p >= 1:
                            t.wait_ge(f_sdv, DNT * p)
                        wgc, wuc = wgup[wl % 2]
                        for j in range(NFC):
                            t.matmul(out=pF6[2 * q][:, :], lhsT=wgc[:, j * 128:(j + 1) * 128],
                                     rhs=xn1sb[:, j * TPC + tt * TT:j * TPC + (tt + 1) * TT],
                                     start=(j == 0), stop=(j == NFC - 1))
                        for j in range(NFC):
                            ins = t.matmul(out=pF6[2 * q + 1][:, :],
                                           lhsT=wuc[:, j * 128:(j + 1) * 128],
                                           rhs=xn1sb[:, j * TPC + tt * TT:
                                                    j * TPC + (tt + 1) * TT],
                                           start=(j == 0), stop=(j == NFC - 1))
                        ins.then_inc(f_st, 1)

                    @block.scalar
                    def _(s, ic=ic, k=k, q=q):
                        if k == 0:
                            ph.wait(s)
                        s.wait_ge(f_st, k + 1)
                        if k >= 2:
                            s.wait_ge(f_sv, k - 1)
                        s.activation(out=gt2f[q][:, :], in_=pF6[2 * q][:, :],
                                     func=AF.Silu,
                                     scale=sgs[:, ic:ic + 1]).then_inc(f_ssc, 1)

                    @block.vector
                    def _(v, ic=ic, k=k, q=q, p=p):
                        if k == 0:
                            ph.wait(v)
                        v.wait_ge(f_ssc, k + 1)
                        if p >= 1:
                            v.wait_ge(f_sdt, DNT * p)
                        v.tensor_scalar_mul(out=ut[:, :], in0=pF6[2 * q + 1][:, :],
                                            scalar1=sus[:, ic:ic + 1])
                        v.engine_nop()
                        v.tensor_mul(out=hst[:, ic * TT:(ic + 1) * TT], in0=gt2f[q][:, :],
                                     in1=ut[:, :]).then_inc(f_sv, 1)
                for jo in range(NFC):
                    m = p * DNT + jo

                    @block.gpsimd
                    def _(g, jo=jo, m=m):
                        if m >= 2:
                            g.wait_ge(f_sdt, m - 1)
                        wdc = wdnp[m % 2]
                        g.dma_start(out=wdc[:, :].rearrange("p (j n) -> p j n", n=128),
                                    in_=wd8[:, jo * 128:(jo + 1) * 128]
                                    .rearrange("(j p) n -> p j n", p=128)).then_inc(f_swd, 16)

                    @block.sync
                    def _(sy, tt=p, jo=jo, m=m):
                        if m == 0:
                            ph.wait(sy)
                        if m >= 2:
                            sy.wait_ge(f_sst, 16 * (m - 1))
                        sy.dma_start(out=x2p[m % 2][:, :],
                                     in_=x1s[jo * 128:(jo + 1) * 128,
                                             tt * TT:(tt + 1) * TT]).then_inc(f_slx, 16)
                        sy.dma_start(out=at2p[m % 2][:, :],
                                     in_=attn2_sl[jo * 128:(jo + 1) * 128,
                                                  tt * TT:(tt + 1) * TT]).then_inc(f_slx, 16)

                    @block.tensor
                    def _(t, m=m, p=p):
                        t.wait_ge(f_swd, 16 * (m + 1))
                        if m >= 4:
                            t.wait_ge(f_sdv, m - 3)
                        t.wait_ge(f_sv, UPT * (p + 1))
                        wdc = wdnp[m % 2]
                        for ic in range(NIC):
                            ins = t.matmul(out=pF6[m % 4][:, :],
                                           lhsT=wdc[:, ic * 128:(ic + 1) * 128],
                                           rhs=hst[:, ic * TT:(ic + 1) * TT],
                                           start=(ic == 0), stop=(ic == NIC - 1))
                        ins.then_inc(f_sdt, 1)

                    @block.vector
                    def _(v, jo=jo, m=m):
                        v.wait_ge(f_sdt, m + 1)
                        v.wait_ge(f_slx, 32 * (m + 1))
                        x2b = x2p[m % 2]
                        v.tensor_scalar_mul(out=f32t[:, :], in0=pF6[m % 4][:, :],
                                            scalar1=sds[:, jo:jo + 1])
                        v.tensor_scalar_add(out=f32u[:, :], in0=at2p[m % 2][:, :],
                                            scalar1=bos[:, jo:jo + 1])
                        v.engine_nop()
                        v.tensor_add(out=x2b[:, :], in0=x2b[:, :], in1=f32u[:, :])
                        v.engine_nop()
                        v.tensor_add(out=x2b[:, :], in0=x2b[:, :],
                                     in1=f32t[:, :]).then_inc(f_sdv, 1)

                    @block.gpsimd
                    def _(g, tt=p, jo=jo, m=m):
                        g.wait_ge(f_sdv, m + 1)
                        g.dma_start(out=x2o[jo * 128:(jo + 1) * 128,
                                            tt * TT:(tt + 1) * TT],
                                    in_=x2p[m % 2][:, :]).then_inc(f_sst, 16)

            @block.gpsimd
            def _(g):
                g.wait_ge(f_sst, 16 * NT * NFC)
                ph.inc(g.engine_nop())

        @block.gpsimd
        def _(g):
            ph.wait(g)



def fused_inmap(core, ins):
    c = core
    x = ins['L0_attn_hidden_states'].reshape(TOK, D)
    x_fm = np.ascontiguousarray(np.asarray(x, np.float32).T)
    m = {"x_sl": np.ascontiguousarray(x_fm[:, c * TPC:(c + 1) * TPC])}
    m0 = l0attn_inmap(c, None, ins)
    m0.pop("xn", None)
    m.update(m0)
    m["owp"] = np.ascontiguousarray(ins["L0_attn_o_w"][256 * c:256 * (c + 1), :])
    m["onw"] = np.ascontiguousarray(ins["L0_attn_o_norm_w"][:, None])
    m1 = mid_inmap(c, None, None, None, ins)
    for k in ("ofm", "xns", "xs", "ow", "onw"):
        m1.pop(k, None)
    m1["identf_m"] = m1.pop("identf")
    m.update(m1)
    m2 = l1attn_inmap(c, None, ins)
    m2.pop("xn1", None)
    m2["wq1"] = m2.pop("wq"); m2["wk1"] = m2.pop("wk"); m2["wv1"] = m2.pop("wv")
    m.update(m2)
    m["wop"] = np.ascontiguousarray(ins["L1_attn_Wo"][256 * c:256 * (c + 1), :])
    m3 = fin_inmap(c, None, None, None, ins)
    for k in ("ctx", "xn1s", "x1s", "wo"):
        m3.pop(k, None)
    m.update(m3)
    return m


def kernel_fused(upto=99, **inputs):
    nc = build_fused(upto=upto)
    maps = [fused_inmap(c, inputs) for c in range(NCORES)]
    res = run_bass_kernel_spmd(nc, maps, core_ids=list(range(NCORES)))
    if upto < 8:
        return [np.asarray(r["dbgo"]) for r in res.results]
    x2_fm = np.concatenate([np.asarray(r["x2_sl"]) for r in res.results], axis=1)
    return np.ascontiguousarray(x2_fm.T).reshape(4, 4096, D)



def kernel(**inputs):
    nc = build_fused(upto=8)
    maps = [fused_inmap(c, inputs) for c in range(NCORES)]
    res = run_bass_kernel_spmd(nc, maps, core_ids=list(range(NCORES)))
    x2_fm = np.concatenate([np.asarray(r["x2_sl"]) for r in res.results], axis=1)
    return np.ascontiguousarray(x2_fm.T).reshape(4, 4096, D)
